# version 10
# baseline (speedup 1.0000x reference)
import numpy as np
import ml_dtypes
from contextlib import ExitStack
import concourse.bass as bass
import concourse.mybir as mybir
from concourse.bass_utils import run_bass_kernel_spmd

F32 = mybir.dt.float32
BF16 = mybir.dt.bfloat16
AF = mybir.ActivationFunctionType
ALU = mybir.AluOpType
AX = mybir.AxisListType
BF16NP = ml_dtypes.bfloat16

NB, S, DM = 4, 4096, 1024
NCORE = 8
TPC = 2048
NEG = -1.0e30


class Pgm:

    NDMA = 6

    def __init__(self, nc, coll=False):
        self.nc = nc
        self.es = ExitStack()
        self.entries = {k: [] for k in "gvsty"}
        self.sems = {}
        self.counts = {k: 0 for k in "gvsty"}
        for k in "gvsty":
            self.sems[k] = self.es.enter_context(nc.semaphore(f"sem_{k}"))
        if coll:
            self.sems["c"] = self.es.enter_context(nc.semaphore("sem_c"))
            self.counts["c"] = 0
        self.scopes = [self.es]
        self.prefix = ""
        self.fence_events = None
        self.fenced = set()
        self.dma_pools = {}
        self.dma_idx = {}
        for q in "gy":
            pool = []
            for i in range(self.NDMA):
                key = f"{q}d{i}"
                self.sems[key] = self.es.enter_context(nc.semaphore(f"sem_{key}"))
                self.counts[key] = 0
                pool.append(key)
            self.dma_pools[q] = pool
            self.dma_idx[q] = 0

    def sb(self, name, shape, dt):
        return self.scopes[-1].enter_context(
            self.nc.sbuf_tensor(self.prefix + name, shape, dt))

    def ps(self, name, shape, dt=F32):
        return self.scopes[-1].enter_context(
            self.nc.psum_tensor(self.prefix + name, shape, dt))

    def push(self, prefix):
        self.scopes.append(ExitStack())
        self.prefix = prefix

    def pop(self):
        self.scopes.pop().close()
        self.prefix = ""

    def fence(self):
        evs = []
        for k, v in self.counts.items():
            if v > 0:
                evs.append((k, v, True))
        self.fence_events = evs
        self.fenced = set()

    def emit(self, eng, fn, waits=(), dma=False, sig=True, force=False,
             coll=False):
        ws = [w for w in waits if w is not None]
        if self.fence_events is not None and eng not in self.fenced:
            ws = list(self.fence_events) + ws
            self.fenced.add(eng)
        tag = self.prefix or "glob"
        if coll:
            prev = self.counts["c"]
            if prev > 0:
                ws.append(("c", prev, True))
            self.counts["c"] += 1
            ev = ("c", self.counts["c"], True)
            self.entries[eng].append((fn, tuple(ws), ev, 1, tag))
            return ev
        if dma and sig:
            pool = self.dma_pools[eng]
            skey = pool[self.dma_idx[eng] % self.NDMA]
            self.dma_idx[eng] += 1
            prev = self.counts[skey]
            if prev > 0:
                ws.append((skey, prev, True))
            self.counts[skey] += 16
            ev = (skey, self.counts[skey], True)
            self.entries[eng].append((fn, tuple(ws), ev, 16, tag))
            return ev
        if eng in "vsg" and fn is not None:
            sig = True
        inc = 1 if sig else 0
        ev = None
        if sig:
            self.counts[eng] += inc
            ev = (eng, self.counts[eng], force)
        self.entries[eng].append((fn, tuple(ws), ev, inc, tag))
        return ev

    def finish(self, eng, waits):
        ws = []
        for w in waits:
            if w is None:
                continue
            ws.append(w if len(w) == 3 else (w[0], w[1], True))
        self.entries[eng].append((None, tuple(ws), None, 0, "glob"))

    def build(self):
        block = self.es.enter_context(self.nc.Block())
        sems = self.sems

        def make(key):
            entries = self.entries[key]

            self_chain = key in "vsg"

            def section(eng):
                waited = {}
                cur_tag = None
                for fn, waits, ev, inc, tag in entries:
                    if self_chain and fn is not None and ev is not None \
                            and ev[0] == key and ev[1] > 1:
                        if ev[1] - 1 > waited.get(key, 0):
                            eng.wait_ge(sems[key], ev[1] - 1)
                            waited[key] = ev[1] - 1
                    for w in waits:
                        wk, wv, wdma = w
                        if wk == key and not wdma and not self_chain:
                            continue
                        if wk == key and not wdma and self_chain:
                            continue
                        if wv > waited.get(wk, 0):
                            eng.wait_ge(sems[wk], wv)
                            waited[wk] = wv
                    if fn is not None:
                        if tag != cur_tag:
                            if cur_tag is not None:
                                self.nc.pop_named_scope(cur_tag)
                            self.nc.push_named_scope(tag)
                            cur_tag = tag
                        inst = fn(eng)
                        if ev is not None:
                            inst.then_inc(sems[ev[0]], inc)
                if cur_tag is not None:
                    self.nc.pop_named_scope(cur_tag)
                return None

            return section

        block.gpsimd(make("g"))
        block.vector(make("v"))
        block.scalar(make("s"))
        block.tensor(make("t"))
        block.sync(make("y"))

    def close(self):
        self.es.close()


class Buf:

    def __init__(self):
        self.writes = {}
        self.reads = {}

    @property
    def write_list(self):
        return list(self.writes.values())

    def w_waits(self):
        return list(self.reads.values()) + list(self.writes.values())

    def wrote(self, ev):
        if ev is not None:
            old = self.writes.get(ev[0])
            if old is None or ev[1] > old[1]:
                self.writes[ev[0]] = ev

    def r_waits(self):
        return list(self.writes.values())

    def read(self, ev):
        if ev is not None:
            old = self.reads.get(ev[0])
            if old is None or ev[1] > old[1]:
                self.reads[ev[0]] = ev


def emit_A1(p, g, io, debug=False, stats_only=False, passes=2, p1_phases=99):
    HALO = 128
    WT = TPC + HALO
    WINW = 1152
    x_fm = io["x_fm"]
    x_tm = io["x_tm"]
    wqkv = io["wqkv"]
    wo = io["wo"]
    g_row = io["g_row"]
    maskA = io["maskA"]
    maskB = io["maskB"]
    s0_o = io["s0_o"]
    rout_o = io["rout_o"]
    xn_o = io.get("xn_o")
    xnfm_o = io.get("xnfm_o")
    if debug:
        dbg_stats = io["dbg_stats"]
        dbg_xq = io["dbg_xq"]
        dbg_qk = io["dbg_qk"]
        dbg_v = io["dbg_v"]
        dbg_sm = io["dbg_sm"]
        dbg_pp = io["dbg_pp"]
        dbg_ls = io["dbg_ls"]
        dbg_o = io["dbg_o"]
        dbg_rr = io["dbg_rr"]
        dbg_ar = io["dbg_ar"]
    idb = g["idb"]
    idf = g["idf"]
    ev_idb = ev_idf = None

    wqkv_sb = p.sb("wqkv_sb", [128, 8 * 3072], BF16)
    wo_sb = p.sb("wo_sb", [128, 8 * 1024], BF16)
    mask_sb = p.sb("mask_sb", [128, 512], F32)
    g_sb = p.sb("g_sb", [1, DM], BF16)
    xq_sb = p.sb("xq_sb", [128, 8 * WINW], BF16)
    qk_sb = p.sb("qk_sb", [128, 16 * WINW], BF16)
    v_sb = p.sb("v_sb", [128, 9 * 1024], BF16)
    o_sb = p.sb("o_sb", [128, 8 * 1024], BF16)
    xn_sb = p.sb("xn_sb", [128, 8 * 1024], BF16)
    xt = [p.sb(f"xt{i}", [128, DM], F32) for i in range(2)]
    xf = p.sb("xf", [128, WINW], F32)
    sq = xf
    r_s = p.sb("r_s", [128, 9], F32)
    ar_s = p.sb("ar_s", [128, 9], F32)
    R_s = p.sb("R_s", [128, 9], F32)
    A_s = p.sb("A_s", [128, 9], F32)
    st_sum = p.sb("st_sum", [128, 9], F32)
    st_ssq = p.sb("st_ssq", [128, 9], F32)
    st_m = p.sb("st_m", [128, 9], F32)
    var9 = p.sb("var9", [128, 9], F32)
    ve9 = p.sb("ve9", [128, 9], F32)
    tmp9 = p.sb("tmp9", [128, 9], F32)
    tmp9b = p.sb("tmp9b", [128, 9], F32)
    ones1c = p.sb("ones1c", [128, 1], F32)
    tmpd = p.sb("tmpd", [128, 1], F32)
    Rrow = p.sb("Rrow", [1, WINW], BF16)
    Arow = p.sb("Arow", [1, WINW], BF16)
    sm = [p.sb(f"sm{i}", [128, 256], BF16) for i in range(2)]
    pp = [p.sb(f"pp{i}", [128, 256], BF16) for i in range(2)]
    pt = [p.sb(f"pt{i}", [128, 256], BF16) for i in range(2)]
    mneg = [p.sb(f"mneg{i}", [128, 1], F32) for i in range(2)]
    lsum = [p.sb(f"lsum{i}", [128, 1], F32) for i in range(2)]
    rl = [p.sb(f"rl{i}", [128, 1], F32) for i in range(2)]
    s0_st = [p.sb(f"s0st{i}", [128, DM], F32) for i in range(2)]
    rout_t = p.sb("rout_t", [128, 2], F32)
    eps5 = p.sb("eps5", [128, 1], F32)
    eps6 = p.sb("eps6", [128, 1], F32)
    xnf_st = p.sb("xnf_st", [128, 1024], BF16)
    bxnf = Buf()

    PA = p.ps("PA", [128, 512])
    PB = p.ps("PB", [128, 512])
    PC = p.ps("PC", [128, 512])
    PD = p.ps("PD", [128, 512])
    PT0 = p.ps("PT0", [128, 256], BF16)
    PT1 = p.ps("PT1", [128, 256], BF16)
    PO0 = p.ps("PO0", [64, 128])
    PO1 = p.ps("PO1", [64, 128])

    bPA, bPB, bPC, bPD = Buf(), Buf(), Buf(), Buf()
    bPT = [Buf(), Buf()]
    bPO = [Buf(), Buf()]
    bxt = [Buf(), Buf()]
    bsq = bxf = Buf()
    bsm = [Buf(), Buf()]
    bpp = [Buf(), Buf()]
    bpt = [Buf(), Buf()]
    bs0 = [Buf(), Buf()]
    bxq, bqk, bv, bo, bxn = Buf(), Buf(), Buf(), Buf(), Buf()
    brows, brt = Buf(), Buf()

    evs_wqkv = []
    for k in range(8):
        evs_wqkv.append(p.emit("g", lambda e, k=k: e.dma_start(
            out=wqkv_sb[:, k * 3072:(k + 1) * 3072],
            in_=wqkv[k * 128:(k + 1) * 128, :]), dma=True))
    evs_wo = []
    for k in range(8):
        evs_wo.append(p.emit("g", lambda e, k=k: e.dma_start(
            out=wo_sb[:, k * 1024:(k + 1) * 1024],
            in_=wo[k * 128:(k + 1) * 128, :]), dma=True))
    p.emit("y", lambda e: e.dma_start(out=mask_sb[:, 0:256], in_=maskA[:, :]), dma=True)
    ev_mask = p.emit("y", lambda e: e.dma_start(out=mask_sb[:, 256:512], in_=maskB[:, :]), dma=True)
    ev_g = p.emit("g", lambda e: e.dma_start(out=g_sb[:, :], in_=g_row[:, :]), dma=True)
    ev_e5 = p.emit("v", lambda e: e.memset(eps5[:, :], 1e-5))
    ev_e6 = p.emit("v", lambda e: e.memset(eps6[:, :], 1e-6))
    ev_o1c = p.emit("v", lambda e: e.memset(ones1c[:, :], 1.0))
    bstats = Buf()

    store_evs = []

    for ps_ in range(passes):
        WLO = 1024 * ps_
        tile_evs = []
        for j in range(9):
            tile = 8 * ps_ + j
            buf, bb = xt[j % 2], bxt[j % 2]
            ev_ld = p.emit("y", lambda e, tile=tile, buf=buf: e.dma_start(
                out=buf[:, :], in_=x_tm[tile * 128:(tile + 1) * 128, :]),
                waits=bb.w_waits(), dma=True)
            bb.wrote(ev_ld)
            if ps_ == 1 and p1_phases < 0.3:
                continue
            ev1 = p.emit("v", lambda e, buf=buf, j=j: e.tensor_reduce(
                st_sum[:, j:j + 1], buf[:, :], axis=AX.X, op=ALU.add),
                waits=[ev_ld] + (bstats.w_waits() if j == 0 else []))
            bb.read(ev1)
            ev_sq = p.emit("s", lambda e, buf=buf: e.activation(
                sq[:, 0:DM], buf[:, :], AF.Square), waits=[ev_ld] + bsq.w_waits())
            bb.read(ev_sq)
            bsq.wrote(ev_sq)
            if ps_ == 1 and p1_phases < 0.35:
                continue
            ev2 = p.emit("v", lambda e, j=j: e.tensor_reduce(
                st_ssq[:, j:j + 1], sq[:, 0:DM], axis=AX.X, op=ALU.add),
                waits=[ev_sq])
            bsq.read(ev2)
        if ps_ == 1 and p1_phases < 0.5:
            break
        ev1f = p.emit("v", lambda e: e.engine_nop(),
                      waits=[(ev1[0], ev1[1], True), (ev2[0], ev2[1], True)],
                      force=True)
        pe = ev1f

        def vstep(fn, extra=()):
            nonlocal pe
            pe = p.emit("v", fn, waits=[pe] + list(extra), force=True)
            return pe

        vstep(lambda e: e.tensor_scalar(
            st_m[:, :], st_sum[:, :], 1.0 / DM, None, ALU.mult))
        vstep(lambda e: e.tensor_tensor(
            tmp9[:, :], st_m[:, :], st_m[:, :], ALU.mult))
        vstep(lambda e: e.scalar_tensor_tensor(
            var9[:, :], st_ssq[:, :], 1.0 / DM, tmp9[:, :],
            ALU.mult, ALU.subtract))
        ev_ve = vstep(lambda e: e.tensor_scalar(
            ve9[:, :], var9[:, :], 1.0, 1e-5, ALU.mult, ALU.add))
        ev_sa = p.emit("s", lambda e: e.activation(
            r_s[:, :], ve9[:, :], AF.Sqrt), waits=[ev_ve])
        vstep(lambda e: e.reciprocal(r_s[:, :], r_s[:, :]), extra=[ev_sa])
        for _ in range(2):
            vstep(lambda e: e.tensor_tensor(
                tmp9[:, :], r_s[:, :], r_s[:, :], ALU.mult))
            vstep(lambda e: e.tensor_tensor(
                tmp9[:, :], tmp9[:, :], ve9[:, :], ALU.mult))
            vstep(lambda e: e.tensor_scalar(
                tmp9[:, :], tmp9[:, :], -0.5, 1.5, ALU.mult, ALU.add))
            vstep(lambda e: e.tensor_tensor(
                r_s[:, :], r_s[:, :], tmp9[:, :], ALU.mult))
        vstep(lambda e: e.tensor_scalar(
            tmp9[:, :], st_m[:, :], -1.0, None, ALU.mult))
        ev_ar = vstep(lambda e: e.tensor_tensor(
            ar_s[:, :], tmp9[:, :], r_s[:, :], ALU.mult))
        vstep(lambda e: e.tensor_tensor(
            tmp9[:, :], r_s[:, :], r_s[:, :], ALU.mult))
        vstep(lambda e: e.tensor_tensor(
            tmp9[:, :], tmp9[:, :], var9[:, :], ALU.mult))
        ev_ve2 = vstep(lambda e: e.tensor_scalar(
            ve9[:, :], tmp9[:, :], 1.0, 1e-6, ALU.mult, ALU.add))
        ev_sa2 = p.emit("s", lambda e: e.activation(
            R_s[:, :], ve9[:, :], AF.Sqrt), waits=[ev_ve2])
        vstep(lambda e: e.reciprocal(R_s[:, :], R_s[:, :]), extra=[ev_sa2])
        for _ in range(2):
            vstep(lambda e: e.tensor_tensor(
                tmp9[:, :], R_s[:, :], R_s[:, :], ALU.mult))
            vstep(lambda e: e.tensor_tensor(
                tmp9[:, :], tmp9[:, :], ve9[:, :], ALU.mult))
            vstep(lambda e: e.tensor_scalar(
                tmp9[:, :], tmp9[:, :], -0.5, 1.5, ALU.mult, ALU.add))
            vstep(lambda e: e.tensor_tensor(
                R_s[:, :], R_s[:, :], tmp9[:, :], ALU.mult))
        vstep(lambda e: e.tensor_tensor(
            R_s[:, :], R_s[:, :], r_s[:, :], ALU.mult))
        vstep(lambda e: e.tensor_scalar(
            tmp9[:, :], st_m[:, :], -1.0, None, ALU.mult))
        ev_RA = vstep(lambda e: e.tensor_tensor(
            A_s[:, :], tmp9[:, :], R_s[:, :], ALU.mult))
        bstats.wrote(ev_RA)
        bstats_last = ev_RA
        ev5 = ev_RA

        if ps_ == 1 and p1_phases < 0.8:
            break
        for j in range(1, 9):
            tile = 8 * ps_ + j
            buf, bb = xt[j % 2], bxt[j % 2]
            qtok = (tile - 1) * 128
            ev_ld2 = p.emit("y", lambda e, tile=tile, buf=buf: e.dma_start(
                out=buf[:, :], in_=x_tm[tile * 128:(tile + 1) * 128, :]),
                waits=bb.w_waits(), dma=True)
            bb.wrote(ev_ld2)
            evx = p.emit("v", lambda e, buf=buf, j=j: e.tensor_scalar(
                xn_sb[:, (j - 1) * 1024:j * 1024], buf[:, :],
                r_s[:, j:j + 1], ar_s[:, j:j + 1], ALU.mult, ALU.add),
                waits=[ev_RA, ev_ld2] + bxn.w_waits())
            bb.read(evx)
            bxn.wrote(evx)
            bstats.read(evx)
            if xn_o is not None:
                evxo = p.emit("y", lambda e, j=j, qtok=qtok: e.dma_start(
                    out=xn_o[qtok:qtok + 128, :],
                    in_=xn_sb[:, (j - 1) * 1024:j * 1024]), waits=[evx], dma=True)
                bxn.read(evxo)
                store_evs.append(evxo)
            if xnfm_o is not None:
                evcs = []
                for f8 in range(8):
                    psT, bbT = (PT0, bPT[0]) if f8 % 2 == 0 else (PT1, bPT[1])
                    evt = p.emit("t", lambda e, j=j, f8=f8, psT=psT: e.transpose(
                        psT[:, 0:128],
                        xn_sb[:, (j - 1) * 1024 + f8 * 128:(j - 1) * 1024 + (f8 + 1) * 128],
                        idb[:, :]),
                        waits=[evx] + bbT.w_waits())
                    bbT.wrote(evt)
                    bxn.read(evt)
                    evc_ = p.emit("v", lambda e, f8=f8, psT=psT: e.tensor_copy(
                        xnf_st[:, f8 * 128:(f8 + 1) * 128], psT[:, 0:128]),
                        waits=[evt] + (bxnf.w_waits() if f8 == 0 else []))
                    bbT.read(evc_)
                    bxnf.wrote(evc_)
                    evcs.append(evc_)
                ev_fo = p.emit("y", lambda e, qtok=qtok: e.dma_start(
                    out=xnfm_o.rearrange("(k p) n -> p k n", p=128)[:, :, qtok:qtok + 128],
                    in_=xnf_st[:].rearrange("p (k n) -> p k n", n=128)),
                    waits=evcs, dma=True)
                bxnf.read(ev_fo)
                store_evs.append(ev_fo)
            evr = p.emit("v", lambda e, buf=buf, j=j: e.tensor_scalar(
                rout_t[:, :], buf[:, 0:2],
                r_s[:, j:j + 1], ar_s[:, j:j + 1], ALU.mult, ALU.add),
                waits=brt.w_waits())
            bb.read(evr)
            brt.wrote(evr)
            bstats.read(evr)
            evro = p.emit("y", lambda e, qtok=qtok: e.dma_start(
                out=rout_o[qtok:qtok + 128, :], in_=rout_t[:, :]),
                waits=[evr], dma=True)
            brt.read(evro)
            store_evs.append(evro)

        if stats_only:
            for idx, src_ in enumerate((st_sum, st_ssq, var9, r_s, ar_s, R_s)):
                evd = p.emit("y", lambda e, idx=idx, src_=src_: e.dma_start(
                    out=dbg_stats[:, idx * 9:(idx + 1) * 9], in_=src_[:, :]),
                    waits=[("v", p.counts["v"], False)], dma=True)
                store_evs.append(evd)
            break

        if ps_ == 1 and p1_phases < 2:
            break
        for j in range(9):
            seg = slice(j * 128, (j + 1) * 128)
            evd1 = p.emit("v", lambda e, j=j: e.tensor_scalar(
                tmpd[:, :], ones1c[:, :], R_s[:, j:j + 1], None, ALU.mult),
                waits=[bstats_last, ev_o1c])
            bstats.read(evd1)
            evt1 = p.emit("t", lambda e: e.transpose(
                PO0[0:1, 0:128], tmpd[:, :], idf[:, :]),
                waits=[evd1, ev_idf] + bPO[0].w_waits())
            bPO[0].wrote(evt1)
            evc1 = p.emit("v", lambda e, seg=seg: e.tensor_copy(
                Rrow[:, seg], PO0[0:1, 0:128]), waits=[evt1])
            bPO[0].read(evc1)
            evd2 = p.emit("v", lambda e, j=j: e.tensor_scalar(
                tmpd[:, :], ones1c[:, :], A_s[:, j:j + 1], None, ALU.mult),
                waits=[evt1])
            bstats.read(evd2)
            evt2 = p.emit("t", lambda e: e.transpose(
                PO1[0:1, 0:128], tmpd[:, :], idf[:, :]),
                waits=[evd2] + bPO[1].w_waits())
            bPO[1].wrote(evt2)
            evc2 = p.emit("v", lambda e, seg=seg: e.tensor_copy(
                Arow[:, seg], PO1[0:1, 0:128]), waits=[evt2])
            bPO[1].read(evc2)
            brows.wrote(evc2)

        ev_rows = brows.write_list[-1]

        if ps_ == 1 and p1_phases < 3:
            break
        for f in range(8):
            ev_ldf = p.emit("y", lambda e, f=f, WLO=WLO: e.dma_start(
                out=xf[:, :], in_=x_fm[f * 128:(f + 1) * 128, WLO:WLO + WINW]),
                waits=bxf.w_waits(), dma=True)
            bxf.wrote(ev_ldf)
            for w in range(3):
                cs = slice(w * 384, (w + 1) * 384)
                pb1, bb1 = (PA, bPA) if w % 2 == 0 else (PC, bPC)
                pb2, bb2 = (PB, bPB) if w % 2 == 0 else (PD, bPD)
                ev_o1 = p.emit("t", lambda e, f=f, cs=cs, pb1=pb1: e.matmul(
                    pb1[:, 0:384], g_sb[0:1, f * 128:(f + 1) * 128],
                    Rrow[0:1, cs], start=True, stop=True),
                    waits=[ev_rows, ev_g] + bb1.w_waits())
                bb1.wrote(ev_o1)
                ev_o2 = p.emit("t", lambda e, f=f, cs=cs, pb2=pb2: e.matmul(
                    pb2[:, 0:384], g_sb[0:1, f * 128:(f + 1) * 128],
                    Arow[0:1, cs], start=True, stop=True),
                    waits=bb2.w_waits())
                bb2.wrote(ev_o2)
                evm = p.emit("v", lambda e, cs=cs, pb1=pb1: e.tensor_tensor(
                    xf[:, cs], xf[:, cs], pb1[:, 0:384], ALU.mult),
                    waits=[ev_ldf, ev_o1])
                bb1.read(evm)
                eva = p.emit("v", lambda e, f=f, cs=cs, pb2=pb2, w=w: e.tensor_tensor(
                    xq_sb[:, f * WINW + w * 384: f * WINW + (w + 1) * 384],
                    xf[:, cs], pb2[:, 0:384], ALU.add),
                    waits=[ev_o2] + bxq.w_waits())
                bb2.read(eva)
                bxf.read(eva)
                bxq.wrote(eva)
        ev_xq = bxq.write_list[-1]

        if ps_ == 1 and p1_phases < 4:
            break
        for oc in range(16):
            for w in range(3):
                pb, bb = (PA, bPA) if (oc * 3 + w) % 2 == 0 else (PB, bPB)
                ev_mm = None
                for k in range(8):
                    ev_mm = p.emit("t", lambda e, oc=oc, w=w, k=k, pb=pb: e.matmul(
                        pb[:, 0:384],
                        wqkv_sb[:, k * 3072 + oc * 128:k * 3072 + (oc + 1) * 128],
                        xq_sb[:, k * WINW + w * 384:k * WINW + (w + 1) * 384],
                        start=(k == 0), stop=(k == 7)),
                        waits=(bb.w_waits() + [ev_xq] + evs_wqkv) if k == 0 else [],
                        sig=(k == 7))
                bb.wrote(ev_mm)
                bxq.read(ev_mm)
                eve = p.emit("s", lambda e, oc=oc, w=w, pb=pb: e.activation(
                    qk_sb[:, oc * WINW + w * 384:oc * WINW + (w + 1) * 384],
                    pb[:, 0:384], AF.Copy), waits=[ev_mm] + bqk.w_waits())
                bb.read(eve)
                bqk.wrote(eve)
        ev_qk = bqk.write_list[-1]

        if ps_ == 1 and p1_phases < 5:
            break
        for j in range(9):
            for nh in range(2):
                pb, bb = (PC, bPC) if (j * 2 + nh) % 2 == 0 else (PD, bPD)
                ev_mm = None
                for k in range(8):
                    ev_mm = p.emit("t", lambda e, j=j, nh=nh, k=k, pb=pb: e.matmul(
                        pb[:, :],
                        xq_sb[:, k * WINW + j * 128:k * WINW + (j + 1) * 128],
                        wqkv_sb[:, k * 3072 + 2048 + nh * 512:k * 3072 + 2048 + (nh + 1) * 512],
                        start=(k == 0), stop=(k == 7)),
                        waits=(bb.w_waits() + [ev_xq] + evs_wqkv) if k == 0 else [],
                        sig=(k == 7))
                bb.wrote(ev_mm)
                bxq.read(ev_mm)
                eve = p.emit("s", lambda e, j=j, nh=nh, pb=pb: e.activation(
                    v_sb[:, j * 1024 + nh * 512:j * 1024 + (nh + 1) * 512],
                    pb[:, :], AF.Copy), waits=[ev_mm] + bv.w_waits())
                bb.read(eve)
                bv.wrote(eve)
        ev_v = bv.write_list[-1]

        if ps_ == 1 and p1_phases < 6:
            break
        inst = 0
        for qt in range(8):
            for hd in range(16):
                i2 = inst % 2
                base = (hd % 2) * 64
                occ = hd // 2
                psS, bbS = (PC, bPC) if i2 == 0 else (PD, bPD)
                ev_s = p.emit("t", lambda e, qt=qt, base=base, occ=occ, psS=psS: e.matmul(
                    psS[:, 0:256],
                    qk_sb[base:base + 64, occ * WINW + 128 + qt * 128: occ * WINW + 128 + (qt + 1) * 128],
                    qk_sb[base:base + 64, (8 + occ) * WINW + qt * 128: (8 + occ) * WINW + qt * 128 + 256],
                    start=True, stop=True),
                    waits=bbS.w_waits() + [ev_qk])
                bbS.wrote(ev_s)
                bqk.read(ev_s)
                mask_c = 0 if (ps_ == 0 and qt == 0) else 256
                ev_m = p.emit("v", lambda e, i2=i2, psS=psS, mask_c=mask_c: e.scalar_tensor_tensor(
                    sm[i2][:, :], psS[:, 0:256], 0.125, mask_sb[:, mask_c:mask_c + 256],
                    ALU.mult, ALU.add),
                    waits=[ev_s, ev_mask] + bsm[i2].w_waits())
                bbS.read(ev_m)
                bsm[i2].wrote(ev_m)
                ev_mx = p.emit("v", lambda e, i2=i2: e.tensor_reduce(
                    mneg[i2][:, :], sm[i2][:, :], axis=AX.X, op=ALU.max,
                    negate=True), waits=[ev_m])
                ev_e = p.emit("s", lambda e, i2=i2: e.activation(
                    pp[i2][:, :], sm[i2][:, :], AF.Exp, bias=mneg[i2][:, :],
                    accum_out=lsum[i2][:, :]),
                    waits=[ev_mx] + bpp[i2].w_waits())
                bsm[i2].read(ev_e)
                ev_rl = p.emit("v", lambda e, i2=i2: e.reciprocal(
                    rl[i2][:, :], lsum[i2][:, :]), waits=[ev_e], force=True)
                ev_pn = p.emit("v", lambda e, i2=i2: e.tensor_scalar(
                    pp[i2][:, :], pp[i2][:, :], rl[i2][:, :], None, ALU.mult),
                    waits=[ev_rl])
                bpp[i2].wrote(ev_pn)
                psT, bbT = (PT0, bPT[0]) if i2 == 0 else (PT1, bPT[1])
                evt = None
                for kc in range(2):
                    evt = p.emit("t", lambda e, i2=i2, kc=kc, psT=psT: e.transpose(
                        psT[:, kc * 128:(kc + 1) * 128],
                        pp[i2][:, kc * 128:(kc + 1) * 128], idb[:, :]),
                        waits=([ev_pn, ev_idb] + bbT.w_waits()) if kc == 0 else [],
                        sig=(kc == 1))
                bpp[i2].read(evt)
                bbT.wrote(evt)
                evc = p.emit("v", lambda e, i2=i2, psT=psT: e.tensor_copy(
                    pt[i2][:, :], psT[:, :]), waits=[evt] + bpt[i2].w_waits())
                bbT.read(evc)
                bpt[i2].wrote(evc)
                psO, bbO = (PO0, bPO[0]) if i2 == 0 else (PO1, bPO[1])
                ev_pv = None
                for kc in range(2):
                    ev_pv = p.emit("t", lambda e, qt=qt, hd=hd, kc=kc, i2=i2, psO=psO: e.matmul(
                        psO[:, :],
                        v_sb[:, (qt + kc) * 1024 + hd * 64:(qt + kc) * 1024 + (hd + 1) * 64],
                        pt[i2][:, kc * 128:(kc + 1) * 128],
                        start=(kc == 0), stop=(kc == 1)),
                        waits=([evc, ev_v] + bbO.w_waits()) if kc == 0 else [],
                        sig=(kc == 1))
                bpt[i2].read(ev_pv)
                bv.read(ev_pv)
                bbO.wrote(ev_pv)
                ev_oe = p.emit("s", lambda e, base=base, occ=occ, qt=qt, psO=psO: e.activation(
                    o_sb[base:base + 64, occ * 1024 + qt * 128:occ * 1024 + (qt + 1) * 128],
                    psO[:, :], AF.Copy),
                    waits=[ev_pv] + bo.w_waits())
                bbO.read(ev_oe)
                bo.wrote(ev_oe)
                inst += 1
        ev_o_all = bo.write_list[-1]

        if debug and ps_ == 0:
            evd = p.emit("y", lambda e: e.dma_start(out=dbg_rr[:, :], in_=Rrow[:, :]),
                         waits=brows.r_waits(), dma=True)
            brows.read(evd)
            store_evs.append(evd)
            evd = p.emit("y", lambda e: e.dma_start(out=dbg_ar[:, :], in_=Arow[:, :]),
                         waits=brows.r_waits(), dma=True)
            brows.read(evd)
            store_evs.append(evd)
            for nm, sbuf_, bb_, last in (
                ("dbg_xq", xq_sb, bxq, ev_xq), ("dbg_qk", qk_sb, bqk, ev_qk),
                ("dbg_v", v_sb, bv, ev_v), ("dbg_o", o_sb, bo, ev_o_all)):
                tgt = {"dbg_xq": dbg_xq, "dbg_qk": dbg_qk, "dbg_v": dbg_v,
                       "dbg_o": dbg_o}[nm]
                evd = p.emit("y", lambda e, tgt=tgt, sbuf_=sbuf_: e.dma_start(
                    out=tgt[:, :], in_=sbuf_[:, :]), waits=[last], dma=True)
                bb_.read(evd)
                store_evs.append(evd)
            for idx, src_ in enumerate((st_m, var9, r_s, ar_s, R_s, A_s)):
                evd = p.emit("y", lambda e, idx=idx, src_=src_: e.dma_start(
                    out=dbg_stats[:, idx * 9:(idx + 1) * 9], in_=src_[:, :]),
                    waits=[bstats_last], dma=True)
                store_evs.append(evd)
            evd = p.emit("y", lambda e: e.dma_start(out=dbg_sm[:, :], in_=sm[0][:, :]),
                         waits=bsm[0].r_waits() + bsm[0].write_list, dma=True)
            bsm[0].read(evd)
            store_evs.append(evd)
            evd = p.emit("y", lambda e: e.dma_start(out=dbg_pp[:, :], in_=pp[0][:, :]),
                         waits=bpp[0].write_list, dma=True)
            bpp[0].read(evd)
            store_evs.append(evd)
            evd = p.emit("y", lambda e: e.dma_start(out=dbg_ls[:, :], in_=lsum[0][:, :]), dma=True)
            store_evs.append(evd)

        if ps_ == 1 and p1_phases < 7:
            break
        for tt in range(8):
            sbuf, bsb = s0_st[tt % 2], bs0[tt % 2]
            for nh in range(2):
                pb, bb = (PA, bPA) if nh == 0 else (PB, bPB)
                ev_mm = None
                for oc in range(8):
                    ev_mm = p.emit("t", lambda e, tt=tt, nh=nh, oc=oc, pb=pb: e.matmul(
                        pb[:, :],
                        o_sb[:, oc * 1024 + tt * 128:oc * 1024 + (tt + 1) * 128],
                        wo_sb[:, oc * 1024 + nh * 512:oc * 1024 + nh * 512 + 512],
                        start=(oc == 0), stop=(oc == 7)),
                        waits=(bb.w_waits() + [ev_o_all] + evs_wo) if oc == 0 else [],
                        sig=(oc == 7))
                bb.wrote(ev_mm)
                bo.read(ev_mm)
                ev_add = p.emit("v", lambda e, tt=tt, nh=nh, pb=pb, sbuf=sbuf: e.tensor_tensor(
                    sbuf[:, nh * 512:(nh + 1) * 512], pb[:, :],
                    xn_sb[:, tt * 1024 + nh * 512:tt * 1024 + (nh + 1) * 512],
                    ALU.add), waits=[ev_mm] + (bsb.w_waits() if nh == 0 else []))
                bb.read(ev_add)
                bxn.read(ev_add)
                if nh == 1:
                    bsb.wrote(ev_add)
                    qtok = ps_ * 1024 + tt * 128
                    ev_st = p.emit("y", lambda e, qtok=qtok, sbuf=sbuf: e.dma_start(
                        out=s0_o[qtok:qtok + 128, :], in_=sbuf[:, :]),
                        waits=[ev_add], dma=True)
                    bsb.read(ev_st)
                    store_evs.append(ev_st)


def mk_glob(p, nc):
    id_bf_t = nc.inline_tensor(np.eye(128, dtype=BF16NP), "id_bf")
    id_f_t = nc.inline_tensor(np.eye(128, dtype=np.float32), "id_f")
    idb = p.sb("glob_idb", [128, 128], BF16)
    idf = p.sb("glob_idf", [128, 128], F32)
    p.emit("g", lambda e: e.dma_start(out=idb[:, :], in_=id_bf_t.ap()[:, :]), dma=True)
    p.emit("g", lambda e: e.dma_start(out=idf[:, :], in_=id_f_t.ap()[:, :]), dma=True)
    return {"idb": idb, "idf": idf}


def finish_all(p):
    p.finish("y", [(k, p.counts[k], True)
                   for k in p.dma_pools["y"] + p.dma_pools["g"]
                   if p.counts[k] > 0])
    p.build()
    p.close()


def build_A1(debug=False, stats_only=False, passes=2, p1_phases=99):
    nc = bass.Bass("TRN2", target_bir_lowering=False)
    WT = TPC + 128
    io = {
        "x_fm": nc.dram_tensor("x_fm", [DM, WT], F32, kind="ExternalInput").ap(),
        "x_tm": nc.dram_tensor("x_tm", [WT, DM], F32, kind="ExternalInput").ap(),
        "wqkv": nc.dram_tensor("wqkv", [DM, 3 * DM], F32, kind="ExternalInput").ap(),
        "wo": nc.dram_tensor("wo", [DM, DM], F32, kind="ExternalInput").ap(),
        "g_row": nc.dram_tensor("g_row", [1, DM], F32, kind="ExternalInput").ap(),
        "maskA": nc.dram_tensor("maskA", [128, 256], F32, kind="ExternalInput").ap(),
        "maskB": nc.dram_tensor("maskB", [128, 256], F32, kind="ExternalInput").ap(),
        "s0_o": nc.dram_tensor("s0", [TPC, DM], F32, kind="ExternalOutput").ap(),
        "xn_o": nc.dram_tensor("xn_bf", [TPC, DM], BF16, kind="ExternalOutput").ap(),
        "rout_o": nc.dram_tensor("rout", [TPC, 2], F32, kind="ExternalOutput").ap(),
    }
    if debug:
        io["dbg_stats"] = nc.dram_tensor("dbg_stats", [128, 6 * 9], F32, kind="ExternalOutput").ap()
        io["dbg_xq"] = nc.dram_tensor("dbg_xq", [128, 8 * 1152], BF16, kind="ExternalOutput").ap()
        io["dbg_qk"] = nc.dram_tensor("dbg_qk", [128, 16 * 1152], BF16, kind="ExternalOutput").ap()
        io["dbg_v"] = nc.dram_tensor("dbg_v", [128, 9 * 1024], BF16, kind="ExternalOutput").ap()
        io["dbg_sm"] = nc.dram_tensor("dbg_sm", [128, 256], BF16, kind="ExternalOutput").ap()
        io["dbg_pp"] = nc.dram_tensor("dbg_pp", [128, 256], BF16, kind="ExternalOutput").ap()
        io["dbg_ls"] = nc.dram_tensor("dbg_ls", [128, 1], F32, kind="ExternalOutput").ap()
        io["dbg_o"] = nc.dram_tensor("dbg_o", [128, 8 * 1024], BF16, kind="ExternalOutput").ap()
        io["dbg_rr"] = nc.dram_tensor("dbg_rr", [1, 1152], BF16, kind="ExternalOutput").ap()
        io["dbg_ar"] = nc.dram_tensor("dbg_ar", [1, 1152], BF16, kind="ExternalOutput").ap()
    p = Pgm(nc)
    g = mk_glob(p, nc)
    p.fence()
    emit_A1(p, g, io, debug=debug, stats_only=stats_only, passes=passes,
            p1_phases=p1_phases)
    finish_all(p)
    return nc


def _l0_masks(first):
    i = np.arange(128)[:, None]
    j = np.arange(256)[None, :]
    base = (j > i) & (j <= i + 128)
    allowed = (base & (j >= 128)) if first else base
    return np.where(allowed, 0.0, NEG).astype(np.float32)


def _a1_inputs(x_full):
    maps = []
    for c in range(NCORE):
        b, h = c // 2, c % 2
        T0 = h * TPC
        if h == 0:
            halo = np.zeros((128, DM), np.float32)
            sl = x_full[b, 0:TPC]
        else:
            halo = x_full[b, T0 - 128:T0]
            sl = x_full[b, T0:T0 + TPC]
        x_tm = np.ascontiguousarray(np.concatenate([halo, sl], axis=0))
        x_fm = np.ascontiguousarray(x_tm.T)
        maps.append({
            "x_fm": x_fm, "x_tm": x_tm,
            "maskA": _l0_masks(h == 0), "maskB": _l0_masks(False),
        })
    return maps


def run_A1(inputs, debug=False, stats_only=False, passes=2, p1_phases=99):
    x = np.asarray(inputs["L0_attn_x"], np.float32)
    wqkv = np.ascontiguousarray(np.asarray(inputs["L0_attn_Wqkv"], np.float32))
    wo = np.ascontiguousarray(np.asarray(inputs["L0_attn_Wo"], np.float32))
    g_row = np.ascontiguousarray(np.asarray(inputs["L0_attn_g"], np.float32).reshape(1, DM))
    maps = _a1_inputs(x)
    for m in maps:
        m["wqkv"] = wqkv
        m["wo"] = wo
        m["g_row"] = g_row
    nc = build_A1(debug=debug, stats_only=stats_only, passes=passes, p1_phases=p1_phases)
    res = run_bass_kernel_spmd(nc, maps, core_ids=list(range(NCORE)))
    return res.results


def build_fused():
    nc = bass.Bass("TRN2", target_bir_lowering=False)
    WT = TPC + 128
    EXT, IN_ = 2561, 512
    E, DFF = 8, 4096
    HID, NCH = 2730, 22
    ei, eo = "ExternalInput", "ExternalOutput"

    def din(name, shape, dt=F32):
        return nc.dram_tensor(name, shape, dt, kind=ei).ap()

    def dint(name, shape, dt=F32):
        return nc.dram_tensor(name, shape, dt, kind="Internal").ap()

    x_fm = din("x_fm", [DM, WT])
    x_tm = din("x_tm", [WT, DM])
    a1_wqkv = din("a1_wqkv", [DM, 3 * DM], BF16)
    a1_wo = din("a1_wo", [DM, DM], BF16)
    a1_g = din("a1_g", [1, DM])
    maskA = din("maskA", [128, 256])
    maskB = din("maskB", [128, 256])
    a2_W1 = din("a2_W1", [E, DM, DFF], BF16)
    a2_b1 = din("a2_b1", [E, DFF])
    a2_W2 = din("a2_W2", [E, DFF, DM], BF16)
    a2_b2 = din("a2_b2", [E, DM])
    b1_wq = din("b1_wq", [DM, DM], BF16)
    b1_wk = din("b1_wk", [DM, DM], BF16)
    b1_wv = din("b1_wv", [DM, DM], BF16)
    b1_wo = din("b1_wo", [DM, DM], BF16)
    b1_vc = din("b1_vc", [128, 16])
    b1_bo = din("b1_bo", [1, DM])
    b1_wff1 = din("b1_wff1", [DM, 4 * DM], BF16)
    b1_wff2 = din("b1_wff2", [4 * DM, DM], BF16)
    b1_wd = din("b1_wd", [DM, DM], BF16)
    b1_vc2 = din("b1_vc2", [128, 16])
    b1_bf1c = din("b1_bf1c", [128, 32])
    b1_bff2 = din("b1_bff2", [1, DM])
    b1_bd = din("b1_bd", [1, DM])
    bf_tbls = din("bf_tbls", [128, 4 * EXT])
    bf_masks = din("bf_masks", [128, 5 * 640])
    hz_bf = din("hz_bf", [128, 1], BF16)
    hz_f = din("hz_f", [128, 1], F32)
    bfw = []
    for l in range(2):
        bfw.append({
            "wqkv": din(f"bf{l}_wqkv", [DM, 3 * IN_], BF16),
            "wo": din(f"bf{l}_wo", [IN_, DM], BF16),
            "qs_r": din(f"bf{l}_qs", [1, 64]),
            "ks_r": din(f"bf{l}_ks", [1, 64]),
            "w1": din(f"bf{l}_w1", [DM, 2 * HID], BF16),
            "w2": din(f"bf{l}_w2", [HID, DM], BF16),
            "vcols": din(f"bf{l}_vc", [128, 16]),
            "b1r": din(f"bf{l}_b1r", [1, 2 * HID], BF16),
            "b2_r": din(f"bf{l}_b2", [1, DM]),
        })
    x2_o = nc.dram_tensor("x2", [TPC, DM], F32, kind=eo).ap()

    s0_d = dint("s0_d", [TPC, DM])
    rout_d = dint("rout_d", [TPC, 2])
    xnfm_d = dint("xnfm_d", [DM, TPC], BF16)
    x1_d = dint("x1_d", [TPC, DM])
    x1fm_d = dint("x1fm_d", [DM, TPC])
    xn1_d = dint("xn1_d", [TPC, DM], BF16)
    xn1fm_d = dint("xn1fm_d", [DM, TPC], BF16)
    xatt_d = dint("xatt_d", [TPC, DM])
    xattfm_d = dint("xattfm_d", [DM, TPC])
    s1_d = dint("s1_d", [TPC, DM])
    xp0_d = dint("xp0_d", [TPC, DM])
    xp0fm_d = dint("xp0fm_d", [DM, TPC])
    y0_d = dint("y0_d", [TPC, DM])
    y0fm_d = dint("y0fm_d", [DM, TPC])
    xp1_d = dint("xp1_d", [TPC, DM])
    xp1fm_d = dint("xp1fm_d", [DM, TPC])
    y1_d = dint("y1_d", [TPC, DM])
    cc_mk_in = dint("cc_mk_in", [128, 8])
    cc_mk_out = dint("cc_mk_out", [256, 8])
    cc_C_in = dint("cc_C_in", [129, 2048])
    cc_C_out = dint("cc_C_out", [129, 2048])
    cc_xn1_in = dint("cc_xn1_in", [DM, 513], BF16)
    cc_xn1_out = dint("cc_xn1_out", [2 * DM, 513], BF16)
    cc_y0_in = dint("cc_y0_in", [DM, 513])
    cc_y0_out = dint("cc_y0_out", [2 * DM, 513])
    cc_xp0_in = dint("cc_xp0_in", [DM, 1])
    cc_xp0_out = dint("cc_xp0_out", [2 * DM, 1])
    cc_xp1_in = dint("cc_xp1_in", [DM, 1])
    cc_xp1_out = dint("cc_xp1_out", [2 * DM, 1])
    xp0c0_d = dint("xp0c0_d", [DM, 1])
    xp1c0_d = dint("xp1c0_d", [DM, 1])

    RG = [[0, 1], [2, 3], [4, 5], [6, 7]]
    p = Pgm(nc, coll=True)
    g = mk_glob(p, nc)
    g["rotm"] = p.sb("glob_rotm", [128, 128], BF16)
    rot_t = nc.inline_tensor(_rot_matrix(), "rot_t")
    p.emit("g", lambda e: e.dma_start(out=g["rotm"][:, :], in_=rot_t.ap()[:, :]),
           dma=True)
    g["hzf"] = p.sb("glob_hzf", [128, 1], F32)
    p.emit("y", lambda e: e.dma_start(out=g["hzf"][:, :], in_=hz_f[:, :]),
           dma=True)
    p.fence()

    p.push("a1_")
    emit_A1(p, g, {
        "x_fm": x_fm, "x_tm": x_tm, "wqkv": a1_wqkv, "wo": a1_wo,
        "g_row": a1_g, "maskA": maskA, "maskB": maskB,
        "s0_o": s0_d, "rout_o": rout_d, "xnfm_o": xnfm_d,
    })
    p.pop()
    p.fence()

    p.push("a2_")
    emit_A2(p, g, {
        "xn_fm": xnfm_d, "rout": rout_d, "x_tm": x_tm[128:WT, :],
        "s0_tm": s0_d, "W1": a2_W1, "b1": a2_b1, "W2": a2_W2, "b2": a2_b2,
        "x1_o": x1_d, "x1fm_o": x1fm_d,
    })
    p.pop()
    p.fence()

    p.push("b1a_")
    emit_B1a(p, g, {
        "x1p_tm": x1_d, "x1p_fm": x1fm_d,
        "wq": b1_wq, "wk": b1_wk, "wv": b1_wv, "wo": b1_wo,
        "vcols": b1_vc, "bo_r": b1_bo,
        "xatt_o": xatt_d, "xn1_o": xn1_d, "xn1fm_o": xn1fm_d,
        "xattfm_o": xattfm_d,
        "cc_mk_in": cc_mk_in, "cc_mk_out": cc_mk_out,
        "cc_C_in": cc_C_in, "cc_C_out": cc_C_out,
    }, pair=True)
    p.pop()
    p.fence()
    ev_t1 = p.emit("y", lambda e: e.dma_start(
        out=cc_xn1_in[:, :], in_=xn1fm_d[:, TPC - 513:TPC]), dma=True)
    p.emit("g", lambda e: e.collective_compute(
        "AllGather", ALU.bypass, replica_groups=RG,
        ins=[cc_xn1_in.opt()], outs=[cc_xn1_out.opt()]),
        waits=[ev_t1], coll=True)
    p.fence()

    p.push("b1b_")
    emit_B1b(p, g, {
        "xatt_tm": xatt_d, "xatt_fm": xattfm_d,
        "wff1": b1_wff1, "wff2": b1_wff2, "w_d": b1_wd,
        "vcols": b1_vc2, "bf1c": b1_bf1c, "bff2_r": b1_bff2, "bd_r": b1_bd,
        "s1_o": s1_d,
    })
    p.pop()
    p.fence()

    p.push("bf0a_")
    emit_Bf_attn(p, g, {
        "y_halo": cc_xn1_out[0:DM, :], "y_own": xn1fm_d, "hz": hz_f,
        "y_tm": xn1_d, "y_tm_cast": True,
        "wqkv": bfw[0]["wqkv"], "wo": bfw[0]["wo"],
        "qs_r": bfw[0]["qs_r"], "ks_r": bfw[0]["ks_r"],
        "tbls": bf_tbls, "masks": bf_masks,
        "xp_o": xp0_d, "xpfm_o": xp0fm_d,
    })
    p.pop()
    p.fence()
    ev_t2 = p.emit("y", lambda e: e.dma_start(
        out=cc_xp0_in[:, :], in_=xp0fm_d[:, TPC - 1:TPC],
        allow_slow_non_contiguous=True), dma=True)
    ev_c2 = p.emit("g", lambda e: e.collective_compute(
        "AllGather", ALU.bypass, replica_groups=RG,
        ins=[cc_xp0_in.opt()], outs=[cc_xp0_out.opt()]),
        waits=[ev_t2], coll=True)
    p.push("g0_")
    c0sb = p.sb("c0sb", [128, 8], F32)
    ev_l = p.emit("y", lambda e: e.dma_start(
        out=c0sb[:].rearrange("p (k c) -> p k c", c=1),
        in_=cc_xp0_out[0:DM, :].rearrange("(k p) c -> p k c", p=128),
        allow_slow_non_contiguous=True),
        waits=[ev_c2], dma=True)
    ev_m = p.emit("v", lambda e: e.tensor_scalar(
        c0sb[:, :], c0sb[:, :], g["hzf"][:, 0:1], None, ALU.mult),
        waits=[ev_l])
    ev_o = p.emit("y", lambda e: e.dma_start(
        out=xp0c0_d.rearrange("(k p) c -> p k c", p=128),
        in_=c0sb[:].rearrange("p (k c) -> p k c", c=1),
        allow_slow_non_contiguous=True),
        waits=[ev_m], dma=True)
    p.pop()
    p.fence()

    p.push("bf0f_")
    emit_Bf_ffn(p, g, {
        "xp_own": xp0fm_d, "xp_c0": xp0c0_d, "xp_tm": xp0_d,
        "w1": bfw[0]["w1"], "w2": bfw[0]["w2"], "vcols": bfw[0]["vcols"],
        "b1r": bfw[0]["b1r"],
        "b2_r": bfw[0]["b2_r"],
        "y_o": y0_d, "yfm_o": y0fm_d,
    })
    p.pop()
    p.fence()
    ev_t3 = p.emit("y", lambda e: e.dma_start(
        out=cc_y0_in[:, :], in_=y0fm_d[:, TPC - 513:TPC]), dma=True)
    p.emit("g", lambda e: e.collective_compute(
        "AllGather", ALU.bypass, replica_groups=RG,
        ins=[cc_y0_in.opt()], outs=[cc_y0_out.opt()]),
        waits=[ev_t3], coll=True)
    p.fence()

    p.push("bf1a_")
    emit_Bf_attn(p, g, {
        "y_halo": cc_y0_out[0:DM, :], "y_own": y0fm_d, "hz": hz_f,
        "y_tm": y0_d,
        "wqkv": bfw[1]["wqkv"], "wo": bfw[1]["wo"],
        "qs_r": bfw[1]["qs_r"], "ks_r": bfw[1]["ks_r"],
        "tbls": bf_tbls, "masks": bf_masks,
        "xp_o": xp1_d, "xpfm_o": xp1fm_d,
    })
    p.pop()
    p.fence()
    ev_t4 = p.emit("y", lambda e: e.dma_start(
        out=cc_xp1_in[:, :], in_=xp1fm_d[:, TPC - 1:TPC],
        allow_slow_non_contiguous=True), dma=True)
    ev_c4 = p.emit("g", lambda e: e.collective_compute(
        "AllGather", ALU.bypass, replica_groups=RG,
        ins=[cc_xp1_in.opt()], outs=[cc_xp1_out.opt()]),
        waits=[ev_t4], coll=True)
    p.push("g1_")
    c1sb = p.sb("c1sb", [128, 8], F32)
    ev_l = p.emit("y", lambda e: e.dma_start(
        out=c1sb[:].rearrange("p (k c) -> p k c", c=1),
        in_=cc_xp1_out[0:DM, :].rearrange("(k p) c -> p k c", p=128),
        allow_slow_non_contiguous=True),
        waits=[ev_c4], dma=True)
    ev_m = p.emit("v", lambda e: e.tensor_scalar(
        c1sb[:, :], c1sb[:, :], g["hzf"][:, 0:1], None, ALU.mult),
        waits=[ev_l])
    ev_o = p.emit("y", lambda e: e.dma_start(
        out=xp1c0_d.rearrange("(k p) c -> p k c", p=128),
        in_=c1sb[:].rearrange("p (k c) -> p k c", c=1),
        allow_slow_non_contiguous=True),
        waits=[ev_m], dma=True)
    p.pop()
    p.fence()

    p.push("bf1f_")
    emit_Bf_ffn(p, g, {
        "xp_own": xp1fm_d, "xp_c0": xp1c0_d, "xp_tm": xp1_d,
        "w1": bfw[1]["w1"], "w2": bfw[1]["w2"], "vcols": bfw[1]["vcols"],
        "b1r": bfw[1]["b1r"],
        "b2_r": bfw[1]["b2_r"],
        "y_o": y1_d,
        "x1_tm": x1_d, "s1_tm": s1_d, "x2_o": x2_o,
    }, final=True)
    p.pop()

    finish_all(p)
    return nc


def fused_in_maps(inputs):
    HID, NCH = 2730, 22
    ins = inputs
    x = np.asarray(ins["L0_attn_x"], np.float32)
    a1maps = _a1_inputs(x)
    W1 = np.ascontiguousarray(np.asarray(ins["L0_ffn_W1"], np.float32).astype(ml_dtypes.bfloat16))
    b1 = np.ascontiguousarray(np.asarray(ins["L0_ffn_b1"], np.float32))
    W2 = np.ascontiguousarray(np.asarray(ins["L0_ffn_W2"], np.float32).astype(ml_dtypes.bfloat16))
    b2 = np.ascontiguousarray(np.asarray(ins["L0_ffn_b2"], np.float32))
    bf_b1r = [np.ascontiguousarray(
        np.asarray(ins["L1_ffn_b1"], np.float32)[l].reshape(1, 2 * HID)
        .astype(ml_dtypes.bfloat16)) for l in range(2)]
    maps = []
    for c in range(NCORE):
        b, h = c // 2, c % 2
        T0 = h * TPC
        hzv = float(h)
        m = {
            "x_fm": a1maps[c]["x_fm"], "x_tm": a1maps[c]["x_tm"],
            "maskA": a1maps[c]["maskA"], "maskB": a1maps[c]["maskB"],
            "a1_wqkv": np.ascontiguousarray(np.asarray(ins["L0_attn_Wqkv"], np.float32).astype(ml_dtypes.bfloat16)),
            "a1_wo": np.ascontiguousarray(np.asarray(ins["L0_attn_Wo"], np.float32).astype(ml_dtypes.bfloat16)),
            "a1_g": np.ascontiguousarray(np.asarray(ins["L0_attn_g"], np.float32).reshape(1, DM)),
            "a2_W1": W1, "a2_b1": b1, "a2_W2": W2, "a2_b2": b2,
            "b1_wq": np.ascontiguousarray(np.asarray(ins["L1_attn_wq"], np.float32).astype(ml_dtypes.bfloat16)),
            "b1_wk": np.ascontiguousarray(np.asarray(ins["L1_attn_wk"], np.float32).astype(ml_dtypes.bfloat16)),
            "b1_wv": np.ascontiguousarray(np.asarray(ins["L1_attn_wv"], np.float32).astype(ml_dtypes.bfloat16)),
            "b1_wo": np.ascontiguousarray(np.asarray(ins["L1_attn_wo"], np.float32).astype(ml_dtypes.bfloat16)),
            "b1_vc": _vcols2(np.asarray(ins["L1_attn_ln1_g"], np.float32),
                             np.asarray(ins["L1_attn_ln1_b"], np.float32)),
            "b1_bo": np.ascontiguousarray(np.asarray(ins["L1_attn_bo"], np.float32).reshape(1, DM)),
            "b1_wff1": np.ascontiguousarray(np.asarray(ins["L1_attn_w_ff1"], np.float32).astype(ml_dtypes.bfloat16)),
            "b1_wff2": np.ascontiguousarray(np.asarray(ins["L1_attn_w_ff2"], np.float32).astype(ml_dtypes.bfloat16)),
            "b1_wd": np.ascontiguousarray(np.asarray(ins["L1_attn_w_d"], np.float32).astype(ml_dtypes.bfloat16)),
            "b1_vc2": _vcols2(np.asarray(ins["L1_attn_ln2_g"], np.float32),
                              np.asarray(ins["L1_attn_ln2_b"], np.float32)),
            "b1_bff2": np.ascontiguousarray(np.asarray(ins["L1_attn_b_ff2"], np.float32).reshape(1, DM)),
            "b1_bd": np.ascontiguousarray(np.asarray(ins["L1_attn_b_d"], np.float32).reshape(1, DM)),
            "bf_tbls": _bf_tables(T0),
            "bf_masks": _bf_masks(h == 0),
            "hz_bf": np.full((128, 1), hzv, ml_dtypes.bfloat16),
            "hz_f": np.full((128, 1), hzv, np.float32),
        }
        bf1 = np.asarray(ins["L1_attn_b_ff1"], np.float32)
        bf1c = np.zeros((128, 32), np.float32)
        for cc in range(32):
            bf1c[:, cc] = bf1[cc * 128:(cc + 1) * 128]
        m["b1_bf1c"] = bf1c
        for l in range(2):
            m[f"bf{l}_wqkv"] = np.ascontiguousarray(np.asarray(ins["L1_ffn_wqkv"], np.float32)[l].astype(ml_dtypes.bfloat16))
            m[f"bf{l}_wo"] = np.ascontiguousarray(np.asarray(ins["L1_ffn_wo"], np.float32)[l].astype(ml_dtypes.bfloat16))
            m[f"bf{l}_qs"] = np.ascontiguousarray(np.asarray(ins["L1_ffn_q_scale"], np.float32)[l].reshape(1, 64))
            m[f"bf{l}_ks"] = np.ascontiguousarray(np.asarray(ins["L1_ffn_k_scale"], np.float32)[l].reshape(1, 64))
            m[f"bf{l}_w1"] = np.ascontiguousarray(np.asarray(ins["L1_ffn_w1"], np.float32)[l].astype(ml_dtypes.bfloat16))
            m[f"bf{l}_w2"] = np.ascontiguousarray(np.asarray(ins["L1_ffn_w2"], np.float32)[l].astype(ml_dtypes.bfloat16))
            m[f"bf{l}_vc"] = _vcols2(np.asarray(ins["L1_ffn_ln_g"], np.float32)[l],
                                     np.asarray(ins["L1_ffn_ln_b"], np.float32)[l])
            m[f"bf{l}_b1r"] = bf_b1r[l]
            m[f"bf{l}_b2"] = np.ascontiguousarray(np.asarray(ins["L1_ffn_b2"], np.float32)[l].reshape(1, DM))
        maps.append(m)
    return maps


def run_fused(inputs):
    nc = build_fused()
    maps = fused_in_maps(inputs)
    res = run_bass_kernel_spmd(nc, maps, core_ids=list(range(NCORE)))
    return res.results


def kernel(**inputs):
    ins = {k: np.asarray(v) for k, v in inputs.items()}
    res = run_fused(ins)
    out = np.zeros((4, 2 * TPC, DM), np.float32)
    for c in range(NCORE):
        b, h = c // 2, c % 2
        out[b, h * TPC:(h + 1) * TPC] = np.asarray(res[c]["x2"], np.float32)
    return out


def kernel_unfused(**inputs):
    ins = {k: np.asarray(v) for k, v in inputs.items()}
    a1 = run_A1(ins)
    a2 = run_A2(ins, a1)
    x1_by_core = [np.asarray(a2[c]["x1"], np.float32) for c in range(NCORE)]
    b1a = run_B1a(ins, x1_by_core)
    b1b = run_B1b(ins, [np.asarray(b1a[c]["xatt"], np.float32)
                        for c in range(NCORE)])
    s1_by_core = [np.asarray(b1b[c]["s1"], np.float32) for c in range(NCORE)]
    xn1_by_core = [np.asarray(b1a[c]["xn1_bf"], np.float32)
                   for c in range(NCORE)]
    f0a = run_Bf_attn(ins, 0, xn1_by_core)
    f0f = run_Bf_ffn(ins, 0, [f0a[c]["xp"] for c in range(NCORE)])
    y0 = [np.asarray(f0f[c]["y_out"], np.float32) for c in range(NCORE)]
    f1a = run_Bf_attn(ins, 1, y0)
    f1f = run_Bf_ffn(ins, 1, [f1a[c]["xp"] for c in range(NCORE)],
                     final=True, x1_by_core=x1_by_core, s1_by_core=s1_by_core)
    out = np.zeros((4, 2 * TPC, DM), np.float32)
    for c in range(NCORE):
        b, h = c // 2, c % 2
        out[b, h * TPC:(h + 1) * TPC] = np.asarray(f1f[c]["x2"], np.float32)
    return out


def emit_A2(p, g, io):
    E, DFF, HD = 8, 4096, 2048
    xn_fm = io["xn_fm"]
    rout = io["rout"]
    x_tm = io["x_tm"]
    s0_tm = io["s0_tm"]
    W1 = io["W1"]
    b1 = io["b1"]
    W2 = io["W2"]
    b2 = io["b2"]
    x1_o = io["x1_o"]
    moe_o = io.get("moe_o")
    x1fm_o = io.get("x1fm_o")
    idf = g["idf"]
    ev_idf = None

    xn_g = [p.sb(f"xn_g{i}", [128, 8 * 512], BF16) for i in range(2)]
    y_acc = p.sb("y_acc", [128, 16 * 1024], F32)
    w1_sb = [p.sb(f"w1_sb{i}", [128, 8 * HD], BF16) for i in range(2)]
    w2_sb = p.sb("w2_sb", [128, 16 * 1024], BF16)
    hr_sb = p.sb("hr_sb", [128, 16 * 512], BF16)
    rt_sb = p.sb("rt_sb", [128, 16 * 2], F32)
    w_sb = p.sb("w_sb", [128, E * 16], F32)
    sR = p.sb("sR", [128, 16], F32)
    aR = p.sb("aR", [128, 16], F32)
    fR = p.sb("fR", [128, 16], F32)
    gR = p.sb("gR", [128, 16], F32)
    tR = p.sb("tR", [128, 16], F32)
    b1cp = p.sb("b1cp", [16, 128], F32)
    b1t = p.sb("b1t", [128, 16], F32)
    b2bf = p.sb("b2bf", [1, DM], BF16)
    ones_r = p.sb("ones_r", [1, 128], BF16)
    x1f_st = p.sb("x1f_st", [128, 512], F32)
    bx1f = Buf()
    xs = [p.sb("xs0", [128, DM], F32)] * 2
    ss = [p.sb("ss0", [128, DM], F32)] * 2
    PA = p.ps("PA", [128, 512])
    PB = p.ps("PB", [128, 512])
    PC = p.ps("PC", [128, 512])
    PD = p.ps("PD", [128, 512])
    PT = p.ps("PT", [128, 128])

    bxn = [Buf(), Buf()]
    by = Buf()
    bw1 = [Buf(), Buf()]
    bw2 = Buf()
    bhr = Buf()
    bPA, bPB, bPC, bPD, bPT = Buf(), Buf(), Buf(), Buf(), Buf()
    bb1cp, bb1t, bb2 = Buf(), Buf(), Buf()
    _b1_, _b2_ = Buf(), Buf()
    bxs = [_b1_] * 2
    bss = [_b2_] * 2

    ev_rt = p.emit("y", lambda e: e.dma_start(
        out=rt_sb[:].rearrange("p (t c) -> p t c", c=2),
        in_=rout.rearrange("(t p) c -> p t c", p=128)), dma=True)
    ev_ones = p.emit("v", lambda e: e.memset(ones_r[:, :], 1.0))
    ev_y0 = p.emit("v", lambda e: e.memset(y_acc[:, :], 0.0))
    by.wrote(ev_y0)
    rts = rt_sb[:].rearrange("p (t c) -> p t c", c=2)
    ev = p.emit("v", lambda e: e.tensor_tensor(
        sR[:, :], rts[:, :, 0], rts[:, :, 1], ALU.add), waits=[ev_rt])
    ev = p.emit("s", lambda e: e.activation(aR[:, :], sR[:, :], AF.Abs), waits=[ev])
    evf = p.emit("v", lambda e: e.memset(fR[:, :], 0.0))
    for k in range(1, 64):
        p.emit("v", lambda e, k=k: e.tensor_single_scalar(
            tR[:, :], aR[:, :], float(k), ALU.is_ge), waits=[ev] if k == 1 else [])
        evf = p.emit("v", lambda e: e.tensor_tensor(
            fR[:, :], fR[:, :], tR[:, :], ALU.add))
    ev_sg = p.emit("s", lambda e: e.sign(gR[:, :], sR[:, :]))
    ev_t = p.emit("v", lambda e: e.tensor_tensor(
        tR[:, :], gR[:, :], fR[:, :], ALU.mult), waits=[ev_sg, evf])
    ev_u = p.emit("v", lambda e: e.tensor_scalar(
        aR[:, :], tR[:, :], 0.125, 8.0, ALU.mult, ALU.add))
    evf2 = p.emit("v", lambda e: e.memset(fR[:, :], 0.0))
    for k in range(1, 16):
        p.emit("v", lambda e, k=k: e.tensor_single_scalar(
            gR[:, :], aR[:, :], float(k), ALU.is_ge))
        evf2 = p.emit("v", lambda e: e.tensor_tensor(
            fR[:, :], fR[:, :], gR[:, :], ALU.add))
    ev_h = p.emit("v", lambda e: e.scalar_tensor_tensor(
        sR[:, :], fR[:, :], -8.0, tR[:, :], ALU.mult, ALU.add))
    ev_h = p.emit("v", lambda e: e.tensor_scalar(
        sR[:, :], sR[:, :], 1.0, 64.0, ALU.mult, ALU.add))
    ev_w = None
    for e_ in range(E):
        p.emit("v", lambda e, e_=e_: e.tensor_single_scalar(
            tR[:, :], sR[:, :], float(e_), ALU.is_equal))
        p.emit("v", lambda e, e_=e_: e.tensor_single_scalar(
            gR[:, :], sR[:, :], float((e_ - 1) % E), ALU.is_equal))
        p.emit("v", lambda e, e_=e_: e.tensor_tensor(
            w_sb[:, 16 * e_:16 * (e_ + 1)], tR[:, :], gR[:, :], ALU.add))
        ev_w = p.emit("v", lambda e, e_=e_: e.tensor_scalar(
            w_sb[:, 16 * e_:16 * (e_ + 1)], w_sb[:, 16 * e_:16 * (e_ + 1)],
            0.5, None, ALU.mult))

    NTG = 4
    it = 0
    for e_ in range(E):
        ev_b2 = p.emit("g", lambda e, e_=e_: e.dma_start(
            out=b2bf[:, :], in_=b2[e_:e_ + 1, :]), waits=bb2.w_waits(), dma=True)
        bb2.wrote(ev_b2)
        for Hh in range(2):
            wbuf, bwb = w1_sb[it % 2], bw1[it % 2]
            evs_w1 = []
            for k in range(8):
                evs_w1.append(p.emit("g", lambda e, e_=e_, Hh=Hh, k=k, wbuf=wbuf: e.dma_start(
                    out=wbuf[:, k * HD:(k + 1) * HD],
                    in_=W1[e_, k * 128:(k + 1) * 128, Hh * HD:(Hh + 1) * HD]),
                    waits=bwb.w_waits() if k == 0 else [], dma=True))
            bwb.wrote(evs_w1[-1])
            for ev2_ in evs_w1:
                bwb.wrote(ev2_)
            evs_w2 = []
            for q in range(4):
                evs_w2.append(p.emit("g", lambda e, e_=e_, Hh=Hh, q=q: e.dma_start(
                    out=w2_sb[:, q * 4096:(q + 1) * 4096].rearrange(
                        "p (c n) -> p c n", n=1024),
                    in_=W2.rearrange("e (c p) n -> e p c n", p=128)[
                        e_, :, Hh * 16 + q * 4:Hh * 16 + (q + 1) * 4, :]),
                    waits=bw2.w_waits() if q == 0 else [], dma=True))
            for ev2_ in evs_w2:
                bw2.wrote(ev2_)
            ev_b1 = p.emit("y", lambda e, e_=e_, Hh=Hh: e.dma_start(
                out=b1cp[:, :],
                in_=b1.rearrange("e (c p) -> e c p", p=128)[e_, Hh * 16:(Hh + 1) * 16, :]),
                waits=bb1cp.w_waits(), dma=True)
            bb1cp.wrote(ev_b1)
            ev_b1t = p.emit("t", lambda e: e.transpose(
                PT[:, 0:16], b1cp[:, :], idf[0:16, 0:16]),
                waits=[ev_b1, ev_idf] + bPT.w_waits())
            bPT.wrote(ev_b1t)
            bb1cp.read(ev_b1t)
            ev_b1c = p.emit("v", lambda e: e.tensor_copy(b1t[:, :], PT[:, 0:16]),
                            waits=[ev_b1t] + bb1t.w_waits())
            bPT.read(ev_b1c)
            bb1t.wrote(ev_b1c)
            for gg in range(NTG):
                xbuf, bxb = xn_g[gg % 2], bxn[gg % 2]
                ev_x = p.emit("y", lambda e, gg=gg, xbuf=xbuf: e.dma_start(
                    out=xbuf[:].rearrange("p (k n) -> p k n", n=512),
                    in_=xn_fm.rearrange("(k p) n -> p k n", p=128)[
                        :, :, gg * 512:(gg + 1) * 512]),
                    waits=bxb.w_waits(), dma=True)
                bxb.wrote(ev_x)
                for c in range(16):
                    pb, bb = (PA, bPA) if c % 2 == 0 else (PB, bPB)
                    ev_mm = None
                    for k in range(8):
                        ev_mm = p.emit("t", lambda e, c=c, k=k, pb=pb, wbuf=wbuf, xbuf=xbuf: e.matmul(
                            pb[:, :],
                            wbuf[:, k * HD + c * 128:k * HD + (c + 1) * 128],
                            xbuf[:, k * 512:(k + 1) * 512],
                            start=(k == 0), stop=(k == 7)),
                            waits=(bb.w_waits() + [ev_x] + evs_w1) if k == 0 else [],
                            sig=(k == 7))
                    bb.wrote(ev_mm)
                    bxb.read(ev_mm)
                    bwb.read(ev_mm)
                    ev_r = p.emit("s", lambda e, c=c, pb=pb: e.activation(
                        hr_sb[:, c * 512:(c + 1) * 512], pb[:, :], AF.Relu,
                        bias=b1t[:, c:c + 1]),
                        waits=[ev_mm, ev_b1c] + (bhr.w_waits() if c == 0 else []))
                    bb.read(ev_r)
                    bb1t.read(ev_r)
                    bhr.wrote(ev_r)
                ev_hr = bhr.write_list[-1]
                for tt in range(4):
                    tok = gg * 4 + tt
                    wcol = 16 * e_ + tok
                    for nh in range(2):
                        pb, bb = (PC, bPC) if (tt * 2 + nh) % 2 == 0 else (PD, bPD)
                        first = True
                        ev_mm = None
                        if Hh == 0:
                            ev_mm = p.emit("t", lambda e, nh=nh, pb=pb: e.matmul(
                                pb[:, :], ones_r[:, :],
                                b2bf[:, nh * 512:(nh + 1) * 512],
                                start=True, stop=False),
                                waits=bb.w_waits() + [ev_b2, ev_ones, ev_hr] + evs_w2,
                                sig=False)
                            first = False
                        for c in range(16):
                            ev_mm = p.emit("t", lambda e, c=c, tt=tt, nh=nh, pb=pb, first=first: e.matmul(
                                pb[:, :],
                                hr_sb[:, c * 512 + tt * 128:c * 512 + tt * 128 + 128],
                                w2_sb[:, c * 1024 + nh * 512:c * 1024 + (nh + 1) * 512],
                                start=(first and c == 0), stop=(c == 15)),
                                waits=(bb.w_waits() + [ev_hr] + evs_w2) if c == 0 and first else [],
                                sig=(c == 15))
                        bb.wrote(ev_mm)
                        bhr.read(ev_mm)
                        bw2.read(ev_mm)
                        ev_st = p.emit("v", lambda e, tok=tok, nh=nh, pb=pb, wcol=wcol: e.scalar_tensor_tensor(
                            y_acc[:, tok * 1024 + nh * 512:tok * 1024 + (nh + 1) * 512],
                            pb[:, :], w_sb[:, wcol:wcol + 1],
                            y_acc[:, tok * 1024 + nh * 512:tok * 1024 + (nh + 1) * 512],
                            ALU.mult, ALU.add),
                            waits=[ev_mm, ev_w] + by.w_waits())
                        bb.read(ev_st)
                        by.wrote(ev_st)
                        by.read(ev_st)
            it += 1

    for tt in range(16):
        xb, bxb2 = xs[tt % 2], bxs[tt % 2]
        sb_, bsb2 = ss[tt % 2], bss[tt % 2]
        ya = y_acc[:, tt * 1024:(tt + 1) * 1024]
        ev_lx = p.emit("y", lambda e, tt=tt, xb=xb: e.dma_start(
            out=xb[:, :], in_=x_tm[tt * 128:(tt + 1) * 128, :]),
            waits=bxb2.w_waits(), dma=True)
        bxb2.wrote(ev_lx)
        ev_ls = p.emit("y", lambda e, tt=tt, sb_=sb_: e.dma_start(
            out=sb_[:, :], in_=s0_tm[tt * 128:(tt + 1) * 128, :]),
            waits=bsb2.w_waits(), dma=True)
        bsb2.wrote(ev_ls)
        ev_mo = None
        if moe_o is not None:
            ev_mo = p.emit("y", lambda e, tt=tt, ya=ya: e.dma_start(
                out=moe_o[tt * 128:(tt + 1) * 128, :], in_=ya),
                waits=by.write_list, dma=True)
            by.read(ev_mo)
        ev_si = p.emit("s", lambda e, ya=ya: e.activation(
            ya, ya, AF.Silu), waits=by.write_list + [ev_mo])
        by.wrote(ev_si)
        ev_m = p.emit("v", lambda e, ya=ya, sb_=sb_: e.tensor_tensor(
            ya, ya, sb_[:, :], ALU.mult), waits=[ev_si, ev_ls])
        bsb2.read(ev_m)
        by.wrote(ev_m)
        ev_a = p.emit("v", lambda e, ya=ya, xb=xb: e.tensor_tensor(
            ya, ya, xb[:, :], ALU.add), waits=[ev_lx])
        bxb2.read(ev_a)
        by.wrote(ev_a)
        ev_o = p.emit("y", lambda e, tt=tt, ya=ya: e.dma_start(
            out=x1_o[tt * 128:(tt + 1) * 128, :], in_=ya),
            waits=[ev_a], dma=True)
        by.read(ev_o)
        if x1fm_o is not None:
            for hf in range(2):
                pb, bb = (PA, bPA) if hf == 0 else (PB, bPB)
                evts = []
                for kc in range(4):
                    f8 = hf * 4 + kc
                    evts.append(p.emit("t", lambda e, tt=tt, f8=f8, kc=kc, pb=pb: e.transpose(
                        pb[:, kc * 128:(kc + 1) * 128],
                        y_acc[:, tt * 1024 + f8 * 128:tt * 1024 + (f8 + 1) * 128],
                        idf[:, :]),
                        waits=([ev_a] + bb.w_waits()) if kc == 0 else [],
                        sig=(kc == 3)))
                evt = evts[-1]
                bb.wrote(evt)
                by.read(evt)
                evc_ = p.emit("v", lambda e, pb=pb: e.tensor_copy(
                    x1f_st[:, :], pb[:, :]),
                    waits=[evt] + bx1f.w_waits())
                bb.read(evc_)
                bx1f.wrote(evc_)
                ev_fo = p.emit("y", lambda e, tt=tt, hf=hf: e.dma_start(
                    out=x1fm_o.rearrange("(k p) n -> p k n", p=128)[
                        :, hf * 4:(hf + 1) * 4, tt * 128:(tt + 1) * 128],
                    in_=x1f_st[:].rearrange("p (k n) -> p k n", n=128)),
                    waits=[evc_], dma=True)
                bx1f.read(ev_fo)


def build_A2():
    nc = bass.Bass("TRN2", target_bir_lowering=False)
    E, DFF = 8, 4096
    io = {
        "xn_fm": nc.dram_tensor("xn_fm", [DM, TPC], BF16, kind="ExternalInput").ap(),
        "rout": nc.dram_tensor("rout", [TPC, 2], F32, kind="ExternalInput").ap(),
        "x_tm": nc.dram_tensor("x_tm", [TPC, DM], F32, kind="ExternalInput").ap(),
        "s0_tm": nc.dram_tensor("s0_tm", [TPC, DM], F32, kind="ExternalInput").ap(),
        "W1": nc.dram_tensor("W1", [E, DM, DFF], F32, kind="ExternalInput").ap(),
        "b1": nc.dram_tensor("b1", [E, DFF], F32, kind="ExternalInput").ap(),
        "W2": nc.dram_tensor("W2", [E, DFF, DM], F32, kind="ExternalInput").ap(),
        "b2": nc.dram_tensor("b2", [E, DM], F32, kind="ExternalInput").ap(),
        "x1_o": nc.dram_tensor("x1", [TPC, DM], F32, kind="ExternalOutput").ap(),
        "moe_o": nc.dram_tensor("moe", [TPC, DM], F32, kind="ExternalOutput").ap(),
    }
    p = Pgm(nc)
    g = mk_glob(p, nc)
    p.fence()
    emit_A2(p, g, io)
    finish_all(p)
    return nc


def run_A2(inputs, a1_results):
    nc = build_A2()
    W1 = np.ascontiguousarray(np.asarray(inputs["L0_ffn_W1"], np.float32))
    b1 = np.ascontiguousarray(np.asarray(inputs["L0_ffn_b1"], np.float32))
    W2 = np.ascontiguousarray(np.asarray(inputs["L0_ffn_W2"], np.float32))
    b2 = np.ascontiguousarray(np.asarray(inputs["L0_ffn_b2"], np.float32))
    x = np.asarray(inputs["L0_attn_x"], np.float32)
    maps = []
    for c in range(NCORE):
        b, h = c // 2, c % 2
        T0 = h * TPC
        xn_fm = np.ascontiguousarray(np.asarray(a1_results[c]["xn_bf"]).T)
        maps.append({
            "xn_fm": xn_fm,
            "rout": np.ascontiguousarray(a1_results[c]["rout"]),
            "x_tm": np.ascontiguousarray(x[b, T0:T0 + TPC]),
            "s0_tm": np.ascontiguousarray(a1_results[c]["s0"]),
            "W1": W1, "b1": b1, "W2": W2, "b2": b2,
        })
    res = run_bass_kernel_spmd(nc, maps, core_ids=list(range(NCORE)))
    return res.results


def emit_B1a(p, g, io, pair=False):
    NT = TPC if pair else 2 * TPC
    x1p_tm = io["x1p_tm"]
    x1p_fm = io["x1p_fm"]
    wq = io["wq"]
    wk = io["wk"]
    wv = io["wv"]
    wo = io["wo"]
    vcols = io["vcols"]
    bo_r = io["bo_r"]
    xatt_o = io["xatt_o"]
    xn1_o = io["xn1_o"]
    xn1fm_o = io.get("xn1fm_o")
    xattfm_o = io.get("xattfm_o")
    if pair:
        cc_mk_in = io["cc_mk_in"]
        cc_mk_out = io["cc_mk_out"]
        cc_C_in = io["cc_C_in"]
        cc_C_out = io["cc_C_out"]
    idb = g["idb"]
    idf = g["idf"]
    ev_idb = ev_idf = None

    NTL = NT // 128
    wQ = p.sb("wQ", [128, 8 * DM], BF16)
    wK = p.sb("wK", [128, 8 * DM], BF16)
    wV = p.sb("wV", [128, 8 * DM], BF16)
    wO = p.sb("wO", [128, 8 * DM], BF16)
    bo_s = p.sb("bo_s", [1, DM], BF16)
    xn1f_st = p.sb("xn1f_st", [128, 1024], BF16)
    bxn1f = Buf()
    xaf_st = p.sb("xaf_st", [128, 512], F32)
    bxaf = Buf()
    vc = p.sb("vc", [128, 16], F32)
    st32a = p.sb("st32a", [128, 32], F32)
    st32b = p.sb("st32b", [128, 32], F32)
    st32c = p.sb("st32c", [128, 32], F32)
    st32d = p.sb("st32d", [128, 32], F32)
    R12 = p.sb("R12", [128, 32], F32)
    A12 = p.sb("A12", [128, 32], F32)
    Rrow = p.sb("Rrow", [1, NT], BF16)
    Arow = p.sb("Arow", [1, NT], BF16)
    ones1c = p.sb("ones1c", [128, 1], F32)
    onesb = p.sb("onesb", [1, 128], BF16)
    onesf = p.sb("onesf", [1, 128], F32)
    onescol = p.sb("onescol", [128, 1], BF16)
    tmpd = p.sb("tmpd", [128, 1], F32)
    xtt = [p.sb(f"xtt{i}", [128, DM], F32) for i in range(2)]
    xft = p.sb("xft", [128, 8 * 512], F32)
    xh = p.sb("xh", [128, 8 * 512], BF16)
    xn1t = p.sb("xn1t", [128, DM], BF16)
    mk = p.sb("mk", [128, 8], F32)
    nmk = p.sb("nmk", [128, 8], F32)
    tmp8 = p.sb("tmp8", [128, 8], F32)
    mbro = p.sb("mbro", [128, DM], F32)
    ke_tm = p.sb("ke_tm", [128, 4 * DM], BF16)
    v_tm = p.sb("v_tm", [128, 4 * DM], BF16)
    Cacc = p.sb("Cacc", [128, 8 * 256], F32)
    C_sb = p.sb("C_sb", [128, 8 * 256], BF16)
    lkrow = p.sb("lkrow", [1, DM], F32)
    mrow = p.sb("mrow", [1, DM], F32)
    rlc = p.sb("rlc", [128, 8], F32)
    q_tm = p.sb("q_tm", [128, DM], BF16)
    qT = p.sb("qT", [128, 8 * 128], BF16)
    at_fm = p.sb("at_fm", [128, 8 * 128], BF16)
    mq = p.sb("mq", [128, 4], F32)
    lq = [p.sb(f"lq{i}", [128, 1], F32) for i in range(4)]
    rq = p.sb("rq", [128, 4], F32)
    x_att = p.sb("x_att", [128, 16 * DM], BF16)
    PA = p.ps("PA", [128, 512])
    PB = p.ps("PB", [128, 512])
    PC = p.ps("PC", [128, 512])
    PD = p.ps("PD", [128, 512])
    PE_ = p.ps("PE_", [128, 512])
    PF = p.ps("PF", [128, 256], BF16)
    PG = p.ps("PG", [128, 256], BF16)
    PR = p.ps("PR", [1, 512])

    bwQ, bwK, bwV, bwO = Buf(), Buf(), Buf(), Buf()
    bPA, bPB, bPC, bPD, bPE, bPF, bPG, bPR = (Buf() for _ in range(8))
    bxtt = [Buf(), Buf()]
    bxft, bxh, bxn1t = Buf(), Buf(), Buf()
    bst, brows = Buf(), Buf()
    bke, bv_, bCa, bC = Buf(), Buf(), Buf(), Buf()
    bmk, bmb, blk, bmrow, brlc = Buf(), Buf(), Buf(), Buf(), Buf()
    bq, bqT, batf = Buf(), Buf(), Buf()
    bxa = Buf()
    bxc2, bhh = Buf(), Buf()
    bx3 = [Buf(), Buf()]
    bx3f = Buf()
    bs1t = [Buf(), Buf()]
    btmpd = Buf()

    ev_vc = p.emit("y", lambda e: e.dma_start(out=vc[:, :], in_=vcols[:, :]), dma=True)
    ev_bo = p.emit("g", lambda e: e.dma_start(out=bo_s[:, :], in_=bo_r[:, :]), dma=True)
    ev_o1 = p.emit("v", lambda e: e.memset(ones1c[:, :], 1.0))
    ev_ob = p.emit("v", lambda e: e.memset(onesb[:, :], 1.0))
    ev_of = p.emit("v", lambda e: e.memset(onesf[:, :], 1.0))
    ev_oc = p.emit("v", lambda e: e.memset(onescol[:, :], 1.0))

    def load_w(tgt, btgt, src):
        evs = []
        for k in range(8):
            evs.append(p.emit("g", lambda e, k=k, tgt=tgt, src=src: e.dma_start(
                out=tgt[:, k * DM:(k + 1) * DM],
                in_=src[k * 128:(k + 1) * 128, :]),
                waits=btgt.w_waits() if k == 0 else [], dma=True))
        for ev_ in evs:
            btgt.wrote(ev_)
        return evs

    evs_wq = load_w(wQ, bwQ, wq)
    evs_wk = load_w(wK, bwK, wk)
    evs_wv = load_w(wV, bwV, wv)
    evs_wo = load_w(wO, bwO, wo)

    if pair:
        for tset in (st32a, st32b, st32c, st32d, R12, A12):
            ev0 = p.emit("v", lambda e, tset=tset: e.memset(tset[:, :], 0.0))
        bst.wrote(ev0)
    for j in range(NTL):
        buf, bb = xtt[j % 2], bxtt[j % 2]
        ev_ld = p.emit("y", lambda e, j=j, buf=buf: e.dma_start(
            out=buf[:, :], in_=x1p_tm[j * 128:(j + 1) * 128, :]),
            waits=bb.w_waits(), dma=True)
        bb.wrote(ev_ld)
        ev1 = p.emit("v", lambda e, buf=buf, j=j: e.tensor_reduce(
            st32a[:, j:j + 1], buf[:, :], axis=AX.X, op=ALU.add),
            waits=[ev_ld] + (bst.w_waits() if j == 0 else []))
        bb.read(ev1)
        ev_sq = p.emit("s", lambda e, buf=buf, j=j: e.activation(
            xft[:, 0:DM], buf[:, :], AF.Square),
            waits=[ev_ld] + bxft.w_waits())
        bb.read(ev_sq)
        bxft.wrote(ev_sq)
        ev2 = p.emit("v", lambda e, j=j: e.tensor_reduce(
            st32b[:, j:j + 1], xft[:, 0:DM], axis=AX.X, op=ALU.add),
            waits=[ev_sq])
        bxft.read(ev2)
    ev1f = p.emit("v", lambda e: e.engine_nop(),
                  waits=[(ev1[0], ev1[1], True), (ev2[0], ev2[1], True)],
                  force=True)
    pe = ev1f

    def vstep(fn, extra=()):
        nonlocal pe
        pe = p.emit("v", fn, waits=[pe] + list(extra), force=True)
        return pe

    vstep(lambda e: e.tensor_scalar(st32c[:, :], st32a[:, :], 1.0 / DM, None, ALU.mult))
    vstep(lambda e: e.tensor_tensor(st32d[:, :], st32c[:, :], st32c[:, :], ALU.mult))
    vstep(lambda e: e.scalar_tensor_tensor(st32b[:, :], st32b[:, :], 1.0 / DM,
                                           st32d[:, :], ALU.mult, ALU.subtract))
    ev_ve = vstep(lambda e: e.tensor_scalar(st32a[:, :], st32b[:, :], 1.0, 1e-5,
                                            ALU.mult, ALU.add))
    ev_sa = p.emit("s", lambda e: e.activation(R12[:, :], st32a[:, :], AF.Sqrt),
                   waits=[ev_ve])
    vstep(lambda e: e.reciprocal(R12[:, :], R12[:, :]), extra=[ev_sa])
    for _ in range(2):
        vstep(lambda e: e.tensor_tensor(st32d[:, :], R12[:, :], R12[:, :], ALU.mult))
        vstep(lambda e: e.tensor_tensor(st32d[:, :], st32d[:, :], st32a[:, :], ALU.mult))
        vstep(lambda e: e.tensor_scalar(st32d[:, :], st32d[:, :], -0.5, 1.5, ALU.mult, ALU.add))
        vstep(lambda e: e.tensor_tensor(R12[:, :], R12[:, :], st32d[:, :], ALU.mult))
    vstep(lambda e: e.tensor_scalar(st32d[:, :], st32c[:, :], -1.0, None, ALU.mult))
    ev_a1 = vstep(lambda e: e.tensor_tensor(A12[:, :], st32d[:, :], R12[:, :], ALU.mult))
    for j in range(16):
        buf, bb = xtt[j % 2], bxtt[j % 2]
        ev_ld2 = p.emit("y", lambda e, j=j, buf=buf: e.dma_start(
            out=buf[:, :], in_=x1p_tm[j * 128:(j + 1) * 128, :]),
            waits=bb.w_waits(), dma=True)
        bb.wrote(ev_ld2)
        evx = p.emit("v", lambda e, buf=buf, j=j: e.tensor_scalar(
            xn1t[:, :], buf[:, :], R12[:, j:j + 1], A12[:, j:j + 1],
            ALU.mult, ALU.add),
            waits=[ev_a1, ev_ld2] + bxn1t.w_waits())
        bb.read(evx)
        bst.read(evx)
        bxn1t.wrote(evx)
        evxo = p.emit("y", lambda e, j=j: e.dma_start(
            out=xn1_o[j * 128:(j + 1) * 128, :], in_=xn1t[:, :]),
            waits=[evx], dma=True)
        bxn1t.read(evxo)
        if xn1fm_o is not None:
            evcs = []
            for f8 in range(8):
                psT, bbT = (PF, bPF) if f8 % 2 == 0 else (PG, bPG)
                evt = p.emit("t", lambda e, f8=f8, psT=psT: e.transpose(
                    psT[:, 0:128], xn1t[:, f8 * 128:(f8 + 1) * 128], idb[:, :]),
                    waits=[evx] + bbT.w_waits())
                bbT.wrote(evt)
                bxn1t.read(evt)
                evc_ = p.emit("v", lambda e, f8=f8, psT=psT: e.tensor_copy(
                    xn1f_st[:, f8 * 128:(f8 + 1) * 128], psT[:, 0:128]),
                    waits=[evt] + (bxn1f.w_waits() if f8 == 0 else []))
                bbT.read(evc_)
                bxn1f.wrote(evc_)
                evcs.append(evc_)
            ev_fo = p.emit("y", lambda e, j=j: e.dma_start(
                out=xn1fm_o.rearrange("(k p) n -> p k n", p=128)[:, :, j * 128:(j + 1) * 128],
                in_=xn1f_st[:].rearrange("p (k n) -> p k n", n=128)),
                waits=evcs, dma=True)
            bxn1f.read(ev_fo)
    vstep(lambda e: e.tensor_tensor(st32d[:, :], R12[:, :], R12[:, :], ALU.mult))
    vstep(lambda e: e.tensor_tensor(st32d[:, :], st32d[:, :], st32b[:, :], ALU.mult))
    ev_v2 = vstep(lambda e: e.tensor_scalar(st32b[:, :], st32d[:, :], 1.0, 1e-5,
                                            ALU.mult, ALU.add))
    ev_sb2 = p.emit("s", lambda e: e.activation(st32d[:, :], st32b[:, :], AF.Sqrt),
                    waits=[ev_v2])
    vstep(lambda e: e.reciprocal(st32d[:, :], st32d[:, :]), extra=[ev_sb2])
    for _ in range(2):
        vstep(lambda e: e.tensor_tensor(st32a[:, :], st32d[:, :], st32d[:, :], ALU.mult))
        vstep(lambda e: e.tensor_tensor(st32a[:, :], st32a[:, :], st32b[:, :], ALU.mult))
        vstep(lambda e: e.tensor_scalar(st32a[:, :], st32a[:, :], -0.5, 1.5, ALU.mult, ALU.add))
        vstep(lambda e: e.tensor_tensor(st32d[:, :], st32d[:, :], st32a[:, :], ALU.mult))
    vstep(lambda e: e.tensor_tensor(A12[:, :], A12[:, :], st32d[:, :], ALU.mult))
    ev_RA = vstep(lambda e: e.tensor_tensor(R12[:, :], R12[:, :], st32d[:, :], ALU.mult))
    bst.wrote(ev_RA)

    for j in range(NTL):
        seg = slice(j * 128, (j + 1) * 128)
        evd1 = p.emit("v", lambda e, j=j: e.tensor_scalar(
            tmpd[:, :], ones1c[:, :], R12[:, j:j + 1], None, ALU.mult),
            waits=[ev_RA, ev_o1] + btmpd.w_waits())
        bst.read(evd1)
        btmpd.wrote(evd1)
        evt1 = p.emit("t", lambda e: e.transpose(
            PR[0:1, 0:128], tmpd[:, :], idf[:, :]),
            waits=[evd1, ev_idf] + bPR.w_waits())
        bPR.wrote(evt1)
        btmpd.read(evt1)
        evc1 = p.emit("v", lambda e, seg=seg: e.tensor_copy(
            Rrow[:, seg], PR[0:1, 0:128]), waits=[evt1])
        bPR.read(evc1)
        brows.wrote(evc1)
        evd2 = p.emit("v", lambda e, j=j: e.tensor_scalar(
            tmpd[:, :], ones1c[:, :], A12[:, j:j + 1], None, ALU.mult),
            waits=[evt1] + btmpd.w_waits())
        bst.read(evd2)
        btmpd.wrote(evd2)
        evt2 = p.emit("t", lambda e: e.transpose(
            PR[0:1, 128:256], tmpd[:, :], idf[:, :]),
            waits=[evd2] + bPR.w_waits())
        bPR.wrote(evt2)
        btmpd.read(evt2)
        evc2 = p.emit("v", lambda e, seg=seg: e.tensor_copy(
            Arow[:, seg], PR[0:1, 128:256]), waits=[evt2])
        bPR.read(evc2)
        brows.wrote(evc2)
    ev_rows = evc2

    SCLW = float(256 ** (-0.25))
    ev_sq_ = []
    for k in range(8):
        ev_sq_.append(p.emit("v", lambda e, k=k: e.tensor_scalar(
            wQ[:, k * DM:(k + 1) * DM], wQ[:, k * DM:(k + 1) * DM],
            SCLW, None, ALU.mult), waits=[evs_wq[k]]))
        bwQ.wrote(ev_sq_[-1])
    ev_sk_ = []
    for k in range(8):
        ev_sk_.append(p.emit("v", lambda e, k=k: e.tensor_scalar(
            wK[:, k * DM:(k + 1) * DM], wK[:, k * DM:(k + 1) * DM],
            SCLW, None, ALU.mult), waits=[evs_wk[k]]))
        bwK.wrote(ev_sk_[-1])

    def build_xh(nt):
        ev_bR = p.emit("t", lambda e, nt=nt: e.matmul(
            PA[:, :], onesb[:, :], Rrow[0:1, nt * 512:(nt + 1) * 512],
            start=True, stop=True),
            waits=[ev_rows, ev_ob] + bPA.w_waits())
        bPA.wrote(ev_bR)
        ev_bA = p.emit("t", lambda e, nt=nt: e.matmul(
            PB[:, :], onesb[:, :], Arow[0:1, nt * 512:(nt + 1) * 512],
            start=True, stop=True),
            waits=bPB.w_waits())
        bPB.wrote(ev_bA)
        ev_lx = p.emit("y", lambda e, nt=nt: e.dma_start(
            out=xft[:].rearrange("p (k n) -> p k n", n=512),
            in_=x1p_fm.rearrange("(k p) n -> p k n", p=128)[:, :, nt * 512:(nt + 1) * 512]),
            waits=bxft.w_waits(), dma=True)
        bxft.wrote(ev_lx)
        ev_h = None
        for f in range(8):
            p.emit("v", lambda e, f=f: e.tensor_tensor(
                xft[:, f * 512:(f + 1) * 512], xft[:, f * 512:(f + 1) * 512],
                PA[:, :], ALU.mult), waits=[ev_lx, ev_bR])
            p.emit("v", lambda e, f=f: e.tensor_tensor(
                xft[:, f * 512:(f + 1) * 512], xft[:, f * 512:(f + 1) * 512],
                PB[:, :], ALU.add), waits=[ev_bA])
            ev_h = p.emit("v", lambda e, f=f: e.tensor_scalar(
                xh[:, f * 512:(f + 1) * 512], xft[:, f * 512:(f + 1) * 512],
                vc[:, f:f + 1], vc[:, 8 + f:8 + f + 1], ALU.mult, ALU.add),
                waits=[ev_vc] + (bxh.w_waits() if f == 0 else []))
            bxh.wrote(ev_h)
        bPA.read(ev_h)
        bPB.read(ev_h)
        bxft.read(ev_h)
        return ev_h

    ev_mk0 = p.emit("v", lambda e: e.memset(mk[:, :], -3.0e38))
    bmk.wrote(ev_mk0)
    for nt in range(NT // 512):
        ev_h = build_xh(nt)
        for kc in range(8):
            pb, bb = (PC, bPC) if kc % 2 == 0 else (PD, bPD)
            ev_mm = None
            for f in range(8):
                ev_mm = p.emit("t", lambda e, kc=kc, f=f, pb=pb: e.matmul(
                    pb[:, :], wK[:, f * DM + kc * 128:f * DM + (kc + 1) * 128],
                    xh[:, f * 512:(f + 1) * 512],
                    start=(f == 0), stop=(f == 7)),
                    waits=(bb.w_waits() + [ev_h] + ev_sk_) if f == 0 else [],
                    sig=(f == 7))
            bb.wrote(ev_mm)
            bxh.read(ev_mm)
            ev_r = p.emit("v", lambda e, kc=kc, pb=pb: e.tensor_reduce(
                tmp8[:, kc:kc + 1], pb[:, :], axis=AX.X, op=ALU.max),
                waits=[ev_mm])
            bb.read(ev_r)
            ev_mx = p.emit("v", lambda e, kc=kc: e.tensor_tensor(
                mk[:, kc:kc + 1], mk[:, kc:kc + 1], tmp8[:, kc:kc + 1], ALU.max))
            bmk.wrote(ev_mx)
    if pair:
        ev_dmk = p.emit("y", lambda e: e.dma_start(
            out=cc_mk_in[:, :], in_=mk[:, :]),
            waits=bmk.write_list, dma=True)
        bmk.read(ev_dmk)
        ev_ccm = p.emit("g", lambda e: e.collective_compute(
            "AllGather", ALU.bypass,
            replica_groups=[[0, 1], [2, 3], [4, 5], [6, 7]],
            ins=[cc_mk_in.opt()],
            outs=[cc_mk_out.opt()]),
            waits=[ev_dmk], coll=True)
        ev_l0 = p.emit("y", lambda e: e.dma_start(
            out=tmp8[:, :], in_=cc_mk_out[0:128, :]),
            waits=[ev_ccm], dma=True)
        ev_l1 = p.emit("y", lambda e: e.dma_start(
            out=nmk[:, :], in_=cc_mk_out[128:256, :]),
            waits=[ev_ccm], dma=True)
        ev_mxp = p.emit("v", lambda e: e.tensor_tensor(
            mk[:, :], tmp8[:, :], nmk[:, :], ALU.max),
            waits=[ev_l0, ev_l1] + bmk.w_waits())
        bmk.wrote(ev_mxp)
    ev_nm = p.emit("v", lambda e: e.tensor_scalar(
        nmk[:, :], mk[:, :], -1.0, None, ALU.mult), force=True)
    bmk.read(ev_nm)
    bmk.wrote(ev_nm)
    ev_mrow = None
    for kc in range(8):
        evd = p.emit("v", lambda e, kc=kc: e.tensor_scalar(
            tmpd[:, :], ones1c[:, :], mk[:, kc:kc + 1], None, ALU.mult),
            waits=[ev_nm, ev_o1] + btmpd.w_waits())
        bmk.read(evd)
        btmpd.wrote(evd)
        evt = p.emit("t", lambda e: e.transpose(
            PR[0:1, 0:128], tmpd[:, :], idf[:, :]),
            waits=[evd, ev_idf] + bPR.w_waits())
        bPR.wrote(evt)
        btmpd.read(evt)
        ev_mrow = p.emit("v", lambda e, kc=kc: e.tensor_copy(
            mrow[:, kc * 128:(kc + 1) * 128], PR[0:1, 0:128]), waits=[evt])
        bPR.read(ev_mrow)
        bmrow.wrote(ev_mrow)
    for half in range(2):
        ev_bm = p.emit("t", lambda e, half=half: e.matmul(
            PE_[:, :], onesf[:, :], mrow[0:1, half * 512:(half + 1) * 512],
            start=True, stop=True), waits=[ev_mrow, ev_of] + bPE.w_waits(), sig=True)
        bPE.wrote(ev_bm)
        ev_mb = p.emit("v", lambda e, half=half: e.tensor_copy(
            mbro[:, half * 512:(half + 1) * 512], PE_[:, :]), waits=[ev_bm])
        bPE.read(ev_mb)
        bmb.wrote(ev_mb)
    ev_lk0 = p.emit("v", lambda e: e.memset(lkrow[:, :], 0.0))
    blk.wrote(ev_lk0)
    ev_C0 = p.emit("v", lambda e: e.memset(Cacc[:, :], 0.0))
    bCa.wrote(ev_C0)

    for nt in range(NT // 512):
        ev_h = build_xh(nt)
        for sub in range(4):
            tk = slice(sub * 128 * 8, 0)
            for half in range(2):
                pb, bb = (PC, bPC) if half == 0 else (PD, bPD)
                ev_mm = None
                for f in range(8):
                    ev_mm = p.emit("t", lambda e, sub=sub, half=half, f=f, pb=pb: e.matmul(
                        pb[:, :],
                        xh[:, f * 512 + sub * 128:f * 512 + (sub + 1) * 128],
                        wK[:, f * DM + half * 512:f * DM + (half + 1) * 512],
                        start=(f == 0), stop=(f == 7)),
                        waits=(bb.w_waits() + [ev_h] + ev_sk_) if f == 0 else [],
                        sig=(f == 7))
                bb.wrote(ev_mm)
                bxh.read(ev_mm)
                ev_s1_ = p.emit("v", lambda e, sub=sub, half=half, pb=pb: e.tensor_tensor(
                    ke_tm[:, sub * DM + half * 512:sub * DM + (half + 1) * 512],
                    pb[:, :], mbro[:, half * 512:(half + 1) * 512], ALU.subtract),
                    waits=[ev_mm, bmb.writes.get("v")] + (bke.w_waits() if sub == 0 and half == 0 else []))
                bb.read(ev_s1_)
                bke.wrote(ev_s1_)
                ev_e1 = p.emit("s", lambda e, sub=sub, half=half: e.activation(
                    ke_tm[:, sub * DM + half * 512:sub * DM + (half + 1) * 512],
                    ke_tm[:, sub * DM + half * 512:sub * DM + (half + 1) * 512],
                    AF.Exp), waits=[ev_s1_])
                bke.wrote(ev_e1)
                pb2, bb2 = (PC, bPC) if half == 1 else (PD, bPD)
                ev_mm2 = None
                for f in range(8):
                    ev_mm2 = p.emit("t", lambda e, sub=sub, half=half, f=f, pb2=pb2: e.matmul(
                        pb2[:, :],
                        xh[:, f * 512 + sub * 128:f * 512 + (sub + 1) * 128],
                        wV[:, f * DM + half * 512:f * DM + (half + 1) * 512],
                        start=(f == 0), stop=(f == 7)),
                        waits=(bb2.w_waits() + [ev_h] + evs_wv) if f == 0 else [],
                        sig=(f == 7))
                bb2.wrote(ev_mm2)
                bxh.read(ev_mm2)
                ev_v1 = p.emit("s", lambda e, sub=sub, half=half, pb2=pb2: e.activation(
                    v_tm[:, sub * DM + half * 512:sub * DM + (half + 1) * 512],
                    pb2[:, :], AF.Copy),
                    waits=[ev_mm2] + (bv_.w_waits() if sub == 0 and half == 0 else []))
                bb2.read(ev_v1)
                bv_.wrote(ev_v1)
        ev_keall = bke.write_list[-1]
        ev_vall = bv_.write_list[-1]
        for half in range(2):
            ev_mm = None
            for sub in range(4):
                ev_mm = p.emit("t", lambda e, sub=sub, half=half: e.matmul(
                    PR[0:1, 0:512], onescol[:, :],
                    ke_tm[:, sub * DM + half * 512:sub * DM + (half + 1) * 512],
                    start=(sub == 0), stop=(sub == 3)),
                    waits=(bPR.w_waits() + [ev_keall, ev_oc]) if sub == 0 else [],
                    sig=(sub == 3))
            bPR.wrote(ev_mm)
            bke.read(ev_mm)
            ev_lka = p.emit("v", lambda e, half=half: e.tensor_tensor(
                lkrow[:, half * 512:(half + 1) * 512],
                lkrow[:, half * 512:(half + 1) * 512], PR[0:1, 0:512], ALU.add),
                waits=[ev_mm])
            bPR.read(ev_lka)
            blk.wrote(ev_lka)
        for hd in range(4):
            for dc in range(2):
                pb, bb = (PC, bPC) if (hd * 2 + dc) % 2 == 0 else (PD, bPD)
                ev_mm = None
                for sub in range(4):
                    ev_mm = p.emit("t", lambda e, hd=hd, dc=dc, sub=sub, pb=pb: e.matmul(
                        pb[:, 0:256],
                        ke_tm[:, sub * DM + hd * 256 + dc * 128:sub * DM + hd * 256 + (dc + 1) * 128],
                        v_tm[:, sub * DM + hd * 256:sub * DM + (hd + 1) * 256],
                        start=(sub == 0), stop=(sub == 3)),
                        waits=(bb.w_waits() + [ev_keall, ev_vall]) if sub == 0 else [],
                        sig=(sub == 3))
                bb.wrote(ev_mm)
                bke.read(ev_mm)
                bv_.read(ev_mm)
                ev_ca = p.emit("v", lambda e, hd=hd, dc=dc, pb=pb: e.tensor_tensor(
                    Cacc[:, (hd * 2 + dc) * 256:(hd * 2 + dc + 1) * 256],
                    Cacc[:, (hd * 2 + dc) * 256:(hd * 2 + dc + 1) * 256],
                    pb[:, 0:256], ALU.add), waits=[ev_mm])
                bb.read(ev_ca)
                bCa.wrote(ev_ca)
    ev_call = bCa.write_list[-1]
    ev_lkall = blk.write_list[-1]

    if pair:
        ev_dC = p.emit("y", lambda e: e.dma_start(
            out=cc_C_in[0:128, :], in_=Cacc[:, :]),
            waits=bCa.write_list, dma=True)
        bCa.read(ev_dC)
        ev_dl = p.emit("y", lambda e: e.dma_start(
            out=cc_C_in[128:129, 0:DM], in_=lkrow[:, :]),
            waits=blk.write_list, dma=True)
        blk.read(ev_dl)
        ev_dl2 = p.emit("y", lambda e: e.dma_start(
            out=cc_C_in[128:129, DM:2 * DM], in_=lkrow[:, :]),
            waits=blk.write_list, dma=True)
        blk.read(ev_dl2)
        ev_ccC = p.emit("g", lambda e: e.collective_compute(
            "AllReduce", ALU.add,
            replica_groups=[[0, 1], [2, 3], [4, 5], [6, 7]],
            ins=[cc_C_in.opt()],
            outs=[cc_C_out.opt()]),
            waits=[ev_dC, ev_dl, ev_dl2], coll=True)
        ev_lC = p.emit("y", lambda e: e.dma_start(
            out=Cacc[:, :], in_=cc_C_out[0:128, :]),
            waits=[ev_ccC] + bCa.w_waits(), dma=True)
        bCa.wrote(ev_lC)
        ev_ll = p.emit("y", lambda e: e.dma_start(
            out=lkrow[:, :], in_=cc_C_out[128:129, 0:DM]),
            waits=[ev_ccC] + blk.w_waits(), dma=True)
        blk.wrote(ev_ll)
        ev_call = ev_lC
        ev_lkall = ev_ll

    ev_rl = p.emit("v", lambda e: e.reciprocal(lkrow[:, :], lkrow[:, :]),
                   waits=[ev_lkall], force=True)
    blk.wrote(ev_rl)
    ev_rlc = None
    for kc in range(8):
        evt = p.emit("t", lambda e, kc=kc: e.transpose(
            PE_[:, 0:1], lkrow[0:1, kc * 128:(kc + 1) * 128], idf[0:1, 0:1]),
            waits=[ev_rl, ev_idf] + bPE.w_waits())
        bPE.wrote(evt)
        blk.read(evt)
        ev_rlc = p.emit("v", lambda e, kc=kc: e.tensor_copy(
            rlc[:, kc:kc + 1], PE_[:, 0:1]), waits=[evt])
        bPE.read(ev_rlc)
        brlc.wrote(ev_rlc)
    ev_cn = None
    for hd in range(4):
        for dc in range(2):
            blkc = hd * 2 + dc
            ev_cn = p.emit("v", lambda e, blkc=blkc, hd=hd, dc=dc: e.tensor_scalar(
                C_sb[:, blkc * 256:(blkc + 1) * 256],
                Cacc[:, blkc * 256:(blkc + 1) * 256],
                rlc[:, hd * 2 + dc:hd * 2 + dc + 1], None, ALU.mult),
                waits=[ev_rlc, ev_call] + (bC.w_waits() if blkc == 0 else []))
            bCa.read(ev_cn)
            brlc.read(ev_cn)
            bC.wrote(ev_cn)
    ev_cready = ev_cn

    for nt in range(4):
        ev_h = build_xh(nt)
        for sub in range(4):
            tile = nt * 4 + sub
            for half in range(2):
                pb, bb = (PC, bPC) if half == 0 else (PD, bPD)
                ev_mm = None
                for f in range(8):
                    ev_mm = p.emit("t", lambda e, sub=sub, half=half, f=f, pb=pb: e.matmul(
                        pb[:, :],
                        xh[:, f * 512 + sub * 128:f * 512 + (sub + 1) * 128],
                        wQ[:, f * DM + half * 512:f * DM + (half + 1) * 512],
                        start=(f == 0), stop=(f == 7)),
                        waits=(bb.w_waits() + [ev_h] + ev_sq_) if f == 0 else [],
                        sig=(f == 7))
                bb.wrote(ev_mm)
                bxh.read(ev_mm)
                ev_qc = p.emit("s", lambda e, half=half, pb=pb: e.activation(
                    q_tm[:, half * 512:(half + 1) * 512], pb[:, :], AF.Copy),
                    waits=[ev_mm] + (bq.w_waits() if half == 0 else []))
                bb.read(ev_qc)
                bq.wrote(ev_qc)
            ev_qm = p.emit("v", lambda e: e.tensor_reduce(
                mq[:, :].rearrange("p (h o) -> p h o", o=1),
                q_tm[:, :].rearrange("p (h n) -> p h n", h=4),
                axis=AX.X, op=ALU.max, negate=True), waits=[bq.writes.get("s")])
            ev_ex = None
            for hd in range(4):
                ev_ex = p.emit("s", lambda e, hd=hd: e.activation(
                    q_tm[:, hd * 256:(hd + 1) * 256],
                    q_tm[:, hd * 256:(hd + 1) * 256], AF.Exp,
                    bias=mq[:, hd:hd + 1], accum_out=lq[hd][:, :]),
                    waits=[ev_qm])
                bq.wrote(ev_ex)
            ev_rq = p.emit("v", lambda e: e.reciprocal(rq[:, 0:1], lq[0][:, :]),
                           waits=[ev_ex], force=True)
            for hd in range(1, 4):
                ev_rq = p.emit("v", lambda e, hd=hd: e.reciprocal(
                    rq[:, hd:hd + 1], lq[hd][:, :]), force=True)
            ev_qs = None
            for hd in range(4):
                ev_qs = p.emit("v", lambda e, hd=hd: e.tensor_scalar(
                    q_tm[:, hd * 256:(hd + 1) * 256],
                    q_tm[:, hd * 256:(hd + 1) * 256],
                    rq[:, hd:hd + 1], None, ALU.mult), waits=[ev_rq])
                bq.wrote(ev_qs)
            for kc in range(8):
                psT, bbT = (PF, bPF) if kc % 2 == 0 else (PG, bPG)
                evt = p.emit("t", lambda e, kc=kc, psT=psT: e.transpose(
                    psT[:, 0:128], q_tm[:, kc * 128:(kc + 1) * 128], idb[:, :]),
                    waits=[ev_qs, ev_idb] + bbT.w_waits())
                bbT.wrote(evt)
                bq.read(evt)
                ev_qt = p.emit("v", lambda e, kc=kc, psT=psT: e.tensor_copy(
                    qT[:, kc * 128:(kc + 1) * 128], psT[:, 0:128]),
                    waits=[evt] + (bqT.w_waits() if kc == 0 else []))
                bbT.read(ev_qt)
                bqT.wrote(ev_qt)
            ev_qtall = bqT.write_list[-1]
            for hd in range(4):
                for ec in range(2):
                    pb, bb = (PC, bPC) if (hd * 2 + ec) % 2 == 0 else (PD, bPD)
                    ev_mm = None
                    for dc in range(2):
                        ev_mm = p.emit("t", lambda e, hd=hd, ec=ec, dc=dc, pb=pb: e.matmul(
                            pb[:, 0:128],
                            C_sb[:, (hd * 2 + dc) * 256 + ec * 128:(hd * 2 + dc) * 256 + (ec + 1) * 128],
                            qT[:, (hd * 2 + dc) * 128:(hd * 2 + dc + 1) * 128],
                            start=(dc == 0), stop=(dc == 1)),
                            waits=(bb.w_waits() + [ev_qtall, ev_cready]) if dc == 0 else [],
                            sig=(dc == 1))
                    bb.wrote(ev_mm)
                    bqT.read(ev_mm)
                    bC.read(ev_mm)
                    ev_ae = p.emit("s", lambda e, hd=hd, ec=ec, pb=pb: e.activation(
                        at_fm[:, (hd * 2 + ec) * 128:(hd * 2 + ec + 1) * 128],
                        pb[:, 0:128], AF.Copy),
                        waits=[ev_mm] + (batf.w_waits() if hd == 0 and ec == 0 else []))
                    bb.read(ev_ae)
                    batf.wrote(ev_ae)
            ev_atall = batf.write_list[-1]
            ev_lxn = p.emit("y", lambda e, tile=tile: e.dma_start(
                out=xn1t[:, :], in_=xn1_o[tile * 128:(tile + 1) * 128, :]),
                waits=bxn1t.w_waits(), dma=True)
            bxn1t.wrote(ev_lxn)
            for half in range(2):
                pb, bb = (PE_, bPE) if half == 0 else (PC, bPC)
                ev_mm = p.emit("t", lambda e, half=half, pb=pb: e.matmul(
                    pb[:, :], onesb[:, :], bo_s[:, half * 512:(half + 1) * 512],
                    start=True, stop=False),
                    waits=bb.w_waits() + [ev_atall, ev_bo, ev_ob] + evs_wo, sig=False)
                for oc in range(8):
                    ev_mm = p.emit("t", lambda e, half=half, oc=oc, pb=pb: e.matmul(
                        pb[:, :],
                        at_fm[:, oc * 128:(oc + 1) * 128],
                        wO[:, oc * DM + half * 512:oc * DM + (half + 1) * 512],
                        start=False, stop=(oc == 7)),
                        waits=[], sig=(oc == 7))
                bb.wrote(ev_mm)
                batf.read(ev_mm)
                ev_xa = p.emit("v", lambda e, half=half, pb=pb: e.tensor_tensor(
                    xft[:, half * 512:(half + 1) * 512], pb[:, :],
                    xn1t[:, half * 512:(half + 1) * 512], ALU.add),
                    waits=[ev_mm, ev_lxn] + (bxft.w_waits() if half == 0 else []))
                bb.read(ev_xa)
                bxn1t.read(ev_xa)
                bxft.wrote(ev_xa)
            ev_xout = p.emit("y", lambda e, tile=tile: e.dma_start(
                out=xatt_o[tile * 128:(tile + 1) * 128, :], in_=xft[:, 0:DM]),
                waits=[ev_xa], dma=True)
            bxft.read(ev_xout)
            if xattfm_o is not None:
                for hf in range(2):
                    pb, bb = (PA, bPA) if hf == 0 else (PB, bPB)
                    for kc in range(4):
                        f8 = hf * 4 + kc
                        evt = p.emit("t", lambda e, f8=f8, kc=kc, pb=pb: e.transpose(
                            pb[:, kc * 128:(kc + 1) * 128],
                            xft[:, f8 * 128:(f8 + 1) * 128], idf[:, :]),
                            waits=([ev_xa] + bb.w_waits()) if kc == 0 else [],
                            sig=(kc == 3))
                    bb.wrote(evt)
                    bxft.read(evt)
                    evc_ = p.emit("v", lambda e, pb=pb: e.tensor_copy(
                        xaf_st[:, :], pb[:, :]),
                        waits=[evt] + bxaf.w_waits())
                    bb.read(evc_)
                    bxaf.wrote(evc_)
                    ev_fo = p.emit("y", lambda e, tile=tile, hf=hf: e.dma_start(
                        out=xattfm_o.rearrange("(k p) n -> p k n", p=128)[
                            :, hf * 4:(hf + 1) * 4, tile * 128:(tile + 1) * 128],
                        in_=xaf_st[:].rearrange("p (k n) -> p k n", n=128)),
                        waits=[evc_], dma=True)
                    bxaf.read(ev_fo)


def build_B1a():
    nc = bass.Bass("TRN2", target_bir_lowering=False)
    NT = 2 * TPC
    io = {
        "x1p_tm": nc.dram_tensor("x1p_tm", [NT, DM], F32, kind="ExternalInput").ap(),
        "x1p_fm": nc.dram_tensor("x1p_fm", [DM, NT], F32, kind="ExternalInput").ap(),
        "wq": nc.dram_tensor("wq", [DM, DM], F32, kind="ExternalInput").ap(),
        "wk": nc.dram_tensor("wk", [DM, DM], F32, kind="ExternalInput").ap(),
        "wv": nc.dram_tensor("wv", [DM, DM], F32, kind="ExternalInput").ap(),
        "wo": nc.dram_tensor("wo", [DM, DM], F32, kind="ExternalInput").ap(),
        "vcols": nc.dram_tensor("vcols", [128, 16], F32, kind="ExternalInput").ap(),
        "bo_r": nc.dram_tensor("bo_r", [1, DM], F32, kind="ExternalInput").ap(),
        "xatt_o": nc.dram_tensor("xatt", [TPC, DM], F32, kind="ExternalOutput").ap(),
        "xn1_o": nc.dram_tensor("xn1_bf", [TPC, DM], BF16, kind="ExternalOutput").ap(),
    }
    p = Pgm(nc)
    g = mk_glob(p, nc)
    p.fence()
    emit_B1a(p, g, io, pair=False)
    finish_all(p)
    return nc


def _vcols2(v1, v2):
    out = np.zeros((128, 16), np.float32)
    for k in range(8):
        out[:, k] = v1[k * 128:(k + 1) * 128]
        out[:, 8 + k] = v2[k * 128:(k + 1) * 128]
    return out


def run_B1a(inputs, x1_by_core):
    nc = build_B1a()
    maps = []
    wqv = np.ascontiguousarray(np.asarray(inputs["L1_attn_wq"], np.float32))
    wkv = np.ascontiguousarray(np.asarray(inputs["L1_attn_wk"], np.float32))
    wvv = np.ascontiguousarray(np.asarray(inputs["L1_attn_wv"], np.float32))
    wov = np.ascontiguousarray(np.asarray(inputs["L1_attn_wo"], np.float32))
    vc = _vcols2(np.asarray(inputs["L1_attn_ln1_g"], np.float32),
                 np.asarray(inputs["L1_attn_ln1_b"], np.float32))
    bo = np.ascontiguousarray(np.asarray(inputs["L1_attn_bo"], np.float32).reshape(1, DM))
    for c in range(NCORE):
        b, h = c // 2, c % 2
        own = x1_by_core[c]
        other = x1_by_core[c + 1 - 2 * h]
        x1p = np.ascontiguousarray(np.concatenate([own, other], axis=0))
        maps.append({
            "x1p_tm": x1p, "x1p_fm": np.ascontiguousarray(x1p.T),
            "wq": wqv, "wk": wkv, "wv": wvv, "wo": wov,
            "vcols": vc, "bo_r": bo,
        })
    res = run_bass_kernel_spmd(nc, maps, core_ids=list(range(NCORE)))
    return res.results


def emit_B1b(p, g, io):
    TW = 256
    NTL = TPC // 128
    NNT = TPC // TW
    xatt_tm = io["xatt_tm"]
    xatt_fm = io["xatt_fm"]
    wff1 = io["wff1"]
    wff2 = io["wff2"]
    w_d = io["w_d"]
    vcols = io["vcols"]
    bf1c = io["bf1c"]
    bff2_r = io["bff2_r"]
    bd_r = io["bd_r"]
    s1_o = io["s1_o"]
    idb = g["idb"]
    idf = g["idf"]
    ev_idb = ev_idf = None

    wF1 = p.sb("wF1", [128, 8 * 4 * DM], BF16)
    wF2 = p.sb("wF2", [128, 32 * DM], BF16)
    wD = p.sb("wD", [128, 8 * DM], BF16)
    vc = p.sb("vc", [128, 16], F32)
    cb1 = p.sb("cb1", [128, 32], F32)
    bff2_s = p.sb("bff2_s", [1, DM], BF16)
    bd_s = p.sb("bd_s", [1, DM], BF16)
    st16a = p.sb("st16a", [128, 16], F32)
    st16b = p.sb("st16b", [128, 16], F32)
    st16c = p.sb("st16c", [128, 16], F32)
    st16d = p.sb("st16d", [128, 16], F32)
    R2 = p.sb("R2", [128, 16], F32)
    A2 = p.sb("A2", [128, 16], F32)
    Rrow = p.sb("Rrow", [1, TPC], BF16)
    Arow = p.sb("Arow", [1, TPC], BF16)
    ones1c = p.sb("ones1c", [128, 1], F32)
    onesb = p.sb("onesb", [1, 128], BF16)
    tmpd = p.sb("tmpd", [128, 1], F32)
    xtt = [p.sb(f"xtt{i}", [128, DM], F32) for i in range(2)]
    xft = p.sb("xft", [128, 8 * TW], F32)
    xc2 = p.sb("xc2", [128, 8 * TW], BF16)
    hh = p.sb("hh", [128, 32 * TW], BF16)
    x3t = p.sb("x3t", [128, DM], F32)
    x3f = p.sb("x3f", [128, 8 * 128], BF16)
    s1t = p.sb("s1t", [128, DM], F32)
    PA = p.ps("PA", [128, 512])
    PB = p.ps("PB", [128, 512])
    PC = p.ps("PC", [128, 512])
    PD = p.ps("PD", [128, 512])
    PF = p.ps("PF", [128, 128], BF16)
    PG = p.ps("PG", [128, 128], BF16)
    PR = p.ps("PR", [1, 512])

    bwF1, bwF2, bwD = Buf(), Buf(), Buf()
    bPA, bPB, bPC, bPD, bPF, bPG, bPR = (Buf() for _ in range(7))
    bxtt = [Buf(), Buf()]
    bxft, bxc2, bhh = Buf(), Buf(), Buf()
    bst, brows, btmpd = Buf(), Buf(), Buf()
    bx3, bx3f, bs1t = Buf(), Buf(), Buf()

    ev_vc = p.emit("y", lambda e: e.dma_start(out=vc[:, :], in_=vcols[:, :]), dma=True)
    ev_cb1 = p.emit("y", lambda e: e.dma_start(out=cb1[:, :], in_=bf1c[:, :]), dma=True)
    ev_b2 = p.emit("g", lambda e: e.dma_start(out=bff2_s[:, :], in_=bff2_r[:, :]), dma=True)
    ev_bd = p.emit("g", lambda e: e.dma_start(out=bd_s[:, :], in_=bd_r[:, :]), dma=True)
    ev_o1 = p.emit("v", lambda e: e.memset(ones1c[:, :], 1.0))
    ev_ob = p.emit("v", lambda e: e.memset(onesb[:, :], 1.0))
    evs_f1 = []
    for k in range(8):
        evs_f1.append(p.emit("g", lambda e, k=k: e.dma_start(
            out=wF1[:, k * 4 * DM:(k + 1) * 4 * DM],
            in_=wff1[k * 128:(k + 1) * 128, :]), dma=True))
    evs_f2 = []
    for q in range(8):
        evs_f2.append(p.emit("g", lambda e, q=q: e.dma_start(
            out=wF2[:, q * 4096:(q + 1) * 4096].rearrange("p (c n) -> p c n", n=DM),
            in_=wff2.rearrange("(c p) n -> p c n", p=128)[:, q * 4:(q + 1) * 4, :]),
            dma=True))
    evs_wd = []
    for k in range(8):
        evs_wd.append(p.emit("g", lambda e, k=k: e.dma_start(
            out=wD[:, k * DM:(k + 1) * DM],
            in_=w_d[k * 128:(k + 1) * 128, :]), dma=True))

    for j in range(NTL):
        buf, bb = xtt[j % 2], bxtt[j % 2]
        ev_ld = p.emit("y", lambda e, j=j, buf=buf: e.dma_start(
            out=buf[:, :], in_=xatt_tm[j * 128:(j + 1) * 128, :]),
            waits=bb.w_waits(), dma=True)
        bb.wrote(ev_ld)
        ev1 = p.emit("v", lambda e, buf=buf, j=j: e.tensor_reduce(
            st16a[:, j:j + 1], buf[:, :], axis=AX.X, op=ALU.add),
            waits=[ev_ld])
        bb.read(ev1)
        ev_sq = p.emit("s", lambda e, buf=buf: e.activation(
            xft[:, 0:DM], buf[:, :], AF.Square), waits=[ev_ld] + bxft.w_waits())
        bb.read(ev_sq)
        bxft.wrote(ev_sq)
        ev2 = p.emit("v", lambda e, j=j: e.tensor_reduce(
            st16b[:, j:j + 1], xft[:, 0:DM], axis=AX.X, op=ALU.add), waits=[ev_sq])
        bxft.read(ev2)
    ev1f = p.emit("v", lambda e: e.engine_nop(),
                  waits=[(ev1[0], ev1[1], True), (ev2[0], ev2[1], True)], force=True)
    pe = ev1f

    def vstep(fn, extra=()):
        nonlocal pe
        pe = p.emit("v", fn, waits=[pe] + list(extra), force=True)
        return pe

    vstep(lambda e: e.tensor_scalar(st16c[:, :], st16a[:, :], 1.0 / DM, None, ALU.mult))
    vstep(lambda e: e.tensor_tensor(st16d[:, :], st16c[:, :], st16c[:, :], ALU.mult))
    vstep(lambda e: e.scalar_tensor_tensor(st16b[:, :], st16b[:, :], 1.0 / DM,
                                           st16d[:, :], ALU.mult, ALU.subtract))
    ev_ve = vstep(lambda e: e.tensor_scalar(st16a[:, :], st16b[:, :], 1.0, 1e-5,
                                            ALU.mult, ALU.add))
    ev_sa = p.emit("s", lambda e: e.activation(R2[:, :], st16a[:, :], AF.Sqrt),
                   waits=[ev_ve])
    vstep(lambda e: e.reciprocal(R2[:, :], R2[:, :]), extra=[ev_sa])
    for _ in range(2):
        vstep(lambda e: e.tensor_tensor(st16d[:, :], R2[:, :], R2[:, :], ALU.mult))
        vstep(lambda e: e.tensor_tensor(st16d[:, :], st16d[:, :], st16a[:, :], ALU.mult))
        vstep(lambda e: e.tensor_scalar(st16d[:, :], st16d[:, :], -0.5, 1.5, ALU.mult, ALU.add))
        vstep(lambda e: e.tensor_tensor(R2[:, :], R2[:, :], st16d[:, :], ALU.mult))
    vstep(lambda e: e.tensor_scalar(st16d[:, :], st16c[:, :], -1.0, None, ALU.mult))
    ev_RA = vstep(lambda e: e.tensor_tensor(A2[:, :], st16d[:, :], R2[:, :], ALU.mult))
    bst.wrote(ev_RA)
    for j in range(NTL):
        seg = slice(j * 128, (j + 1) * 128)
        evd1 = p.emit("v", lambda e, j=j: e.tensor_scalar(
            tmpd[:, :], ones1c[:, :], R2[:, j:j + 1], None, ALU.mult),
            waits=[ev_RA, ev_o1] + btmpd.w_waits())
        bst.read(evd1)
        btmpd.wrote(evd1)
        evt1 = p.emit("t", lambda e: e.transpose(
            PR[0:1, 0:128], tmpd[:, :], idf[:, :]),
            waits=[evd1, ev_idf] + bPR.w_waits())
        bPR.wrote(evt1)
        btmpd.read(evt1)
        evc1 = p.emit("v", lambda e, seg=seg: e.tensor_copy(
            Rrow[:, seg], PR[0:1, 0:128]), waits=[evt1])
        bPR.read(evc1)
        brows.wrote(evc1)
        evd2 = p.emit("v", lambda e, j=j: e.tensor_scalar(
            tmpd[:, :], ones1c[:, :], A2[:, j:j + 1], None, ALU.mult),
            waits=[evt1] + btmpd.w_waits())
        bst.read(evd2)
        btmpd.wrote(evd2)
        evt2 = p.emit("t", lambda e: e.transpose(
            PR[0:1, 128:256], tmpd[:, :], idf[:, :]),
            waits=[evd2] + bPR.w_waits())
        bPR.wrote(evt2)
        btmpd.read(evt2)
        evc2 = p.emit("v", lambda e, seg=seg: e.tensor_copy(
            Arow[:, seg], PR[0:1, 128:256]), waits=[evt2])
        bPR.read(evc2)
        brows.wrote(evc2)
    ev_rows = evc2

    for nt in range(NNT):
        ev_bR = p.emit("t", lambda e, nt=nt: e.matmul(
            PA[:, 0:TW], onesb[:, :], Rrow[0:1, nt * TW:(nt + 1) * TW],
            start=True, stop=True), waits=[ev_rows, ev_ob] + bPA.w_waits())
        bPA.wrote(ev_bR)
        ev_bA = p.emit("t", lambda e, nt=nt: e.matmul(
            PB[:, 0:TW], onesb[:, :], Arow[0:1, nt * TW:(nt + 1) * TW],
            start=True, stop=True), waits=bPB.w_waits())
        bPB.wrote(ev_bA)
        ev_lx = p.emit("y", lambda e, nt=nt: e.dma_start(
            out=xft[:].rearrange("p (k n) -> p k n", n=TW),
            in_=xatt_fm.rearrange("(k p) n -> p k n", p=128)[:, :, nt * TW:(nt + 1) * TW]),
            waits=bxft.w_waits(), dma=True)
        bxft.wrote(ev_lx)
        ev_h = None
        for f in range(8):
            p.emit("v", lambda e, f=f: e.tensor_tensor(
                xft[:, f * TW:(f + 1) * TW], xft[:, f * TW:(f + 1) * TW],
                PA[:, 0:TW], ALU.mult), waits=[ev_lx, ev_bR])
            p.emit("v", lambda e, f=f: e.tensor_tensor(
                xft[:, f * TW:(f + 1) * TW], xft[:, f * TW:(f + 1) * TW],
                PB[:, 0:TW], ALU.add), waits=[ev_bA])
            ev_h = p.emit("v", lambda e, f=f: e.tensor_scalar(
                xc2[:, f * TW:(f + 1) * TW], xft[:, f * TW:(f + 1) * TW],
                vc[:, f:f + 1], vc[:, 8 + f:8 + f + 1], ALU.mult, ALU.add),
                waits=[ev_vc] + (bxc2.w_waits() if f == 0 else []))
            bxc2.wrote(ev_h)
        bPA.read(ev_h)
        bPB.read(ev_h)
        bxft.read(ev_h)
        for c in range(32):
            pb, bb = (PC, bPC) if c % 2 == 0 else (PD, bPD)
            ev_mm = None
            for f in range(8):
                ev_mm = p.emit("t", lambda e, c=c, f=f, pb=pb: e.matmul(
                    pb[:, 0:TW],
                    wF1[:, f * 4096 + c * 128:f * 4096 + (c + 1) * 128],
                    xc2[:, f * TW:(f + 1) * TW],
                    start=(f == 0), stop=(f == 7)),
                    waits=(bb.w_waits() + [ev_h] + evs_f1) if f == 0 else [],
                    sig=(f == 7))
            bb.wrote(ev_mm)
            bxc2.read(ev_mm)
            ev_g = p.emit("s", lambda e, c=c, pb=pb: e.activation(
                hh[:, c * TW:(c + 1) * TW], pb[:, 0:TW], AF.Gelu_apprx_tanh,
                bias=cb1[:, c:c + 1]),
                waits=[ev_mm, ev_cb1] + (bhh.w_waits() if c == 0 else []))
            bb.read(ev_g)
            bhh.wrote(ev_g)
        ev_hh = bhh.write_list[-1]
        for sub in range(TW // 128):
            tile = nt * (TW // 128) + sub
            buf, bb2 = xtt[sub % 2], bxtt[sub % 2]
            ev_lr = p.emit("y", lambda e, tile=tile, buf=buf: e.dma_start(
                out=buf[:, :], in_=xatt_tm[tile * 128:(tile + 1) * 128, :]),
                waits=bb2.w_waits(), dma=True)
            bb2.wrote(ev_lr)
            for half in range(2):
                pb, bb = (PA, bPA) if half == 0 else (PB, bPB)
                ev_mm = p.emit("t", lambda e, half=half, pb=pb: e.matmul(
                    pb[:, :], onesb[:, :], bff2_s[:, half * 512:(half + 1) * 512],
                    start=True, stop=False),
                    waits=bb.w_waits() + [ev_hh, ev_b2, ev_ob] + evs_f2, sig=False)
                for c in range(32):
                    ev_mm = p.emit("t", lambda e, c=c, sub=sub, half=half, pb=pb: e.matmul(
                        pb[:, :],
                        hh[:, c * TW + sub * 128:c * TW + sub * 128 + 128],
                        wF2[:, c * DM + half * 512:c * DM + (half + 1) * 512],
                        start=False, stop=(c == 31)),
                        waits=[], sig=(c == 31))
                bb.wrote(ev_mm)
                bhh.read(ev_mm)
                ev_x3 = p.emit("v", lambda e, half=half, pb=pb, buf=buf: e.tensor_tensor(
                    x3t[:, half * 512:(half + 1) * 512], pb[:, :],
                    buf[:, half * 512:(half + 1) * 512], ALU.add),
                    waits=[ev_mm, ev_lr] + (bx3.w_waits() if half == 0 else []))
                bb.read(ev_x3)
                bb2.read(ev_x3)
                bx3.wrote(ev_x3)
            ev_c3 = p.emit("v", lambda e: e.tensor_copy(
                x3f[:, 0:0 + DM] if False else xc2[:, 0:DM], x3t[:, :]),
                waits=[ev_x3] + bxc2.w_waits(), sig=True)
            bx3.read(ev_c3)
            bxc2.wrote(ev_c3)
            for kc in range(8):
                psT, bbT = (PF, bPF) if kc % 2 == 0 else (PG, bPG)
                evt = p.emit("t", lambda e, kc=kc, psT=psT: e.transpose(
                    psT[:, :], xc2[:, kc * 128:(kc + 1) * 128], idb[:, :]),
                    waits=[ev_c3, ev_idb] + bbT.w_waits())
                bbT.wrote(evt)
                bxc2.read(evt)
                ev_xf = p.emit("v", lambda e, kc=kc, psT=psT: e.tensor_copy(
                    x3f[:, kc * 128:(kc + 1) * 128], psT[:, :]),
                    waits=[evt] + (bx3f.w_waits() if kc == 0 else []))
                bbT.read(ev_xf)
                bx3f.wrote(ev_xf)
            ev_x3f = bx3f.write_list[-1]
            for half in range(2):
                pb, bb = (PC, bPC) if half == 0 else (PD, bPD)
                ev_mm = p.emit("t", lambda e, half=half, pb=pb: e.matmul(
                    pb[:, :], onesb[:, :], bd_s[:, half * 512:(half + 1) * 512],
                    start=True, stop=False),
                    waits=bb.w_waits() + [ev_x3f, ev_bd] + evs_wd, sig=False)
                for kc in range(8):
                    ev_mm = p.emit("t", lambda e, kc=kc, half=half, pb=pb: e.matmul(
                        pb[:, :],
                        x3f[:, kc * 128:(kc + 1) * 128],
                        wD[:, kc * DM + half * 512:kc * DM + (half + 1) * 512],
                        start=False, stop=(kc == 7)),
                        waits=[], sig=(kc == 7))
                bb.wrote(ev_mm)
                bx3f.read(ev_mm)
                ev_s1 = p.emit("v", lambda e, half=half, pb=pb: e.tensor_copy(
                    s1t[:, half * 512:(half + 1) * 512], pb[:, :]),
                    waits=[ev_mm] + (bs1t.w_waits() if half == 0 else []))
                bb.read(ev_s1)
                bs1t.wrote(ev_s1)
            ev_so = p.emit("y", lambda e, tile=tile: e.dma_start(
                out=s1_o[tile * 128:(tile + 1) * 128, :], in_=s1t[:, :]),
                waits=[ev_s1], dma=True)
            bs1t.read(ev_so)


def build_B1b():
    nc = bass.Bass("TRN2", target_bir_lowering=False)
    io = {
        "xatt_tm": nc.dram_tensor("xatt_tm", [TPC, DM], F32, kind="ExternalInput").ap(),
        "xatt_fm": nc.dram_tensor("xatt_fm", [DM, TPC], F32, kind="ExternalInput").ap(),
        "wff1": nc.dram_tensor("wff1", [DM, 4 * DM], F32, kind="ExternalInput").ap(),
        "wff2": nc.dram_tensor("wff2", [4 * DM, DM], F32, kind="ExternalInput").ap(),
        "w_d": nc.dram_tensor("w_d", [DM, DM], F32, kind="ExternalInput").ap(),
        "vcols": nc.dram_tensor("vcols", [128, 16], F32, kind="ExternalInput").ap(),
        "bf1c": nc.dram_tensor("bf1c", [128, 32], F32, kind="ExternalInput").ap(),
        "bff2_r": nc.dram_tensor("bff2_r", [1, DM], F32, kind="ExternalInput").ap(),
        "bd_r": nc.dram_tensor("bd_r", [1, DM], F32, kind="ExternalInput").ap(),
        "s1_o": nc.dram_tensor("s1", [TPC, DM], F32, kind="ExternalOutput").ap(),
    }
    p = Pgm(nc)
    g = mk_glob(p, nc)
    p.fence()
    emit_B1b(p, g, io)
    finish_all(p)
    return nc


def run_B1b(inputs, xatt_by_core):
    nc = build_B1b()
    wff1 = np.ascontiguousarray(np.asarray(inputs["L1_attn_w_ff1"], np.float32))
    wff2 = np.ascontiguousarray(np.asarray(inputs["L1_attn_w_ff2"], np.float32))
    wd = np.ascontiguousarray(np.asarray(inputs["L1_attn_w_d"], np.float32))
    vc = _vcols2(np.asarray(inputs["L1_attn_ln2_g"], np.float32),
                 np.asarray(inputs["L1_attn_ln2_b"], np.float32))
    bf1 = np.asarray(inputs["L1_attn_b_ff1"], np.float32)
    bf1c = np.zeros((128, 32), np.float32)
    for c in range(32):
        bf1c[:, c] = bf1[c * 128:(c + 1) * 128]
    bff2 = np.ascontiguousarray(np.asarray(inputs["L1_attn_b_ff2"], np.float32).reshape(1, DM))
    bd = np.ascontiguousarray(np.asarray(inputs["L1_attn_b_d"], np.float32).reshape(1, DM))
    maps = []
    for c in range(NCORE):
        xa = np.ascontiguousarray(xatt_by_core[c])
        maps.append({
            "xatt_tm": xa, "xatt_fm": np.ascontiguousarray(xa.T),
            "wff1": wff1, "wff2": wff2, "w_d": wd,
            "vcols": vc, "bf1c": bf1c, "bff2_r": bff2, "bd_r": bd,
        })
    res = run_bass_kernel_spmd(nc, maps, core_ids=list(range(NCORE)))
    return res.results


def _rot_matrix():
    R = np.zeros((128, 128), np.float32)
    for b0 in (0, 64):
        for i in range(32):
            R[b0 + i, b0 + 32 + i] = -1.0
            R[b0 + 32 + i, b0 + i] = 1.0
    return np.ascontiguousarray(R.T).astype(BF16NP)


def emit_Bf_attn(p, g, io):
    EXT = 2561
    KE = 2560
    IN = 512
    y_fm = io.get("y_fm")
    y_halo = io.get("y_halo")
    y_own = io.get("y_own")
    hz = io.get("hz")
    y_tm = io["y_tm"]
    y_tm_cast = io.get("y_tm_cast", False)
    wqkv = io["wqkv"]
    wo = io["wo"]
    qs_r = io["qs_r"]
    ks_r = io["ks_r"]
    tbls = io["tbls"]
    masks = io["masks"]
    xp_o = io["xp_o"]
    xpfm_o = io.get("xpfm_o")
    idb = g["idb"]
    idf = g["idf"]
    rotm = g["rotm"]
    ev_idb = ev_rot = None

    yb = p.sb("yb", [128, 8 * EXT], BF16)
    wQKV = p.sb("wQKV", [128, 8 * 3 * IN], BF16)
    wO = p.sb("wO", [128, 4 * DM], BF16)
    qsr = p.sb("qsr", [1, 64], F32)
    ksr = p.sb("ksr", [1, 64], F32)
    tb = p.sb("tb", [128, 4 * EXT], BF16)
    mk_sb = p.sb("mk_sb", [128, 5 * 640], BF16)
    q_sb = p.sb("q_sb", [128, 4 * TPC], BF16)
    k_sb = p.sb("k_sb", [128, 4 * KE], BF16)
    v_sb = p.sb("v_sb", [128, 20 * IN], BF16)
    o_sb = p.sb("o_sb", [128, 4 * TPC], BF16)
    sqs = [p.sb(f"sqs{i}", [128, 512], F32) for i in range(4)]
    ssr = [p.sb(f"ssr{i}", [1, 512], F32) for i in range(2)]
    eps1 = p.sb("eps1", [1, 1], F32)
    onescol = p.sb("onescol", [128, 1], F32)
    scl = [p.sb(f"scl{i}", [128, 512], F32) for i in range(2)]
    t1b = [p.sb(f"t1b{i}", [128, 512], BF16) for i in range(2)]
    sm2 = [p.sb(f"sm2_{i}", [128, 640], BF16) for i in range(3)]
    ptb = [p.sb(f"ptb{i}", [128, 640], BF16) for i in range(2)]
    m1 = [p.sb(f"m1_{i}", [128, 1], F32) for i in range(3)]
    ls1 = [p.sb(f"ls1_{i}", [128, 1], F32) for i in range(3)]
    rl2 = [p.sb(f"rl2_{i}", [128, 1], F32) for i in range(3)]
    yt = [p.sb("yt0", [128, DM], F32)] * 2
    xpt = p.sb("xpt", [128, DM], F32)
    xpf_st = p.sb("xpf_st", [128, 512], F32)
    bxpf = Buf()
    PA = p.ps("PA", [128, 512])
    PB = p.ps("PB", [128, 512])
    PC = p.ps("PC", [128, 512])
    PD = p.ps("PD", [128, 512])
    PT0 = p.ps("PT0", [128, 640], BF16)
    PT1 = p.ps("PT1", [128, 640], BF16)
    PO0 = p.ps("PO0", [64, 128])
    PO1 = p.ps("PO1", [64, 128])

    bPA, bPB, bPC, bPD = Buf(), Buf(), Buf(), Buf()
    bPT = [Buf(), Buf()]
    bPO = [Buf(), Buf()]
    byb, bq, bk, bv, bo = Buf(), Buf(), Buf(), Buf(), Buf()
    bsq = [Buf(), Buf(), Buf(), Buf()]
    bssr = [Buf(), Buf()]
    bscl = [Buf(), Buf()]
    bt1 = [Buf(), Buf()]
    bsm = [Buf(), Buf(), Buf()]
    bpt = [Buf(), Buf()]
    _byt = Buf()
    byt = [_byt] * 2
    bxpt = Buf()

    ev_qsr = p.emit("y", lambda e: e.dma_start(out=qsr[:, :], in_=qs_r[:, :]), dma=True)
    ev_ksr = p.emit("y", lambda e: e.dma_start(out=ksr[:, :], in_=ks_r[:, :]), dma=True)
    ev_tb = p.emit("g", lambda e: e.dma_start(out=tb[:, :], in_=tbls[:, :]), dma=True)
    ev_mk = p.emit("g", lambda e: e.dma_start(out=mk_sb[:, :], in_=masks[:, :]), dma=True)
    ev_e1 = p.emit("v", lambda e: e.memset(eps1[:, :], 1e-24))
    ev_oc = p.emit("v", lambda e: e.memset(onescol[:, :], 1.0))
    evs_y = []
    if y_fm is not None:
        for k in range(8):
            evs_y.append(p.emit("g", lambda e, k=k: e.dma_start(
                out=yb[:, k * EXT:(k + 1) * EXT],
                in_=y_fm[k * 128:(k + 1) * 128, :]), dma=True))
    else:
        hzb = p.sb("hzb", [128, 1], F32)
        ev_hz = p.emit("y", lambda e: e.dma_start(out=hzb[:, :], in_=hz[:, :]),
                       dma=True)
        for k in range(8):
            ev_h_ = p.emit("g", lambda e, k=k: e.dma_start(
                out=yb[:, k * EXT:k * EXT + 513],
                in_=y_halo[k * 128:(k + 1) * 128, :]), dma=True)
            ev_hm = p.emit("v", lambda e, k=k: e.tensor_scalar(
                yb[:, k * EXT:k * EXT + 513], yb[:, k * EXT:k * EXT + 513],
                hzb[:, 0:1], None, ALU.mult), waits=[ev_h_, ev_hz])
            evs_y.append(ev_hm)
            evs_y.append(p.emit("g", lambda e, k=k: e.dma_start(
                out=yb[:, k * EXT + 513:(k + 1) * EXT],
                in_=y_own[k * 128:(k + 1) * 128, :]), dma=True))
    evs_w = []
    for k in range(8):
        evs_w.append(p.emit("g", lambda e, k=k: e.dma_start(
            out=wQKV[:, k * 3 * IN:(k + 1) * 3 * IN],
            in_=wqkv[k * 128:(k + 1) * 128, :]), dma=True))
    evs_wo = []
    for k in range(4):
        evs_wo.append(p.emit("g", lambda e, k=k: e.dma_start(
            out=wO[:, k * DM:(k + 1) * DM],
            in_=wo[k * 128:(k + 1) * 128, :]), dma=True))

    def ysl(f, base, width):
        off = base - (1 if f >= 4 else 0)
        return yb[:, f * EXT + off:f * EXT + off + width]

    _base = p.prefix
    p.prefix = _base + "prj."
    for nt in range(5):
        base = 1 + nt * 512
        for kc in range(4):
            pb, bb = (PA, bPA) if kc % 2 == 0 else (PB, bPB)
            ev_mm = None
            for f in range(8):
                ev_mm = p.emit("t", lambda e, nt=nt, kc=kc, f=f, pb=pb, base=base: e.matmul(
                    pb[:, :], wQKV[:, f * 3 * IN + IN + kc * 128:f * 3 * IN + IN + (kc + 1) * 128],
                    ysl(f, base, 512), start=(f == 0), stop=(f == 7)),
                    waits=(bb.w_waits() + evs_y + evs_w) if f == 0 else [],
                    sig=(f == 7))
            bb.wrote(ev_mm)
            byb.read(ev_mm)
            ev_ev = p.emit("s", lambda e, nt=nt, kc=kc, pb=pb: e.activation(
                k_sb[:, kc * KE + nt * 512:kc * KE + (nt + 1) * 512], pb[:, :],
                AF.Copy), waits=[ev_mm] + (bk.w_waits() if nt == 0 and kc == 0 else []))
            bb.read(ev_ev)
            bk.wrote(ev_ev)
        if nt >= 1:
            for qc in range(4):
                pb, bb = (PC, bPC) if qc % 2 == 0 else (PD, bPD)
                ev_mm = None
                for f in range(8):
                    ev_mm = p.emit("t", lambda e, nt=nt, qc=qc, f=f, pb=pb, base=base: e.matmul(
                        pb[:, :], wQKV[:, f * 3 * IN + qc * 128:f * 3 * IN + (qc + 1) * 128],
                        ysl(f, base, 512), start=(f == 0), stop=(f == 7)),
                        waits=(bb.w_waits() + evs_y + evs_w) if f == 0 else [],
                        sig=(f == 7))
                bb.wrote(ev_mm)
                byb.read(ev_mm)
                ev_ev = p.emit("s", lambda e, nt=nt, qc=qc, pb=pb: e.activation(
                    q_sb[:, qc * TPC + (nt - 1) * 512:qc * TPC + nt * 512], pb[:, :],
                    AF.Copy), waits=[ev_mm] + (bq.w_waits() if nt == 1 and qc == 0 else []))
                bb.read(ev_ev)
                bq.wrote(ev_ev)
        for sub in range(4):
            s_ = nt * 4 + sub
            pb, bb = (PC, bPC) if sub % 2 == 0 else (PD, bPD)
            ev_mm = None
            for f in range(8):
                ev_mm = p.emit("t", lambda e, s_=s_, f=f, pb=pb, base=base, sub=sub: e.matmul(
                    pb[:, :],
                    ysl(f, base + sub * 128, 128),
                    wQKV[:, f * 3 * IN + 2 * IN:f * 3 * IN + 3 * IN],
                    start=(f == 0), stop=(f == 7)),
                    waits=(bb.w_waits() + evs_y + evs_w) if f == 0 else [],
                    sig=(f == 7))
            bb.wrote(ev_mm)
            byb.read(ev_mm)
            ev_ev = p.emit("s", lambda e, s_=s_, pb=pb: e.activation(
                v_sb[:, s_ * IN:(s_ + 1) * IN], pb[:, :], AF.Copy),
                waits=[ev_mm] + (bv.w_waits() if s_ == 0 else []))
            bb.read(ev_ev)
            bv.wrote(ev_ev)
    ev_kall = bk.write_list[-1]
    ev_qall = bq.write_list[-1]
    ev_vall = bv.write_list[-1]

    p.prefix = _base + "rot."
    def l2rot(tag, sb_, bsb, nseg, seglen, qs_ap, ev_qs_, cos_off, sin_off, tab_base):
        ev_seed = [bsb.writes.get("s"), bsb.writes.get("v")]
        n_it = nseg * 4
        ST = {}

        def sl_of(it):
            nt, ch = it // 4, it % 4
            return (nt, sb_[:, ch * seglen + nt * 512:ch * seglen + nt * 512 + 512])

        def rS1(it):
            nt, sl_ = sl_of(it)
            j4 = it % 4
            st = ST[it] = {}
            ev_s2 = p.emit("s", lambda e, sl_=sl_, j4=j4: e.activation(
                sqs[j4][:, :], sl_, AF.Square),
                waits=bsq[j4].w_waits() + ev_seed)
            bsq[j4].wrote(ev_s2)
            st["sq"] = ev_s2

        def rS2(it):
            j4, j2 = it % 4, it % 2
            st = ST[it]
            pb, bb = (PA, bPA) if j2 == 0 else (PB, bPB)
            ev_cp = None
            for hh_ in range(2):
                ev_mm = p.emit("t", lambda e, hh_=hh_, pb=pb, j4=j4: e.matmul(
                    pb[:, :][0:1, :],
                    onescol[hh_ * 64:(hh_ + 1) * 64, :],
                    sqs[j4][hh_ * 64:(hh_ + 1) * 64, :],
                    start=True, stop=True),
                    waits=[st["sq"], ev_oc] + bb.w_waits())
                bb.wrote(ev_mm)
                bsq[j4].read(ev_mm)
                ev_sq2 = p.emit("s", lambda e, pb=pb, j2=j2: e.activation(
                    ssr[j2][:, :], pb[:, :][0:1, :], AF.Sqrt, bias=eps1[:, :]),
                    waits=[ev_mm, ev_e1] + bssr[j2].w_waits())
                bb.read(ev_sq2)
                bssr[j2].wrote(ev_sq2)
                ev_rc = p.emit("v", lambda e, j2=j2: e.reciprocal(
                    ssr[j2][:, :], ssr[j2][:, :]), waits=[ev_sq2], force=True)
                bssr[j2].wrote(ev_rc)
                ev_sc = p.emit("t", lambda e, qs_ap=qs_ap, pb=pb, j2=j2: e.matmul(
                    pb[0:64, :], qs_ap, ssr[j2][:, :], start=True, stop=True),
                    waits=[ev_rc, ev_qs_] + bb.w_waits(), sig=True)
                bb.wrote(ev_sc)
                bssr[j2].read(ev_sc)
                ev_cp = p.emit("v", lambda e, hh_=hh_, pb=pb, j2=j2: e.tensor_copy(
                    scl[j2][hh_ * 64:(hh_ + 1) * 64, :], pb[:, :][0:64, :]),
                    waits=[ev_sc] + (bscl[j2].w_waits() if hh_ == 0 else []))
                bb.read(ev_cp)
                bscl[j2].wrote(ev_cp)
            st["scl"] = ev_cp

        def rS3(it):
            nt, sl_ = sl_of(it)
            j2 = it % 2
            st = ST[it]
            ev_qp = p.emit("v", lambda e, sl_=sl_, j2=j2: e.tensor_tensor(
                sl_, sl_, scl[j2][:, :], ALU.mult),
                waits=[st["scl"], st["sq"]])
            bscl[j2].read(ev_qp)
            pbr, bbr = (PC, bPC) if j2 == 0 else (PD, bPD)
            ev_rt = p.emit("t", lambda e, sl_=sl_, pbr=pbr: e.matmul(
                pbr[:, :], rotm[:, :], sl_, start=True, stop=True),
                waits=[ev_qp, ev_rot] + bbr.w_waits())
            bbr.wrote(ev_rt)
            ev_t1 = p.emit("v", lambda e, sl_=sl_, nt=nt, cos_off=cos_off, tab_base=tab_base, j2=j2: e.tensor_tensor(
                t1b[j2][:, :], sl_, tb[:, cos_off + tab_base + nt * 512:cos_off + tab_base + nt * 512 + 512],
                ALU.mult), waits=[ev_tb] + bt1[j2].w_waits())
            bt1[j2].wrote(ev_t1)
            st["rt"] = ev_rt
            st["t1"] = ev_t1

        def rS4(it):
            nt, sl_ = sl_of(it)
            j4, j2 = it % 4, it % 2
            st = ST.pop(it)
            pbr, bbr = (PC, bPC) if j2 == 0 else (PD, bPD)
            ev_t2 = p.emit("v", lambda e, nt=nt, sin_off=sin_off, tab_base=tab_base, pbr=pbr, j4=j4: e.tensor_tensor(
                sqs[j4][:, :], pbr[:, :], tb[:, sin_off + tab_base + nt * 512:sin_off + tab_base + nt * 512 + 512],
                ALU.mult), waits=[st["rt"]] + bsq[j4].w_waits())
            bbr.read(ev_t2)
            bsq[j4].wrote(ev_t2)
            ev_qq = p.emit("v", lambda e, sl_=sl_, j4=j4, j2=j2: e.tensor_tensor(
                sl_, t1b[j2][:, :], sqs[j4][:, :], ALU.add),
                waits=[st["t1"]])
            bt1[j2].read(ev_qq)
            bsq[j4].read(ev_qq)
            last_qq[0] = ev_qq

        for step in range(n_it + 3):
            if step < n_it:
                rS1(step)
            if 1 <= step <= n_it:
                rS2(step - 1)
            if 2 <= step <= n_it + 1:
                rS3(step - 2)
            if step >= 3:
                rS4(step - 3)

    last_qq = [None]
    l2rot("q", q_sb, bq, 4, TPC, qsr[:, :], ev_qsr, 0, EXT, 513)
    ev_qrot = last_qq[0]
    bq.wrote(ev_qrot)
    l2rot("k", k_sb, bk, 5, KE, ksr[:, :], ev_ksr, 2 * EXT, 3 * EXT, 1)
    ev_krot = last_qq[0]
    bk.wrote(ev_krot)

    p.prefix = _base + "att."
    W_ = 640
    NIT = 128
    STM = {}

    def att_dec(it):
        return it // 32, (it // 4) % 8, it % 4

    def att_S1(it):
        w, hd, qi = att_dec(it)
        i3 = it % 3
        base_r = (hd % 2) * 64
        qcc = hd // 2
        qcol = qcc * TPC + w * 512 + qi * 128
        kcol = qcc * KE + w * 512 + qi * 128
        pbS1, bbS1 = (PA, bPA) if it % 2 == 0 else (PC, bPC)
        pbS2, bbS2 = (PB, bPB) if it % 2 == 0 else (PD, bPD)
        ev_s1_ = p.emit("t", lambda e, base_r=base_r, qcol=qcol, kcol=kcol, pbS1=pbS1: e.matmul(
            pbS1[:, :],
            q_sb[base_r:base_r + 64, qcol:qcol + 128],
            k_sb[base_r:base_r + 64, kcol:kcol + 512],
            start=True, stop=True),
            waits=bbS1.w_waits() + [ev_krot, ev_qrot])
        bbS1.wrote(ev_s1_)
        bq.read(ev_s1_)
        bk.read(ev_s1_)
        ev_s2_ = p.emit("t", lambda e, base_r=base_r, qcol=qcol, kcol=kcol, pbS2=pbS2: e.matmul(
            pbS2[:, 0:128],
            q_sb[base_r:base_r + 64, qcol:qcol + 128],
            k_sb[base_r:base_r + 64, kcol + 512:kcol + 640],
            start=True, stop=True),
            waits=bbS2.w_waits())
        bbS2.wrote(ev_s2_)
        bq.read(ev_s2_)
        bk.read(ev_s2_)
        mslot = (1 + qi) if w == 0 else 0
        ev_m1 = p.emit("v", lambda e, i3=i3, mslot=mslot, pbS1=pbS1: e.scalar_tensor_tensor(
            sm2[i3][:, 0:512], pbS1[:, :], 8.0,
            mk_sb[:, mslot * W_:mslot * W_ + 512], ALU.mult, ALU.add),
            waits=[ev_s1_, ev_s2_, ev_mk] + bsm[i3].w_waits())
        bbS1.read(ev_m1)
        ev_m2 = p.emit("v", lambda e, i3=i3, mslot=mslot, pbS2=pbS2: e.scalar_tensor_tensor(
            sm2[i3][:, 512:640], pbS2[:, 0:128], 8.0,
            mk_sb[:, mslot * W_ + 512:(mslot + 1) * W_], ALU.mult, ALU.add))
        bbS2.read(ev_m2)
        bsm[i3].wrote(ev_m2)
        ev_mx = p.emit("v", lambda e, i3=i3: e.tensor_reduce(
            m1[i3][:, :], sm2[i3][:, 0:W_], axis=AX.X, op=ALU.max,
            negate=True))
        STM[it] = {"mx": ev_mx}

    def att_S2(it):
        i3 = it % 3
        st = STM[it]
        ev_e1_ = p.emit("s", lambda e, i3=i3: e.activation(
            sm2[i3][:, 0:W_], sm2[i3][:, 0:W_], AF.Exp,
            bias=m1[i3][:, :], accum_out=ls1[i3][:, :]), waits=[st["mx"]])
        bsm[i3].wrote(ev_e1_)
        ev_rl_ = p.emit("v", lambda e, i3=i3: e.reciprocal(
            rl2[i3][:, :], ls1[i3][:, :]), waits=[ev_e1_], force=True)
        ev_pn = p.emit("v", lambda e, i3=i3: e.tensor_scalar(
            sm2[i3][:, 0:W_], sm2[i3][:, 0:W_], rl2[i3][:, :], None,
            ALU.mult), waits=[ev_rl_])
        bsm[i3].wrote(ev_pn)
        psT, bbT = (PT0, bPT[0]) if it % 2 == 0 else (PT1, bPT[1])
        evt = None
        for kc in range(5):
            evt = p.emit("t", lambda e, i3=i3, kc=kc, psT=psT: e.transpose(
                psT[:, kc * 128:(kc + 1) * 128],
                sm2[i3][:, kc * 128:(kc + 1) * 128], idb[:, :]),
                waits=([ev_pn] + bbT.w_waits()) if kc == 0 else [],
                sig=(kc == 4))
        bbT.wrote(evt)
        bsm[i3].read(evt)
        evc_h = p.emit("v", lambda e, it=it, psT=psT: e.tensor_copy(
            ptb[it % 2][:, 0:640], psT[:, 0:640]),
            waits=[evt] + bpt[it % 2].w_waits())
        bbT.read(evc_h)
        bpt[it % 2].wrote(evc_h)
        st["pt"] = evc_h

    def att_S3(it):
        w, hd, qi = att_dec(it)
        st = STM.pop(it)
        base_r = (hd % 2) * 64
        qcc = hd // 2
        psO, bbO = (PO0, bPO[0]) if it % 2 == 0 else (PO1, bPO[1])
        ev_pv = None
        for kc in range(5):
            s_ = 4 * w + qi + kc
            ev_pv = p.emit("t", lambda e, s_=s_, hd=hd, kc=kc, it=it, psO=psO: e.matmul(
                psO[:, :],
                v_sb[:, s_ * IN + hd * 64:s_ * IN + (hd + 1) * 64],
                ptb[it % 2][:, kc * 128:(kc + 1) * 128],
                start=(kc == 0), stop=(kc == 4)),
                waits=([st["pt"], ev_vall] + bbO.w_waits()) if kc == 0 else [],
                sig=(kc == 4))
        bpt[it % 2].read(ev_pv)
        bv.read(ev_pv)
        bbO.wrote(ev_pv)
        ev_oe = p.emit("s", lambda e, base_r=base_r, qcc=qcc, w=w, qi=qi, psO=psO: e.activation(
            o_sb[base_r:base_r + 64, qcc * TPC + w * 512 + qi * 128:qcc * TPC + w * 512 + (qi + 1) * 128],
            psO[:, :], AF.Copy),
            waits=[ev_pv] + (bo.w_waits() if it == 0 else []))
        bbO.read(ev_oe)
        bo.wrote(ev_oe)

    for step in range(NIT + 2):
        if step < NIT:
            att_S1(step)
        if 1 <= step <= NIT:
            att_S2(step - 1)
        if step >= 2:
            att_S3(step - 2)
    ev_oall = bo.write_list[-1]

    p.prefix = _base + "out."
    for tt in range(16):
        buf, bb2 = yt[tt % 2], byt[tt % 2]
        ev_ly = p.emit("g" if y_tm_cast else "y",
                       lambda e, tt=tt, buf=buf: e.dma_start(
                           out=buf[:, :], in_=y_tm[tt * 128:(tt + 1) * 128, :]),
                       waits=bb2.w_waits(), dma=True)
        bb2.wrote(ev_ly)
        for half in range(2):
            pb, bb = (PC, bPC) if half == 0 else (PD, bPD)
            ev_mm = None
            for oc in range(4):
                ev_mm = p.emit("t", lambda e, tt=tt, half=half, oc=oc, pb=pb: e.matmul(
                    pb[:, :],
                    o_sb[:, oc * TPC + tt * 128:oc * TPC + (tt + 1) * 128],
                    wO[:, oc * DM + half * 512:oc * DM + (half + 1) * 512],
                    start=(oc == 0), stop=(oc == 3)),
                    waits=(bb.w_waits() + [ev_oall] + evs_wo) if oc == 0 else [],
                    sig=(oc == 3))
            bb.wrote(ev_mm)
            bo.read(ev_mm)
            ev_xp = p.emit("v", lambda e, half=half, pb=pb, buf=buf: e.tensor_tensor(
                xpt[:, half * 512:(half + 1) * 512], pb[:, :],
                buf[:, half * 512:(half + 1) * 512], ALU.add),
                waits=[ev_mm, ev_ly] + (bxpt.w_waits() if half == 0 else []))
            bb.read(ev_xp)
            bb2.read(ev_xp)
            bxpt.wrote(ev_xp)
        ev_xo = p.emit("y", lambda e, tt=tt: e.dma_start(
            out=xp_o[tt * 128:(tt + 1) * 128, :], in_=xpt[:, :]),
            waits=[ev_xp], dma=True)
        bxpt.read(ev_xo)
        if xpfm_o is not None:
            for hf in range(2):
                pb, bb = (PA, bPA) if hf == 0 else (PB, bPB)
                for kc in range(4):
                    f8 = hf * 4 + kc
                    evt = p.emit("t", lambda e, f8=f8, kc=kc, pb=pb: e.transpose(
                        pb[:, kc * 128:(kc + 1) * 128],
                        xpt[:, f8 * 128:(f8 + 1) * 128], idf[:, :]),
                        waits=([ev_xp] + bb.w_waits()) if kc == 0 else [],
                        sig=(kc == 3))
                bb.wrote(evt)
                bxpt.read(evt)
                evc_ = p.emit("v", lambda e, pb=pb: e.tensor_copy(
                    xpf_st[:, :], pb[:, :]),
                    waits=[evt] + bxpf.w_waits())
                bb.read(evc_)
                bxpf.wrote(evc_)
                ev_fo = p.emit("y", lambda e, tt=tt, hf=hf: e.dma_start(
                    out=xpfm_o.rearrange("(k p) n -> p k n", p=128)[
                        :, hf * 4:(hf + 1) * 4, tt * 128:(tt + 1) * 128],
                    in_=xpf_st[:].rearrange("p (k n) -> p k n", n=128)),
                    waits=[evc_], dma=True)
                bxpf.read(ev_fo)
    p.prefix = _base


def build_Bf_attn():
    nc = bass.Bass("TRN2", target_bir_lowering=False)
    EXT, IN = 2561, 512
    io = {
        "y_fm": nc.dram_tensor("y_fm", [DM, EXT], F32, kind="ExternalInput").ap(),
        "y_tm": nc.dram_tensor("y_tm", [TPC, DM], F32, kind="ExternalInput").ap(),
        "wqkv": nc.dram_tensor("wqkv", [DM, 3 * IN], F32, kind="ExternalInput").ap(),
        "wo": nc.dram_tensor("wo", [IN, DM], F32, kind="ExternalInput").ap(),
        "qs_r": nc.dram_tensor("qs_r", [1, 64], F32, kind="ExternalInput").ap(),
        "ks_r": nc.dram_tensor("ks_r", [1, 64], F32, kind="ExternalInput").ap(),
        "tbls": nc.dram_tensor("tbls", [128, 4 * EXT], F32, kind="ExternalInput").ap(),
        "masks": nc.dram_tensor("masks", [128, 5 * 640], F32, kind="ExternalInput").ap(),
        "xp_o": nc.dram_tensor("xp", [TPC, DM], F32, kind="ExternalOutput").ap(),
    }
    p = Pgm(nc)
    g = mk_glob(p, nc)
    g["rotm"] = p.sb("glob_rotm", [128, 128], BF16)
    rot_t = nc.inline_tensor(_rot_matrix(), "rot_t")
    p.emit("g", lambda e: e.dma_start(out=g["rotm"][:, :], in_=rot_t.ap()[:, :]), dma=True)
    p.fence()
    emit_Bf_attn(p, g, io)
    finish_all(p)
    return nc


def _bf_tables(T0):
    EXT = 2561
    t = np.arange(T0 - 513, T0 + 2048, dtype=np.float64)
    half = np.arange(0, 64, 2, dtype=np.float64)
    inv = 1.0 / 10000.0 ** (half / 64.0)
    fr = t[:, None] * inv[None, :]
    fr = np.concatenate([fr, fr], axis=1)
    base = (half + 0.4 * 64) / (1.4 * 64)
    power = (t - 2048.0) / 256.0
    scale = base[None, :] ** power[:, None]
    scale = np.concatenate([scale, scale], axis=1)
    cos, sin = np.cos(fr), np.sin(fr)
    blocks = [cos * scale, sin * scale, cos / scale, sin / scale]
    out = np.zeros((128, 4 * EXT), np.float32)
    for i, b in enumerate(blocks):
        bT = b.T.astype(np.float32)
        out[0:64, i * EXT:(i + 1) * EXT] = bT
        out[64:128, i * EXT:(i + 1) * EXT] = bT
    return out


def _bf_masks(first):
    out = np.zeros((128, 5 * 640), np.float32)
    i = np.arange(128)[:, None]
    j = np.arange(640)[None, :]
    band = (j >= i) & (j <= i + 512)
    out[:, 0:640] = np.where(band, 0.0, NEG)
    for qi in range(4):
        af = band & ((j + qi * 128) >= 512) if first else band
        out[:, (1 + qi) * 640:(2 + qi) * 640] = np.where(af, 0.0, NEG)
    return out


def run_Bf_attn(inputs, l, y_by_core):
    nc = build_Bf_attn()
    wqkv = np.ascontiguousarray(np.asarray(inputs["L1_ffn_wqkv"], np.float32)[l])
    wo = np.ascontiguousarray(np.asarray(inputs["L1_ffn_wo"], np.float32)[l])
    qs = np.ascontiguousarray(np.asarray(inputs["L1_ffn_q_scale"], np.float32)[l].reshape(1, 64))
    ks = np.ascontiguousarray(np.asarray(inputs["L1_ffn_k_scale"], np.float32)[l].reshape(1, 64))
    maps = []
    for c in range(NCORE):
        b, h = c // 2, c % 2
        T0 = h * TPC
        own = np.asarray(y_by_core[c], np.float32)
        if h == 0:
            halo = np.zeros((513, DM), np.float32)
        else:
            halo = np.asarray(y_by_core[c - 1], np.float32)[-513:]
        yext = np.concatenate([halo, own], axis=0)
        maps.append({
            "y_fm": np.ascontiguousarray(yext.T), "y_tm": np.ascontiguousarray(own),
            "wqkv": wqkv, "wo": wo, "qs_r": qs, "ks_r": ks,
            "tbls": _bf_tables(T0), "masks": _bf_masks(h == 0),
        })
    res = run_bass_kernel_spmd(nc, maps, core_ids=list(range(NCORE)))
    return res.results


def emit_Bf_ffn(p, g, io, final=False):
    HID = 2730
    NCH = 22
    TW = 128
    NNT = TPC // TW
    xp_fm = io.get("xp_fm")
    xp_own = io.get("xp_own")
    xp_c0 = io.get("xp_c0")
    xp_tm = io["xp_tm"]
    w1 = io["w1"]
    w2 = io["w2"]
    vcols = io["vcols"]
    b1r = io["b1r"]
    b2_r = io["b2_r"]
    y_o = io["y_o"]
    yfm_o = io.get("yfm_o")
    if final:
        x1_tm = io["x1_tm"]
        s1_tm = io["s1_tm"]
        x2_o = io["x2_o"]
    idf = g["idf"]
    idb = g["idb"]

    w1a = p.sb("w1a", [128, 8 * HID], BF16)
    w1g = p.sb("w1g", [128, 8 * HID], BF16)
    w2s = p.sb("w2s", [128, NCH * DM], BF16)
    vc = p.sb("vc", [128, 16], F32)
    b2s = p.sb("b2s", [1, DM], BF16)
    onesb = p.sb("onesb", [1, 128], BF16)
    onescf = p.sb("onescf", [128, 1], F32)
    Rrb = p.sb("Rrb", [1, TPC], BF16)
    Arb = p.sb("Arb", [1, TPC], BF16)
    p.push(p.prefix + "st_")
    sq_s = p.sb("sq_s", [128, TW], F32)
    murow = p.sb("murow", [1, TPC], F32)
    ssrow = p.sb("ssrow", [1, TPC], F32)
    rowt = p.sb("rowt", [1, TPC], F32)
    rowt2 = p.sb("rowt2", [1, TPC], F32)
    p.prefix = p.prefix[:-3]
    xft = p.sb("xft", [128, 8 * TW], F32)
    hn = p.sb("hn", [128, 8 * TW], BF16)
    aa = p.sb("aa", [128, NCH * TW], BF16)
    gg = p.sb("gg", [128, NCH * TW], BF16)
    xpt = [p.sb("xpt0", [128, DM], F32)] * 2
    yt = p.sb("yt", [128, DM], F32)
    if yfm_o is not None:
        yf_st = p.sb("yf_st", [128, 512], F32)
    byf = Buf()
    if final:
        x1t = p.sb("x1t", [128, DM], F32)
        s1t = p.sb("s1t", [128, DM], F32)
    PA = p.ps("PA", [128, 512])
    PB = p.ps("PB", [128, 512])
    PC = p.ps("PC", [128, 512])
    PD = p.ps("PD", [128, 512])
    PR = p.ps("PR", [1, 512])
    PTb0 = p.ps("PTb0", [128, 512], BF16)
    PTb1 = p.ps("PTb1", [128, 512], BF16)

    bPA, bPB, bPC, bPD, bPR = (Buf() for _ in range(5))
    bPTb0, bPTb1 = Buf(), Buf()
    bsq, bmu, bss, bR, bA = Buf(), Buf(), Buf(), Buf(), Buf()
    bxft, bhn, baa, bgg = Buf(), Buf(), Buf(), Buf()
    _bx = Buf()
    bxpt = [_bx] * 2
    byt = Buf()
    bx1t, bs1t = Buf(), Buf()

    ev_idf = None
    ev_vc = p.emit("y", lambda e: e.dma_start(out=vc[:, :], in_=vcols[:, :]), dma=True)
    ev_b2 = p.emit("g", lambda e: e.dma_start(out=b2s[:, :], in_=b2_r[:, :]), dma=True)
    ev_ob = p.emit("v", lambda e: e.memset(onesb[:, :], 1.0))
    ev_ocf = p.emit("v", lambda e: e.memset(onescf[:, :], 1.0))
    evs_w1a = []
    evs_w1g = []
    for k in range(8):
        evs_w1a.append(p.emit("g", lambda e, k=k: e.dma_start(
            out=w1a[:, k * HID:(k + 1) * HID],
            in_=w1[k * 128:(k + 1) * 128, 0:HID]), dma=True))
        evs_w1g.append(p.emit("g", lambda e, k=k: e.dma_start(
            out=w1g[:, k * HID:(k + 1) * HID],
            in_=w1[k * 128:(k + 1) * 128, HID:2 * HID]), dma=True))
    evs_w2 = []
    for c in range(NCH):
        rows = 128 if c < NCH - 1 else HID - 128 * (NCH - 1)
        evs_w2.append(p.emit("g", lambda e, c=c, rows=rows: e.dma_start(
            out=w2s[0:rows, c * DM:(c + 1) * DM],
            in_=w2[c * 128:c * 128 + rows, :]), dma=True))

    def load_xft(nt):
        if xp_fm is not None:
            ev_lx = p.emit("y", lambda e, nt=nt: e.dma_start(
                out=xft[:, 0:4 * TW].rearrange("p (k n) -> p k n", n=TW),
                in_=xp_fm.rearrange("(k p) n -> p k n", p=128)[
                    :, 0:4, 1 + nt * TW:1 + (nt + 1) * TW]),
                waits=bxft.w_waits(), dma=True)
            bxft.wrote(ev_lx)
            evs_sh = []
            for f in range(4, 8):
                evs_sh.append(p.emit("y", lambda e, nt=nt, f=f: e.dma_start(
                    out=xft[:, f * TW:(f + 1) * TW],
                    in_=xp_fm[f * 128:(f + 1) * 128, nt * TW:(nt + 1) * TW]),
                    dma=True))
        else:
            ev_lx = p.emit("y", lambda e, nt=nt: e.dma_start(
                out=xft[:, 0:4 * TW].rearrange("p (k n) -> p k n", n=TW),
                in_=xp_own.rearrange("(k p) n -> p k n", p=128)[
                    :, 0:4, nt * TW:(nt + 1) * TW]),
                waits=bxft.w_waits(), dma=True)
            bxft.wrote(ev_lx)
            evs_sh = []
            for f in range(4, 8):
                if nt == 0:
                    evs_sh.append(p.emit("y", lambda e, f=f: e.dma_start(
                        out=xft[:, f * TW:f * TW + 1],
                        in_=xp_c0[f * 128:(f + 1) * 128, :]), dma=True))
                    evs_sh.append(p.emit("y", lambda e, f=f: e.dma_start(
                        out=xft[:, f * TW + 1:(f + 1) * TW],
                        in_=xp_own[f * 128:(f + 1) * 128, 0:TW - 1]), dma=True))
                else:
                    evs_sh.append(p.emit("y", lambda e, nt=nt, f=f: e.dma_start(
                        out=xft[:, f * TW:(f + 1) * TW],
                        in_=xp_own[f * 128:(f + 1) * 128,
                                   nt * TW - 1:(nt + 1) * TW - 1]),
                        dma=True))
        for e2 in evs_sh:
            bxft.wrote(e2)
        return ev_lx, evs_sh

    for nt in range(NNT):
        ev_lx, evs_sh = load_xft(nt)
        ev_mm = None
        for f in range(8):
            ev_mm = p.emit("t", lambda e, f=f, nt=nt: e.matmul(
                PR[0:1, 0:TW], onescf[:, :], xft[:, f * TW:(f + 1) * TW],
                start=(f == 0), stop=(f == 7)),
                waits=([ev_lx] + evs_sh + [ev_ocf] + bPR.w_waits()) if f == 0 else [],
                sig=(f == 7))
        bPR.wrote(ev_mm)
        bxft.read(ev_mm)
        ev_cpm = p.emit("v", lambda e, nt=nt: e.tensor_copy(
            murow[:, nt * TW:(nt + 1) * TW], PR[0:1, 0:TW]), waits=[ev_mm])
        bPR.read(ev_cpm)
        bmu.wrote(ev_cpm)
        ev_mm2 = None
        for f in range(8):
            ev_q = p.emit("s", lambda e, f=f: e.activation(
                sq_s[:, :], xft[:, f * TW:(f + 1) * TW], AF.Square),
                waits=bsq.w_waits() + ([ev_lx] + evs_sh if f == 0 else []))
            bsq.wrote(ev_q)
            bxft.read(ev_q)
            ev_mm2 = p.emit("t", lambda e, f=f: e.matmul(
                PR[0:1, 256:256 + TW], onescf[:, :], sq_s[:, :],
                start=(f == 0), stop=(f == 7)),
                waits=[ev_q] + (bPR.w_waits() if f == 0 else []), sig=True)
            bPR.wrote(ev_mm2)
            bsq.read(ev_mm2)
        ev_cps = p.emit("v", lambda e, nt=nt: e.tensor_copy(
            ssrow[:, nt * TW:(nt + 1) * TW], PR[0:1, 256:256 + TW]), waits=[ev_mm2])
        bPR.read(ev_cps)
        bss.wrote(ev_cps)
    pe = p.emit("v", lambda e: e.engine_nop(),
                waits=[(x[0], x[1], True) for x in (bmu.write_list[-1:] + bss.write_list[-1:])],
                force=True)

    def vrow(fn, extra=()):
        nonlocal pe
        pe = p.emit("v", fn, waits=[pe] + list(extra), force=True)
        return pe

    vrow(lambda e: e.tensor_scalar(murow[:, :], murow[:, :], 1.0 / DM, None, ALU.mult))
    vrow(lambda e: e.tensor_tensor(rowt[:, :], murow[:, :], murow[:, :], ALU.mult))
    vrow(lambda e: e.scalar_tensor_tensor(ssrow[:, :], ssrow[:, :], 1.0 / DM,
                                          rowt[:, :], ALU.mult, ALU.subtract))
    ev_ve = vrow(lambda e: e.tensor_scalar(ssrow[:, :], ssrow[:, :], 1.0, 1e-5,
                                           ALU.mult, ALU.add))
    ev_sa = p.emit("s", lambda e: e.activation(rowt[:, :], ssrow[:, :], AF.Sqrt),
                   waits=[ev_ve])
    vrow(lambda e: e.reciprocal(rowt[:, :], rowt[:, :]), extra=[ev_sa])
    for _ in range(2):
        vrow(lambda e: e.tensor_tensor(rowt2[:, :], rowt[:, :], rowt[:, :], ALU.mult))
        vrow(lambda e: e.tensor_tensor(rowt2[:, :], rowt2[:, :], ssrow[:, :], ALU.mult))
        vrow(lambda e: e.tensor_scalar(rowt2[:, :], rowt2[:, :], -0.5, 1.5, ALU.mult, ALU.add))
        vrow(lambda e: e.tensor_tensor(rowt[:, :], rowt[:, :], rowt2[:, :], ALU.mult))
    vrow(lambda e: e.tensor_scalar(murow[:, :], murow[:, :], -1.0, None, ALU.mult))
    vrow(lambda e: e.tensor_tensor(murow[:, :], murow[:, :], rowt[:, :], ALU.mult))
    vrow(lambda e: e.tensor_copy(Rrb[:, :], rowt[:, :]))
    ev_rows = vrow(lambda e: e.tensor_copy(Arb[:, :], murow[:, :]))
    bR.wrote(ev_rows)

    _pfx = p.prefix
    p.pop()
    p.prefix = _pfx
    p.fence()
    b1rs = p.sb("b1rs", [1, 2 * HID], BF16)
    ev_b1r = p.emit("g", lambda e: e.dma_start(out=b1rs[:, :], in_=b1r[:, :]),
                    dma=True)

    for nt in range(NNT):
        ev_bR = p.emit("t", lambda e, nt=nt: e.matmul(
            PA[:, 0:TW], onesb[:, :], Rrb[0:1, nt * TW:(nt + 1) * TW],
            start=True, stop=True), waits=[ev_rows, ev_ob] + bPA.w_waits())
        bPA.wrote(ev_bR)
        ev_bA = p.emit("t", lambda e, nt=nt: e.matmul(
            PB[:, 0:TW], onesb[:, :], Arb[0:1, nt * TW:(nt + 1) * TW],
            start=True, stop=True), waits=bPB.w_waits())
        bPB.wrote(ev_bA)
        ev_lx, evs_sh = load_xft(nt)
        ev_h = None
        for f in range(8):
            p.emit("v", lambda e, f=f: e.tensor_tensor(
                xft[:, f * TW:(f + 1) * TW], xft[:, f * TW:(f + 1) * TW],
                PA[:, 0:TW], ALU.mult), waits=[ev_lx, ev_bR] + evs_sh)
            p.emit("v", lambda e, f=f: e.tensor_tensor(
                xft[:, f * TW:(f + 1) * TW], xft[:, f * TW:(f + 1) * TW],
                PB[:, 0:TW], ALU.add), waits=[ev_bA])
            ev_h = p.emit("v", lambda e, f=f: e.tensor_scalar(
                hn[:, f * TW:(f + 1) * TW], xft[:, f * TW:(f + 1) * TW],
                vc[:, f:f + 1], vc[:, 8 + f:8 + f + 1], ALU.mult, ALU.add),
                waits=[ev_vc] + (bhn.w_waits() if f == 0 else []))
            bhn.wrote(ev_h)
        bPA.read(ev_h)
        bPB.read(ev_h)
        bxft.read(ev_h)
        NDC = (HID + 511) // 512
        ev_ff1 = [None, None]
        for half in range(2):
            wt, evw = (w1a, evs_w1a) if half == 0 else (w1g, evs_w1g)
            tgt, btgt = (aa, baa) if half == 0 else (gg, bgg)
            for dc in range(NDC):
                ncols = 512 if dc < NDC - 1 else HID - 512 * (NDC - 1)
                pb, bb = (PC, bPC) if dc % 2 == 0 else (PD, bPD)
                ev_mm = p.emit("t", lambda e, half=half, dc=dc, ncols=ncols, pb=pb: e.matmul(
                    pb[:, 0:ncols], onesb[:, :],
                    b1rs[0:1, half * HID + dc * 512:half * HID + dc * 512 + ncols],
                    start=True, stop=False),
                    waits=bb.w_waits() + [ev_b1r, ev_ob, ev_h] + evw, sig=False)
                for f in range(8):
                    ev_mm = p.emit("t", lambda e, half=half, dc=dc, ncols=ncols, f=f, pb=pb, wt=wt: e.matmul(
                        pb[:, 0:ncols],
                        hn[:, f * TW:(f + 1) * TW],
                        wt[:, f * HID + dc * 512:f * HID + dc * 512 + ncols],
                        start=False, stop=(f == 7)),
                        waits=[], sig=(f == 7))
                bb.wrote(ev_mm)
                bhn.read(ev_mm)
                if half == 0:
                    ev_ev = p.emit("v", lambda e, dc=dc, ncols=ncols, pb=pb, tgt=tgt: e.tensor_copy(
                        tgt[:, dc * 512:dc * 512 + ncols], pb[:, 0:ncols]),
                        waits=[ev_mm] + btgt.w_waits())
                else:
                    ev_ev = p.emit("s", lambda e, dc=dc, ncols=ncols, pb=pb, tgt=tgt: e.activation(
                        tgt[:, dc * 512:dc * 512 + ncols], pb[:, 0:ncols], AF.Gelu),
                        waits=[ev_mm] + btgt.w_waits())
                bb.read(ev_ev)
                btgt.wrote(ev_ev)
                ev_ff1[half] = ev_ev
        ev_aall, ev_gall = ev_ff1
        ev_e = None
        for dc in range(NDC):
            ncols = 512 if dc < NDC - 1 else HID - 512 * (NDC - 1)
            ev_e = p.emit("v", lambda e, dc=dc, ncols=ncols: e.tensor_tensor(
                aa[:, dc * 512:dc * 512 + ncols], aa[:, dc * 512:dc * 512 + ncols],
                gg[:, dc * 512:dc * 512 + ncols], ALU.mult),
                waits=([ev_aall, ev_gall] + baa.w_waits()) if dc == 0 else [])
            baa.wrote(ev_e)
            bgg.read(ev_e)
        ev_tp = None
        for rd in range(6):
            cs = list(range(rd * 4, min(rd * 4 + 4, NCH)))
            psT, bbT = (PTb0, bPTb0) if rd % 2 == 0 else (PTb1, bPTb1)
            evt = None
            for i_, c in enumerate(cs):
                rows = 128 if c < NCH - 1 else HID - 128 * (NCH - 1)
                evt = p.emit("t", lambda e, c=c, i_=i_, rows=rows, psT=psT: e.transpose(
                    psT[0:rows, i_ * 128:i_ * 128 + 128],
                    aa[:, c * 128:c * 128 + rows], idb[:, :]),
                    waits=([ev_e] + bbT.w_waits()) if i_ == 0 else [],
                    sig=(i_ == len(cs) - 1))
            bbT.wrote(evt)
            baa.read(evt)
            ev_tp = p.emit("v", lambda e, rd=rd, ncs=len(cs), psT=psT: e.tensor_copy(
                gg[:, rd * 512:rd * 512 + ncs * 128], psT[:, 0:ncs * 128]),
                waits=[evt] + bgg.w_waits())
            bbT.read(ev_tp)
            bgg.wrote(ev_tp)
        for sub in range(TW // 128):
            tile = nt * (TW // 128) + sub
            xb = xpt[sub % 2]
            ev_lr = p.emit("y", lambda e, tile=tile, xb=xb: e.dma_start(
                out=xb[:, :], in_=xp_tm[tile * 128:(tile + 1) * 128, :]),
                waits=bxpt[0].w_waits(), dma=True)
            bxpt[0].wrote(ev_lr)
            for half in range(2):
                pb, bb = (PA, bPA) if half == 0 else (PB, bPB)
                ev_mm = p.emit("t", lambda e, half=half, pb=pb: e.matmul(
                    pb[:, :], onesb[:, :], b2s[:, half * 512:(half + 1) * 512],
                    start=True, stop=False),
                    waits=bb.w_waits() + [ev_tp, ev_b2, ev_ob] + evs_w2, sig=False)
                for c in range(NCH):
                    rows = 128 if c < NCH - 1 else HID - 128 * (NCH - 1)
                    gcol = (c // 4) * 512 + (c % 4) * 128
                    ev_mm = p.emit("t", lambda e, c=c, gcol=gcol, half=half, pb=pb, rows=rows: e.matmul(
                        pb[:, :],
                        gg[0:rows, gcol:gcol + 128],
                        w2s[0:rows, c * DM + half * 512:c * DM + (half + 1) * 512],
                        start=False, stop=(c == NCH - 1)),
                        waits=[], sig=(c == NCH - 1))
                bb.wrote(ev_mm)
                bgg.read(ev_mm)
                ev_y = p.emit("v", lambda e, half=half, pb=pb, xb=xb: e.tensor_tensor(
                    yt[:, half * 512:(half + 1) * 512], pb[:, :],
                    xb[:, half * 512:(half + 1) * 512], ALU.add),
                    waits=[ev_mm, ev_lr] + (byt.w_waits() if half == 0 else []))
                bb.read(ev_y)
                bxpt[0].read(ev_y)
                byt.wrote(ev_y)
            ev_yo = p.emit("y", lambda e, tile=tile: e.dma_start(
                out=y_o[tile * 128:(tile + 1) * 128, :], in_=yt[:, :]),
                waits=[ev_y], dma=True)
            byt.read(ev_yo)
            if yfm_o is not None:
                for hf in range(2):
                    pb, bb = (PC, bPC) if hf == 0 else (PD, bPD)
                    for kc in range(4):
                        f8 = hf * 4 + kc
                        evt = p.emit("t", lambda e, f8=f8, kc=kc, pb=pb: e.transpose(
                            pb[:, kc * 128:(kc + 1) * 128],
                            yt[:, f8 * 128:(f8 + 1) * 128], idf[:, :]),
                            waits=([ev_y] + bb.w_waits()) if kc == 0 else [],
                            sig=(kc == 3))
                    bb.wrote(evt)
                    byt.read(evt)
                    evc_ = p.emit("v", lambda e, pb=pb: e.tensor_copy(
                        yf_st[:, :], pb[:, :]),
                        waits=[evt] + byf.w_waits())
                    bb.read(evc_)
                    byf.wrote(evc_)
                    ev_fo = p.emit("y", lambda e, tile=tile, hf=hf: e.dma_start(
                        out=yfm_o.rearrange("(k p) n -> p k n", p=128)[
                            :, hf * 4:(hf + 1) * 4, tile * 128:(tile + 1) * 128],
                        in_=yf_st[:].rearrange("p (k n) -> p k n", n=128)),
                        waits=[evc_], dma=True)
                    byf.read(ev_fo)
            if final:
                ev_l1 = p.emit("y", lambda e, tile=tile: e.dma_start(
                    out=x1t[:, :], in_=x1_tm[tile * 128:(tile + 1) * 128, :]),
                    waits=bx1t.w_waits(), dma=True)
                bx1t.wrote(ev_l1)
                ev_l2 = p.emit("y", lambda e, tile=tile: e.dma_start(
                    out=s1t[:, :], in_=s1_tm[tile * 128:(tile + 1) * 128, :]),
                    waits=bs1t.w_waits(), dma=True)
                bs1t.wrote(ev_l2)
                ev_g1 = p.emit("s", lambda e: e.activation(
                    yt[:, :], yt[:, :], AF.Silu), waits=[ev_y, ev_yo])
                byt.wrote(ev_g1)
                ev_m1_ = p.emit("v", lambda e: e.tensor_tensor(
                    yt[:, :], yt[:, :], s1t[:, :], ALU.mult), waits=[ev_g1, ev_l2])
                bs1t.read(ev_m1_)
                byt.wrote(ev_m1_)
                ev_a1_ = p.emit("v", lambda e: e.tensor_tensor(
                    yt[:, :], yt[:, :], x1t[:, :], ALU.add), waits=[ev_l1])
                bx1t.read(ev_a1_)
                byt.wrote(ev_a1_)
                ev_x2o = p.emit("y", lambda e, tile=tile: e.dma_start(
                    out=x2_o[tile * 128:(tile + 1) * 128, :], in_=yt[:, :]),
                    waits=[ev_a1_], dma=True)
                byt.read(ev_x2o)


def build_Bf_ffn(final=False):
    nc = bass.Bass("TRN2", target_bir_lowering=False)
    HID, NCH = 2730, 22
    io = {
        "xp_fm": nc.dram_tensor("xp_fm", [DM, TPC + 1], F32, kind="ExternalInput").ap(),
        "xp_tm": nc.dram_tensor("xp_tm", [TPC, DM], F32, kind="ExternalInput").ap(),
        "w1": nc.dram_tensor("w1", [DM, 2 * HID], F32, kind="ExternalInput").ap(),
        "w2": nc.dram_tensor("w2", [HID, DM], F32, kind="ExternalInput").ap(),
        "vcols": nc.dram_tensor("vcols", [128, 16], F32, kind="ExternalInput").ap(),
        "b1r": nc.dram_tensor("b1r", [1, 2 * HID], BF16, kind="ExternalInput").ap(),
        "b2_r": nc.dram_tensor("b2_r", [1, DM], F32, kind="ExternalInput").ap(),
        "y_o": nc.dram_tensor("y_out", [TPC, DM], F32, kind="ExternalOutput").ap(),
    }
    if final:
        io["x1_tm"] = nc.dram_tensor("x1_tm", [TPC, DM], F32, kind="ExternalInput").ap()
        io["s1_tm"] = nc.dram_tensor("s1_tm", [TPC, DM], F32, kind="ExternalInput").ap()
        io["x2_o"] = nc.dram_tensor("x2", [TPC, DM], F32, kind="ExternalOutput").ap()
    p = Pgm(nc)
    g = mk_glob(p, nc)
    p.fence()
    emit_Bf_ffn(p, g, io, final=final)
    finish_all(p)
    return nc


def run_Bf_ffn(inputs, l, xp_by_core, final=False, x1_by_core=None, s1_by_core=None):
    nc = build_Bf_ffn(final=final)
    HID = 2730
    NCH = 22
    w1v = np.ascontiguousarray(np.asarray(inputs["L1_ffn_w1"], np.float32)[l])
    w2v = np.ascontiguousarray(np.asarray(inputs["L1_ffn_w2"], np.float32)[l])
    vc = _vcols2(np.asarray(inputs["L1_ffn_ln_g"], np.float32)[l],
                 np.asarray(inputs["L1_ffn_ln_b"], np.float32)[l])
    b1rv = np.ascontiguousarray(
        np.asarray(inputs["L1_ffn_b1"], np.float32)[l].reshape(1, 2 * HID)
        .astype(ml_dtypes.bfloat16))
    b2v = np.ascontiguousarray(np.asarray(inputs["L1_ffn_b2"], np.float32)[l].reshape(1, DM))
    maps = []
    for c in range(NCORE):
        b, h = c // 2, c % 2
        own = np.asarray(xp_by_core[c], np.float32)
        halo = (np.zeros((1, DM), np.float32) if h == 0
                else np.asarray(xp_by_core[c - 1], np.float32)[-1:])
        xpe = np.concatenate([halo, own], axis=0)
        m = {
            "xp_fm": np.ascontiguousarray(xpe.T), "xp_tm": np.ascontiguousarray(own),
            "w1": w1v, "w2": w2v, "vcols": vc,
            "b1r": b1rv, "b2_r": b2v,
        }
        if final:
            m["x1_tm"] = np.ascontiguousarray(x1_by_core[c])
            m["s1_tm"] = np.ascontiguousarray(s1_by_core[c])
        maps.append(m)
    res = run_bass_kernel_spmd(nc, maps, core_ids=list(range(NCORE)))
    return res.results


# version 11
# speedup vs baseline: 1.0171x; 1.0171x over previous
import numpy as np
import ml_dtypes
from contextlib import ExitStack
import concourse.bass as bass
import concourse.mybir as mybir
from concourse.bass_utils import run_bass_kernel_spmd

F32 = mybir.dt.float32
BF16 = mybir.dt.bfloat16
AF = mybir.ActivationFunctionType
ALU = mybir.AluOpType
AX = mybir.AxisListType
BF16NP = ml_dtypes.bfloat16

NB, S, DM = 4, 4096, 1024
NCORE = 8
TPC = 2048
NEG = -1.0e30


class Pgm:

    NDMA = 6

    def __init__(self, nc, coll=False):
        self.nc = nc
        self.es = ExitStack()
        self.entries = {k: [] for k in "gvsty"}
        self.sems = {}
        self.counts = {k: 0 for k in "gvsty"}
        for k in "gvsty":
            self.sems[k] = self.es.enter_context(nc.semaphore(f"sem_{k}"))
        if coll:
            self.sems["c"] = self.es.enter_context(nc.semaphore("sem_c"))
            self.counts["c"] = 0
        self.scopes = [self.es]
        self.prefix = ""
        self.fence_events = None
        self.fenced = set()
        self.dma_pools = {}
        self.dma_idx = {}
        for q in "gy":
            pool = []
            for i in range(self.NDMA):
                key = f"{q}d{i}"
                self.sems[key] = self.es.enter_context(nc.semaphore(f"sem_{key}"))
                self.counts[key] = 0
                pool.append(key)
            self.dma_pools[q] = pool
            self.dma_idx[q] = 0

    def sb(self, name, shape, dt):
        return self.scopes[-1].enter_context(
            self.nc.sbuf_tensor(self.prefix + name, shape, dt))

    def ps(self, name, shape, dt=F32):
        return self.scopes[-1].enter_context(
            self.nc.psum_tensor(self.prefix + name, shape, dt))

    def push(self, prefix):
        self.scopes.append(ExitStack())
        self.prefix = prefix

    def pop(self):
        self.scopes.pop().close()
        self.prefix = ""

    def fence(self):
        evs = []
        for k, v in self.counts.items():
            if v > 0:
                evs.append((k, v, True))
        self.fence_events = evs
        self.fenced = set()

    def emit(self, eng, fn, waits=(), dma=False, sig=True, force=False,
             coll=False):
        ws = [w for w in waits if w is not None]
        if self.fence_events is not None and eng not in self.fenced:
            ws = list(self.fence_events) + ws
            self.fenced.add(eng)
        tag = self.prefix or "glob"
        if coll:
            prev = self.counts["c"]
            if prev > 0:
                ws.append(("c", prev, True))
            self.counts["c"] += 1
            ev = ("c", self.counts["c"], True)
            self.entries[eng].append((fn, tuple(ws), ev, 1, tag))
            return ev
        if dma and sig:
            pool = self.dma_pools[eng]
            skey = pool[self.dma_idx[eng] % self.NDMA]
            self.dma_idx[eng] += 1
            prev = self.counts[skey]
            if prev > 0:
                ws.append((skey, prev, True))
            self.counts[skey] += 16
            ev = (skey, self.counts[skey], True)
            self.entries[eng].append((fn, tuple(ws), ev, 16, tag))
            return ev
        if eng in "vsg" and fn is not None:
            sig = True
        inc = 1 if sig else 0
        ev = None
        if sig:
            self.counts[eng] += inc
            ev = (eng, self.counts[eng], force)
        self.entries[eng].append((fn, tuple(ws), ev, inc, tag))
        return ev

    def finish(self, eng, waits):
        ws = []
        for w in waits:
            if w is None:
                continue
            ws.append(w if len(w) == 3 else (w[0], w[1], True))
        self.entries[eng].append((None, tuple(ws), None, 0, "glob"))

    def build(self):
        block = self.es.enter_context(self.nc.Block())
        sems = self.sems

        def make(key):
            entries = self.entries[key]

            self_chain = key in "vsg"

            def section(eng):
                waited = {}
                cur_tag = None
                for fn, waits, ev, inc, tag in entries:
                    if self_chain and fn is not None and ev is not None \
                            and ev[0] == key and ev[1] > 1:
                        if ev[1] - 1 > waited.get(key, 0):
                            eng.wait_ge(sems[key], ev[1] - 1)
                            waited[key] = ev[1] - 1
                    for w in waits:
                        wk, wv, wdma = w
                        if wk == key and not wdma and not self_chain:
                            continue
                        if wk == key and not wdma and self_chain:
                            continue
                        if wv > waited.get(wk, 0):
                            eng.wait_ge(sems[wk], wv)
                            waited[wk] = wv
                    if fn is not None:
                        if tag != cur_tag:
                            if cur_tag is not None:
                                self.nc.pop_named_scope(cur_tag)
                            self.nc.push_named_scope(tag)
                            cur_tag = tag
                        inst = fn(eng)
                        if ev is not None:
                            inst.then_inc(sems[ev[0]], inc)
                if cur_tag is not None:
                    self.nc.pop_named_scope(cur_tag)
                return None

            return section

        block.gpsimd(make("g"))
        block.vector(make("v"))
        block.scalar(make("s"))
        block.tensor(make("t"))
        block.sync(make("y"))

    def close(self):
        self.es.close()


class Buf:

    def __init__(self):
        self.writes = {}
        self.reads = {}

    @property
    def write_list(self):
        return list(self.writes.values())

    def w_waits(self):
        return list(self.reads.values()) + list(self.writes.values())

    def wrote(self, ev):
        if ev is not None:
            old = self.writes.get(ev[0])
            if old is None or ev[1] > old[1]:
                self.writes[ev[0]] = ev

    def r_waits(self):
        return list(self.writes.values())

    def read(self, ev):
        if ev is not None:
            old = self.reads.get(ev[0])
            if old is None or ev[1] > old[1]:
                self.reads[ev[0]] = ev


def emit_A1(p, g, io, debug=False, stats_only=False, passes=2, p1_phases=99):
    HALO = 128
    WT = TPC + HALO
    WINW = 1152
    x_fm = io["x_fm"]
    x_tm = io["x_tm"]
    wqkv = io["wqkv"]
    wo = io["wo"]
    g_row = io["g_row"]
    maskA = io["maskA"]
    maskB = io["maskB"]
    s0_o = io["s0_o"]
    rout_o = io["rout_o"]
    xn_o = io.get("xn_o")
    xnfm_o = io.get("xnfm_o")
    if debug:
        dbg_stats = io["dbg_stats"]
        dbg_xq = io["dbg_xq"]
        dbg_qk = io["dbg_qk"]
        dbg_v = io["dbg_v"]
        dbg_sm = io["dbg_sm"]
        dbg_pp = io["dbg_pp"]
        dbg_ls = io["dbg_ls"]
        dbg_o = io["dbg_o"]
        dbg_rr = io["dbg_rr"]
        dbg_ar = io["dbg_ar"]
    idb = g["idb"]
    idf = g["idf"]
    ev_idb = ev_idf = None

    wqkv_sb = p.sb("wqkv_sb", [128, 8 * 3072], BF16)
    wo_sb = p.sb("wo_sb", [128, 8 * 1024], BF16)
    mask_sb = p.sb("mask_sb", [128, 512], F32)
    g_sb = p.sb("g_sb", [1, DM], BF16)
    xq_sb = p.sb("xq_sb", [128, 8 * WINW], BF16)
    qk_sb = p.sb("qk_sb", [128, 16 * WINW], BF16)
    v_sb = p.sb("v_sb", [128, 9 * 1024], BF16)
    o_sb = p.sb("o_sb", [128, 8 * 1024], BF16)
    xn_sb = p.sb("xn_sb", [128, 8 * 1024], BF16)
    xt = [p.sb(f"xt{i}", [128, DM], F32) for i in range(2)]
    xf = p.sb("xf", [128, WINW], F32)
    sq = xf
    r_s = p.sb("r_s", [128, 9], F32)
    ar_s = p.sb("ar_s", [128, 9], F32)
    R_s = p.sb("R_s", [128, 9], F32)
    A_s = p.sb("A_s", [128, 9], F32)
    st_sum = p.sb("st_sum", [128, 9], F32)
    st_ssq = p.sb("st_ssq", [128, 9], F32)
    st_m = p.sb("st_m", [128, 9], F32)
    var9 = p.sb("var9", [128, 9], F32)
    ve9 = p.sb("ve9", [128, 9], F32)
    tmp9 = p.sb("tmp9", [128, 9], F32)
    tmp9b = p.sb("tmp9b", [128, 9], F32)
    ones1c = p.sb("ones1c", [128, 1], F32)
    tmpd = p.sb("tmpd", [128, 1], F32)
    Rrow = p.sb("Rrow", [1, WINW], BF16)
    Arow = p.sb("Arow", [1, WINW], BF16)
    sm = [p.sb(f"sm{i}", [128, 256], BF16) for i in range(2)]
    pp = [p.sb(f"pp{i}", [128, 256], BF16) for i in range(2)]
    pt = [p.sb(f"pt{i}", [128, 256], BF16) for i in range(2)]
    mneg = [p.sb(f"mneg{i}", [128, 1], F32) for i in range(2)]
    lsum = [p.sb(f"lsum{i}", [128, 1], F32) for i in range(2)]
    rl = [p.sb(f"rl{i}", [128, 1], F32) for i in range(2)]
    s0_st = [p.sb(f"s0st{i}", [128, DM], F32) for i in range(2)]
    rout_t = p.sb("rout_t", [128, 2], F32)
    eps5 = p.sb("eps5", [128, 1], F32)
    eps6 = p.sb("eps6", [128, 1], F32)
    xnf_st = p.sb("xnf_st", [128, 1024], BF16)
    bxnf = Buf()

    PA = p.ps("PA", [128, 512])
    PB = p.ps("PB", [128, 512])
    PC = p.ps("PC", [128, 512])
    PD = p.ps("PD", [128, 512])
    PT0 = p.ps("PT0", [128, 256], BF16)
    PT1 = p.ps("PT1", [128, 256], BF16)
    PO0 = p.ps("PO0", [64, 128])
    PO1 = p.ps("PO1", [64, 128])

    bPA, bPB, bPC, bPD = Buf(), Buf(), Buf(), Buf()
    bPT = [Buf(), Buf()]
    bPO = [Buf(), Buf()]
    bxt = [Buf(), Buf()]
    bsq = bxf = Buf()
    bsm = [Buf(), Buf()]
    bpp = [Buf(), Buf()]
    bpt = [Buf(), Buf()]
    bs0 = [Buf(), Buf()]
    bxq, bqk, bv, bo, bxn = Buf(), Buf(), Buf(), Buf(), Buf()
    brows, brt = Buf(), Buf()

    evs_wqkv = []
    for k in range(8):
        evs_wqkv.append(p.emit("g", lambda e, k=k: e.dma_start(
            out=wqkv_sb[:, k * 3072:(k + 1) * 3072],
            in_=wqkv[k * 128:(k + 1) * 128, :]), dma=True))
    evs_wo = []
    for k in range(8):
        evs_wo.append(p.emit("g", lambda e, k=k: e.dma_start(
            out=wo_sb[:, k * 1024:(k + 1) * 1024],
            in_=wo[k * 128:(k + 1) * 128, :]), dma=True))
    p.emit("y", lambda e: e.dma_start(out=mask_sb[:, 0:256], in_=maskA[:, :]), dma=True)
    ev_mask = p.emit("y", lambda e: e.dma_start(out=mask_sb[:, 256:512], in_=maskB[:, :]), dma=True)
    ev_g = p.emit("g", lambda e: e.dma_start(out=g_sb[:, :], in_=g_row[:, :]), dma=True)
    ev_e5 = p.emit("v", lambda e: e.memset(eps5[:, :], 1e-5))
    ev_e6 = p.emit("v", lambda e: e.memset(eps6[:, :], 1e-6))
    ev_o1c = p.emit("v", lambda e: e.memset(ones1c[:, :], 1.0))
    bstats = Buf()

    store_evs = []

    for ps_ in range(passes):
        WLO = 1024 * ps_
        tile_evs = []
        for j in range(9):
            tile = 8 * ps_ + j
            buf, bb = xt[j % 2], bxt[j % 2]
            ev_ld = p.emit("y", lambda e, tile=tile, buf=buf: e.dma_start(
                out=buf[:, :], in_=x_tm[tile * 128:(tile + 1) * 128, :]),
                waits=bb.w_waits(), dma=True)
            bb.wrote(ev_ld)
            if ps_ == 1 and p1_phases < 0.3:
                continue
            ev1 = p.emit("v", lambda e, buf=buf, j=j: e.tensor_reduce(
                st_sum[:, j:j + 1], buf[:, :], axis=AX.X, op=ALU.add),
                waits=[ev_ld] + (bstats.w_waits() if j == 0 else []))
            bb.read(ev1)
            ev_sq = p.emit("s", lambda e, buf=buf: e.activation(
                sq[:, 0:DM], buf[:, :], AF.Square), waits=[ev_ld] + bsq.w_waits())
            bb.read(ev_sq)
            bsq.wrote(ev_sq)
            if ps_ == 1 and p1_phases < 0.35:
                continue
            ev2 = p.emit("v", lambda e, j=j: e.tensor_reduce(
                st_ssq[:, j:j + 1], sq[:, 0:DM], axis=AX.X, op=ALU.add),
                waits=[ev_sq])
            bsq.read(ev2)
        if ps_ == 1 and p1_phases < 0.5:
            break
        ev1f = p.emit("v", lambda e: e.engine_nop(),
                      waits=[(ev1[0], ev1[1], True), (ev2[0], ev2[1], True)],
                      force=True)
        pe = ev1f

        def vstep(fn, extra=()):
            nonlocal pe
            pe = p.emit("v", fn, waits=[pe] + list(extra), force=True)
            return pe

        vstep(lambda e: e.tensor_scalar(
            st_m[:, :], st_sum[:, :], 1.0 / DM, None, ALU.mult))
        vstep(lambda e: e.tensor_tensor(
            tmp9[:, :], st_m[:, :], st_m[:, :], ALU.mult))
        vstep(lambda e: e.scalar_tensor_tensor(
            var9[:, :], st_ssq[:, :], 1.0 / DM, tmp9[:, :],
            ALU.mult, ALU.subtract))
        ev_ve = vstep(lambda e: e.tensor_scalar(
            ve9[:, :], var9[:, :], 1.0, 1e-5, ALU.mult, ALU.add))
        ev_sa = p.emit("s", lambda e: e.activation(
            r_s[:, :], ve9[:, :], AF.Sqrt), waits=[ev_ve])
        vstep(lambda e: e.reciprocal(r_s[:, :], r_s[:, :]), extra=[ev_sa])
        for _ in range(2):
            vstep(lambda e: e.tensor_tensor(
                tmp9[:, :], r_s[:, :], r_s[:, :], ALU.mult))
            vstep(lambda e: e.tensor_tensor(
                tmp9[:, :], tmp9[:, :], ve9[:, :], ALU.mult))
            vstep(lambda e: e.tensor_scalar(
                tmp9[:, :], tmp9[:, :], -0.5, 1.5, ALU.mult, ALU.add))
            vstep(lambda e: e.tensor_tensor(
                r_s[:, :], r_s[:, :], tmp9[:, :], ALU.mult))
        vstep(lambda e: e.tensor_scalar(
            tmp9[:, :], st_m[:, :], -1.0, None, ALU.mult))
        ev_ar = vstep(lambda e: e.tensor_tensor(
            ar_s[:, :], tmp9[:, :], r_s[:, :], ALU.mult))
        vstep(lambda e: e.tensor_tensor(
            tmp9[:, :], r_s[:, :], r_s[:, :], ALU.mult))
        vstep(lambda e: e.tensor_tensor(
            tmp9[:, :], tmp9[:, :], var9[:, :], ALU.mult))
        ev_ve2 = vstep(lambda e: e.tensor_scalar(
            ve9[:, :], tmp9[:, :], 1.0, 1e-6, ALU.mult, ALU.add))
        ev_sa2 = p.emit("s", lambda e: e.activation(
            R_s[:, :], ve9[:, :], AF.Sqrt), waits=[ev_ve2])
        vstep(lambda e: e.reciprocal(R_s[:, :], R_s[:, :]), extra=[ev_sa2])
        for _ in range(2):
            vstep(lambda e: e.tensor_tensor(
                tmp9[:, :], R_s[:, :], R_s[:, :], ALU.mult))
            vstep(lambda e: e.tensor_tensor(
                tmp9[:, :], tmp9[:, :], ve9[:, :], ALU.mult))
            vstep(lambda e: e.tensor_scalar(
                tmp9[:, :], tmp9[:, :], -0.5, 1.5, ALU.mult, ALU.add))
            vstep(lambda e: e.tensor_tensor(
                R_s[:, :], R_s[:, :], tmp9[:, :], ALU.mult))
        vstep(lambda e: e.tensor_tensor(
            R_s[:, :], R_s[:, :], r_s[:, :], ALU.mult))
        vstep(lambda e: e.tensor_scalar(
            tmp9[:, :], st_m[:, :], -1.0, None, ALU.mult))
        ev_RA = vstep(lambda e: e.tensor_tensor(
            A_s[:, :], tmp9[:, :], R_s[:, :], ALU.mult))
        bstats.wrote(ev_RA)
        bstats_last = ev_RA
        ev5 = ev_RA

        if ps_ == 1 and p1_phases < 0.8:
            break
        for j in range(1, 9):
            tile = 8 * ps_ + j
            buf, bb = xt[j % 2], bxt[j % 2]
            qtok = (tile - 1) * 128
            ev_ld2 = p.emit("y", lambda e, tile=tile, buf=buf: e.dma_start(
                out=buf[:, :], in_=x_tm[tile * 128:(tile + 1) * 128, :]),
                waits=bb.w_waits(), dma=True)
            bb.wrote(ev_ld2)
            evx = p.emit("v", lambda e, buf=buf, j=j: e.tensor_scalar(
                xn_sb[:, (j - 1) * 1024:j * 1024], buf[:, :],
                r_s[:, j:j + 1], ar_s[:, j:j + 1], ALU.mult, ALU.add),
                waits=[ev_RA, ev_ld2] + bxn.w_waits())
            bb.read(evx)
            bxn.wrote(evx)
            bstats.read(evx)
            if xn_o is not None:
                evxo = p.emit("y", lambda e, j=j, qtok=qtok: e.dma_start(
                    out=xn_o[qtok:qtok + 128, :],
                    in_=xn_sb[:, (j - 1) * 1024:j * 1024]), waits=[evx], dma=True)
                bxn.read(evxo)
                store_evs.append(evxo)
            if xnfm_o is not None:
                evcs = []
                for f8 in range(8):
                    psT, bbT = (PT0, bPT[0]) if f8 % 2 == 0 else (PT1, bPT[1])
                    evt = p.emit("t", lambda e, j=j, f8=f8, psT=psT: e.transpose(
                        psT[:, 0:128],
                        xn_sb[:, (j - 1) * 1024 + f8 * 128:(j - 1) * 1024 + (f8 + 1) * 128],
                        idb[:, :]),
                        waits=[evx] + bbT.w_waits())
                    bbT.wrote(evt)
                    bxn.read(evt)
                    evc_ = p.emit("v", lambda e, f8=f8, psT=psT: e.tensor_copy(
                        xnf_st[:, f8 * 128:(f8 + 1) * 128], psT[:, 0:128]),
                        waits=[evt] + (bxnf.w_waits() if f8 == 0 else []))
                    bbT.read(evc_)
                    bxnf.wrote(evc_)
                    evcs.append(evc_)
                ev_fo = p.emit("y", lambda e, qtok=qtok: e.dma_start(
                    out=xnfm_o.rearrange("(k p) n -> p k n", p=128)[:, :, qtok:qtok + 128],
                    in_=xnf_st[:].rearrange("p (k n) -> p k n", n=128)),
                    waits=evcs, dma=True)
                bxnf.read(ev_fo)
                store_evs.append(ev_fo)
            evr = p.emit("v", lambda e, buf=buf, j=j: e.tensor_scalar(
                rout_t[:, :], buf[:, 0:2],
                r_s[:, j:j + 1], ar_s[:, j:j + 1], ALU.mult, ALU.add),
                waits=brt.w_waits())
            bb.read(evr)
            brt.wrote(evr)
            bstats.read(evr)
            evro = p.emit("y", lambda e, qtok=qtok: e.dma_start(
                out=rout_o[qtok:qtok + 128, :], in_=rout_t[:, :]),
                waits=[evr], dma=True)
            brt.read(evro)
            store_evs.append(evro)

        if stats_only:
            for idx, src_ in enumerate((st_sum, st_ssq, var9, r_s, ar_s, R_s)):
                evd = p.emit("y", lambda e, idx=idx, src_=src_: e.dma_start(
                    out=dbg_stats[:, idx * 9:(idx + 1) * 9], in_=src_[:, :]),
                    waits=[("v", p.counts["v"], False)], dma=True)
                store_evs.append(evd)
            break

        if ps_ == 1 and p1_phases < 2:
            break
        for j in range(9):
            seg = slice(j * 128, (j + 1) * 128)
            evd1 = p.emit("v", lambda e, j=j: e.tensor_scalar(
                tmpd[:, :], ones1c[:, :], R_s[:, j:j + 1], None, ALU.mult),
                waits=[bstats_last, ev_o1c])
            bstats.read(evd1)
            evt1 = p.emit("t", lambda e: e.transpose(
                PO0[0:1, 0:128], tmpd[:, :], idf[:, :]),
                waits=[evd1, ev_idf] + bPO[0].w_waits())
            bPO[0].wrote(evt1)
            evc1 = p.emit("v", lambda e, seg=seg: e.tensor_copy(
                Rrow[:, seg], PO0[0:1, 0:128]), waits=[evt1])
            bPO[0].read(evc1)
            evd2 = p.emit("v", lambda e, j=j: e.tensor_scalar(
                tmpd[:, :], ones1c[:, :], A_s[:, j:j + 1], None, ALU.mult),
                waits=[evt1])
            bstats.read(evd2)
            evt2 = p.emit("t", lambda e: e.transpose(
                PO1[0:1, 0:128], tmpd[:, :], idf[:, :]),
                waits=[evd2] + bPO[1].w_waits())
            bPO[1].wrote(evt2)
            evc2 = p.emit("v", lambda e, seg=seg: e.tensor_copy(
                Arow[:, seg], PO1[0:1, 0:128]), waits=[evt2])
            bPO[1].read(evc2)
            brows.wrote(evc2)

        ev_rows = brows.write_list[-1]

        if ps_ == 1 and p1_phases < 3:
            break
        for f in range(8):
            ev_ldf = p.emit("y", lambda e, f=f, WLO=WLO: e.dma_start(
                out=xf[:, :], in_=x_fm[f * 128:(f + 1) * 128, WLO:WLO + WINW]),
                waits=bxf.w_waits(), dma=True)
            bxf.wrote(ev_ldf)
            for w in range(3):
                cs = slice(w * 384, (w + 1) * 384)
                pb1, bb1 = (PA, bPA) if w % 2 == 0 else (PC, bPC)
                pb2, bb2 = (PB, bPB) if w % 2 == 0 else (PD, bPD)
                ev_o1 = p.emit("t", lambda e, f=f, cs=cs, pb1=pb1: e.matmul(
                    pb1[:, 0:384], g_sb[0:1, f * 128:(f + 1) * 128],
                    Rrow[0:1, cs], start=True, stop=True),
                    waits=[ev_rows, ev_g] + bb1.w_waits())
                bb1.wrote(ev_o1)
                ev_o2 = p.emit("t", lambda e, f=f, cs=cs, pb2=pb2: e.matmul(
                    pb2[:, 0:384], g_sb[0:1, f * 128:(f + 1) * 128],
                    Arow[0:1, cs], start=True, stop=True),
                    waits=bb2.w_waits())
                bb2.wrote(ev_o2)
                evm = p.emit("v", lambda e, cs=cs, pb1=pb1: e.tensor_tensor(
                    xf[:, cs], xf[:, cs], pb1[:, 0:384], ALU.mult),
                    waits=[ev_ldf, ev_o1])
                bb1.read(evm)
                eva = p.emit("v", lambda e, f=f, cs=cs, pb2=pb2, w=w: e.tensor_tensor(
                    xq_sb[:, f * WINW + w * 384: f * WINW + (w + 1) * 384],
                    xf[:, cs], pb2[:, 0:384], ALU.add),
                    waits=[ev_o2] + bxq.w_waits())
                bb2.read(eva)
                bxf.read(eva)
                bxq.wrote(eva)
        ev_xq = bxq.write_list[-1]

        if ps_ == 1 and p1_phases < 4:
            break
        for oc in range(16):
            for w in range(3):
                pb, bb = (PA, bPA) if (oc * 3 + w) % 2 == 0 else (PB, bPB)
                ev_mm = None
                for k in range(8):
                    ev_mm = p.emit("t", lambda e, oc=oc, w=w, k=k, pb=pb: e.matmul(
                        pb[:, 0:384],
                        wqkv_sb[:, k * 3072 + oc * 128:k * 3072 + (oc + 1) * 128],
                        xq_sb[:, k * WINW + w * 384:k * WINW + (w + 1) * 384],
                        start=(k == 0), stop=(k == 7)),
                        waits=(bb.w_waits() + [ev_xq] + evs_wqkv) if k == 0 else [],
                        sig=(k == 7))
                bb.wrote(ev_mm)
                bxq.read(ev_mm)
                eve = p.emit("s", lambda e, oc=oc, w=w, pb=pb: e.activation(
                    qk_sb[:, oc * WINW + w * 384:oc * WINW + (w + 1) * 384],
                    pb[:, 0:384], AF.Copy), waits=[ev_mm] + bqk.w_waits())
                bb.read(eve)
                bqk.wrote(eve)
        ev_qk = bqk.write_list[-1]

        if ps_ == 1 and p1_phases < 5:
            break
        for j in range(9):
            for nh in range(2):
                pb, bb = (PC, bPC) if (j * 2 + nh) % 2 == 0 else (PD, bPD)
                ev_mm = None
                for k in range(8):
                    ev_mm = p.emit("t", lambda e, j=j, nh=nh, k=k, pb=pb: e.matmul(
                        pb[:, :],
                        xq_sb[:, k * WINW + j * 128:k * WINW + (j + 1) * 128],
                        wqkv_sb[:, k * 3072 + 2048 + nh * 512:k * 3072 + 2048 + (nh + 1) * 512],
                        start=(k == 0), stop=(k == 7)),
                        waits=(bb.w_waits() + [ev_xq] + evs_wqkv) if k == 0 else [],
                        sig=(k == 7))
                bb.wrote(ev_mm)
                bxq.read(ev_mm)
                eve = p.emit("s", lambda e, j=j, nh=nh, pb=pb: e.activation(
                    v_sb[:, j * 1024 + nh * 512:j * 1024 + (nh + 1) * 512],
                    pb[:, :], AF.Copy), waits=[ev_mm] + bv.w_waits())
                bb.read(eve)
                bv.wrote(eve)
        ev_v = bv.write_list[-1]

        if ps_ == 1 and p1_phases < 6:
            break
        inst = 0
        for qt in range(8):
            for hd in range(16):
                i2 = inst % 2
                base = (hd % 2) * 64
                occ = hd // 2
                psS, bbS = (PC, bPC) if i2 == 0 else (PD, bPD)
                ev_s = p.emit("t", lambda e, qt=qt, base=base, occ=occ, psS=psS: e.matmul(
                    psS[:, 0:256],
                    qk_sb[base:base + 64, occ * WINW + 128 + qt * 128: occ * WINW + 128 + (qt + 1) * 128],
                    qk_sb[base:base + 64, (8 + occ) * WINW + qt * 128: (8 + occ) * WINW + qt * 128 + 256],
                    start=True, stop=True),
                    waits=bbS.w_waits() + [ev_qk])
                bbS.wrote(ev_s)
                bqk.read(ev_s)
                mask_c = 0 if (ps_ == 0 and qt == 0) else 256
                ev_m = p.emit("v", lambda e, i2=i2, psS=psS, mask_c=mask_c: e.scalar_tensor_tensor(
                    sm[i2][:, :], psS[:, 0:256], 0.125, mask_sb[:, mask_c:mask_c + 256],
                    ALU.mult, ALU.add),
                    waits=[ev_s, ev_mask] + bsm[i2].w_waits())
                bbS.read(ev_m)
                bsm[i2].wrote(ev_m)
                ev_mx = p.emit("v", lambda e, i2=i2: e.tensor_reduce(
                    mneg[i2][:, :], sm[i2][:, :], axis=AX.X, op=ALU.max,
                    negate=True), waits=[ev_m])
                ev_e = p.emit("s", lambda e, i2=i2: e.activation(
                    pp[i2][:, :], sm[i2][:, :], AF.Exp, bias=mneg[i2][:, :],
                    accum_out=lsum[i2][:, :]),
                    waits=[ev_mx] + bpp[i2].w_waits())
                bsm[i2].read(ev_e)
                ev_rl = p.emit("v", lambda e, i2=i2: e.reciprocal(
                    rl[i2][:, :], lsum[i2][:, :]), waits=[ev_e], force=True)
                ev_pn = p.emit("v", lambda e, i2=i2: e.tensor_scalar(
                    pp[i2][:, :], pp[i2][:, :], rl[i2][:, :], None, ALU.mult),
                    waits=[ev_rl])
                bpp[i2].wrote(ev_pn)
                psT, bbT = (PT0, bPT[0]) if i2 == 0 else (PT1, bPT[1])
                evt = None
                for kc in range(2):
                    evt = p.emit("t", lambda e, i2=i2, kc=kc, psT=psT: e.transpose(
                        psT[:, kc * 128:(kc + 1) * 128],
                        pp[i2][:, kc * 128:(kc + 1) * 128], idb[:, :]),
                        waits=([ev_pn, ev_idb] + bbT.w_waits()) if kc == 0 else [],
                        sig=(kc == 1))
                bpp[i2].read(evt)
                bbT.wrote(evt)
                evc = p.emit("v", lambda e, i2=i2, psT=psT: e.tensor_copy(
                    pt[i2][:, :], psT[:, :]), waits=[evt] + bpt[i2].w_waits())
                bbT.read(evc)
                bpt[i2].wrote(evc)
                psO, bbO = (PO0, bPO[0]) if i2 == 0 else (PO1, bPO[1])
                ev_pv = None
                for kc in range(2):
                    ev_pv = p.emit("t", lambda e, qt=qt, hd=hd, kc=kc, i2=i2, psO=psO: e.matmul(
                        psO[:, :],
                        v_sb[:, (qt + kc) * 1024 + hd * 64:(qt + kc) * 1024 + (hd + 1) * 64],
                        pt[i2][:, kc * 128:(kc + 1) * 128],
                        start=(kc == 0), stop=(kc == 1)),
                        waits=([evc, ev_v] + bbO.w_waits()) if kc == 0 else [],
                        sig=(kc == 1))
                bpt[i2].read(ev_pv)
                bv.read(ev_pv)
                bbO.wrote(ev_pv)
                ev_oe = p.emit("s", lambda e, base=base, occ=occ, qt=qt, psO=psO: e.activation(
                    o_sb[base:base + 64, occ * 1024 + qt * 128:occ * 1024 + (qt + 1) * 128],
                    psO[:, :], AF.Copy),
                    waits=[ev_pv] + bo.w_waits())
                bbO.read(ev_oe)
                bo.wrote(ev_oe)
                inst += 1
        ev_o_all = bo.write_list[-1]

        if debug and ps_ == 0:
            evd = p.emit("y", lambda e: e.dma_start(out=dbg_rr[:, :], in_=Rrow[:, :]),
                         waits=brows.r_waits(), dma=True)
            brows.read(evd)
            store_evs.append(evd)
            evd = p.emit("y", lambda e: e.dma_start(out=dbg_ar[:, :], in_=Arow[:, :]),
                         waits=brows.r_waits(), dma=True)
            brows.read(evd)
            store_evs.append(evd)
            for nm, sbuf_, bb_, last in (
                ("dbg_xq", xq_sb, bxq, ev_xq), ("dbg_qk", qk_sb, bqk, ev_qk),
                ("dbg_v", v_sb, bv, ev_v), ("dbg_o", o_sb, bo, ev_o_all)):
                tgt = {"dbg_xq": dbg_xq, "dbg_qk": dbg_qk, "dbg_v": dbg_v,
                       "dbg_o": dbg_o}[nm]
                evd = p.emit("y", lambda e, tgt=tgt, sbuf_=sbuf_: e.dma_start(
                    out=tgt[:, :], in_=sbuf_[:, :]), waits=[last], dma=True)
                bb_.read(evd)
                store_evs.append(evd)
            for idx, src_ in enumerate((st_m, var9, r_s, ar_s, R_s, A_s)):
                evd = p.emit("y", lambda e, idx=idx, src_=src_: e.dma_start(
                    out=dbg_stats[:, idx * 9:(idx + 1) * 9], in_=src_[:, :]),
                    waits=[bstats_last], dma=True)
                store_evs.append(evd)
            evd = p.emit("y", lambda e: e.dma_start(out=dbg_sm[:, :], in_=sm[0][:, :]),
                         waits=bsm[0].r_waits() + bsm[0].write_list, dma=True)
            bsm[0].read(evd)
            store_evs.append(evd)
            evd = p.emit("y", lambda e: e.dma_start(out=dbg_pp[:, :], in_=pp[0][:, :]),
                         waits=bpp[0].write_list, dma=True)
            bpp[0].read(evd)
            store_evs.append(evd)
            evd = p.emit("y", lambda e: e.dma_start(out=dbg_ls[:, :], in_=lsum[0][:, :]), dma=True)
            store_evs.append(evd)

        if ps_ == 1 and p1_phases < 7:
            break
        for tt in range(8):
            sbuf, bsb = s0_st[tt % 2], bs0[tt % 2]
            for nh in range(2):
                pb, bb = (PA, bPA) if nh == 0 else (PB, bPB)
                ev_mm = None
                for oc in range(8):
                    ev_mm = p.emit("t", lambda e, tt=tt, nh=nh, oc=oc, pb=pb: e.matmul(
                        pb[:, :],
                        o_sb[:, oc * 1024 + tt * 128:oc * 1024 + (tt + 1) * 128],
                        wo_sb[:, oc * 1024 + nh * 512:oc * 1024 + nh * 512 + 512],
                        start=(oc == 0), stop=(oc == 7)),
                        waits=(bb.w_waits() + [ev_o_all] + evs_wo) if oc == 0 else [],
                        sig=(oc == 7))
                bb.wrote(ev_mm)
                bo.read(ev_mm)
                ev_add = p.emit("v", lambda e, tt=tt, nh=nh, pb=pb, sbuf=sbuf: e.tensor_tensor(
                    sbuf[:, nh * 512:(nh + 1) * 512], pb[:, :],
                    xn_sb[:, tt * 1024 + nh * 512:tt * 1024 + (nh + 1) * 512],
                    ALU.add), waits=[ev_mm] + (bsb.w_waits() if nh == 0 else []))
                bb.read(ev_add)
                bxn.read(ev_add)
                if nh == 1:
                    bsb.wrote(ev_add)
                    qtok = ps_ * 1024 + tt * 128
                    ev_st = p.emit("y", lambda e, qtok=qtok, sbuf=sbuf: e.dma_start(
                        out=s0_o[qtok:qtok + 128, :], in_=sbuf[:, :]),
                        waits=[ev_add], dma=True)
                    bsb.read(ev_st)
                    store_evs.append(ev_st)


def mk_glob(p, nc):
    id_bf_t = nc.inline_tensor(np.eye(128, dtype=BF16NP), "id_bf")
    id_f_t = nc.inline_tensor(np.eye(128, dtype=np.float32), "id_f")
    idb = p.sb("glob_idb", [128, 128], BF16)
    idf = p.sb("glob_idf", [128, 128], F32)
    p.emit("g", lambda e: e.dma_start(out=idb[:, :], in_=id_bf_t.ap()[:, :]), dma=True)
    p.emit("g", lambda e: e.dma_start(out=idf[:, :], in_=id_f_t.ap()[:, :]), dma=True)
    return {"idb": idb, "idf": idf}


def finish_all(p):
    p.finish("y", [(k, p.counts[k], True)
                   for k in p.dma_pools["y"] + p.dma_pools["g"]
                   if p.counts[k] > 0])
    p.build()
    p.close()


def build_A1(debug=False, stats_only=False, passes=2, p1_phases=99):
    nc = bass.Bass("TRN2", target_bir_lowering=False)
    WT = TPC + 128
    io = {
        "x_fm": nc.dram_tensor("x_fm", [DM, WT], F32, kind="ExternalInput").ap(),
        "x_tm": nc.dram_tensor("x_tm", [WT, DM], F32, kind="ExternalInput").ap(),
        "wqkv": nc.dram_tensor("wqkv", [DM, 3 * DM], F32, kind="ExternalInput").ap(),
        "wo": nc.dram_tensor("wo", [DM, DM], F32, kind="ExternalInput").ap(),
        "g_row": nc.dram_tensor("g_row", [1, DM], F32, kind="ExternalInput").ap(),
        "maskA": nc.dram_tensor("maskA", [128, 256], F32, kind="ExternalInput").ap(),
        "maskB": nc.dram_tensor("maskB", [128, 256], F32, kind="ExternalInput").ap(),
        "s0_o": nc.dram_tensor("s0", [TPC, DM], F32, kind="ExternalOutput").ap(),
        "xn_o": nc.dram_tensor("xn_bf", [TPC, DM], BF16, kind="ExternalOutput").ap(),
        "rout_o": nc.dram_tensor("rout", [TPC, 2], F32, kind="ExternalOutput").ap(),
    }
    if debug:
        io["dbg_stats"] = nc.dram_tensor("dbg_stats", [128, 6 * 9], F32, kind="ExternalOutput").ap()
        io["dbg_xq"] = nc.dram_tensor("dbg_xq", [128, 8 * 1152], BF16, kind="ExternalOutput").ap()
        io["dbg_qk"] = nc.dram_tensor("dbg_qk", [128, 16 * 1152], BF16, kind="ExternalOutput").ap()
        io["dbg_v"] = nc.dram_tensor("dbg_v", [128, 9 * 1024], BF16, kind="ExternalOutput").ap()
        io["dbg_sm"] = nc.dram_tensor("dbg_sm", [128, 256], BF16, kind="ExternalOutput").ap()
        io["dbg_pp"] = nc.dram_tensor("dbg_pp", [128, 256], BF16, kind="ExternalOutput").ap()
        io["dbg_ls"] = nc.dram_tensor("dbg_ls", [128, 1], F32, kind="ExternalOutput").ap()
        io["dbg_o"] = nc.dram_tensor("dbg_o", [128, 8 * 1024], BF16, kind="ExternalOutput").ap()
        io["dbg_rr"] = nc.dram_tensor("dbg_rr", [1, 1152], BF16, kind="ExternalOutput").ap()
        io["dbg_ar"] = nc.dram_tensor("dbg_ar", [1, 1152], BF16, kind="ExternalOutput").ap()
    p = Pgm(nc)
    g = mk_glob(p, nc)
    p.fence()
    emit_A1(p, g, io, debug=debug, stats_only=stats_only, passes=passes,
            p1_phases=p1_phases)
    finish_all(p)
    return nc


def _l0_masks(first):
    i = np.arange(128)[:, None]
    j = np.arange(256)[None, :]
    base = (j > i) & (j <= i + 128)
    allowed = (base & (j >= 128)) if first else base
    return np.where(allowed, 0.0, NEG).astype(np.float32)


def _a1_inputs(x_full):
    maps = []
    for c in range(NCORE):
        b, h = c // 2, c % 2
        T0 = h * TPC
        if h == 0:
            halo = np.zeros((128, DM), np.float32)
            sl = x_full[b, 0:TPC]
        else:
            halo = x_full[b, T0 - 128:T0]
            sl = x_full[b, T0:T0 + TPC]
        x_tm = np.ascontiguousarray(np.concatenate([halo, sl], axis=0))
        x_fm = np.ascontiguousarray(x_tm.T)
        maps.append({
            "x_fm": x_fm, "x_tm": x_tm,
            "maskA": _l0_masks(h == 0), "maskB": _l0_masks(False),
        })
    return maps


def run_A1(inputs, debug=False, stats_only=False, passes=2, p1_phases=99):
    x = np.asarray(inputs["L0_attn_x"], np.float32)
    wqkv = np.ascontiguousarray(np.asarray(inputs["L0_attn_Wqkv"], np.float32))
    wo = np.ascontiguousarray(np.asarray(inputs["L0_attn_Wo"], np.float32))
    g_row = np.ascontiguousarray(np.asarray(inputs["L0_attn_g"], np.float32).reshape(1, DM))
    maps = _a1_inputs(x)
    for m in maps:
        m["wqkv"] = wqkv
        m["wo"] = wo
        m["g_row"] = g_row
    nc = build_A1(debug=debug, stats_only=stats_only, passes=passes, p1_phases=p1_phases)
    res = run_bass_kernel_spmd(nc, maps, core_ids=list(range(NCORE)))
    return res.results


def build_fused():
    nc = bass.Bass("TRN2", target_bir_lowering=False)
    WT = TPC + 128
    EXT, IN_ = 2561, 512
    E, DFF = 8, 4096
    HID, NCH = 2730, 22
    ei, eo = "ExternalInput", "ExternalOutput"

    def din(name, shape, dt=F32):
        return nc.dram_tensor(name, shape, dt, kind=ei).ap()

    def dint(name, shape, dt=F32):
        return nc.dram_tensor(name, shape, dt, kind="Internal").ap()

    x_fm = din("x_fm", [DM, WT])
    x_tm = din("x_tm", [WT, DM])
    a1_wqkv = din("a1_wqkv", [DM, 3 * DM], BF16)
    a1_wo = din("a1_wo", [DM, DM], BF16)
    a1_g = din("a1_g", [1, DM])
    maskA = din("maskA", [128, 256])
    maskB = din("maskB", [128, 256])
    a2_W1 = din("a2_W1", [E, DM, DFF], BF16)
    a2_b1 = din("a2_b1", [E, DFF])
    a2_W2 = din("a2_W2", [E, DFF, DM], BF16)
    a2_b2 = din("a2_b2", [E, DM])
    b1_wq = din("b1_wq", [DM, DM], BF16)
    b1_wk = din("b1_wk", [DM, DM], BF16)
    b1_wv = din("b1_wv", [DM, DM], BF16)
    b1_wo = din("b1_wo", [DM, DM], BF16)
    b1_vc = din("b1_vc", [128, 16])
    b1_bo = din("b1_bo", [1, DM])
    b1_wff1 = din("b1_wff1", [DM, 4 * DM], BF16)
    b1_wff2 = din("b1_wff2", [4 * DM, DM], BF16)
    b1_wd = din("b1_wd", [DM, DM], BF16)
    b1_vc2 = din("b1_vc2", [128, 16])
    b1_bf1c = din("b1_bf1c", [128, 32])
    b1_bff2 = din("b1_bff2", [1, DM])
    b1_bd = din("b1_bd", [1, DM])
    bf_tbls = din("bf_tbls", [128, 4 * EXT])
    bf_masks = din("bf_masks", [128, 5 * 640])
    hz_bf = din("hz_bf", [128, 1], BF16)
    hz_f = din("hz_f", [128, 1], F32)
    bfw = []
    for l in range(2):
        bfw.append({
            "wqkv": din(f"bf{l}_wqkv", [DM, 3 * IN_], BF16),
            "wo": din(f"bf{l}_wo", [IN_, DM], BF16),
            "qs_r": din(f"bf{l}_qs", [1, 64]),
            "ks_r": din(f"bf{l}_ks", [1, 64]),
            "w1": din(f"bf{l}_w1", [DM, 2 * HID], BF16),
            "w2": din(f"bf{l}_w2", [HID, DM], BF16),
            "vcols": din(f"bf{l}_vc", [128, 16]),
            "b1r": din(f"bf{l}_b1r", [1, 2 * HID], BF16),
            "b2_r": din(f"bf{l}_b2", [1, DM]),
        })
    x2_o = nc.dram_tensor("x2", [TPC, DM], F32, kind=eo).ap()

    s0_d = dint("s0_d", [TPC, DM])
    rout_d = dint("rout_d", [TPC, 2])
    xnfm_d = dint("xnfm_d", [DM, TPC], BF16)
    x1_d = dint("x1_d", [TPC, DM])
    x1fm_d = dint("x1fm_d", [DM, TPC])
    xn1_d = dint("xn1_d", [TPC, DM], BF16)
    xn1fm_d = dint("xn1fm_d", [DM, TPC], BF16)
    xatt_d = dint("xatt_d", [TPC, DM])
    xattfm_d = dint("xattfm_d", [DM, TPC])
    s1_d = dint("s1_d", [TPC, DM])
    xp0_d = dint("xp0_d", [TPC, DM])
    xp0fm_d = dint("xp0fm_d", [DM, TPC])
    y0_d = dint("y0_d", [TPC, DM])
    y0fm_d = dint("y0fm_d", [DM, TPC])
    xp1_d = dint("xp1_d", [TPC, DM])
    xp1fm_d = dint("xp1fm_d", [DM, TPC])
    y1_d = dint("y1_d", [TPC, DM])
    cc_mk_in = dint("cc_mk_in", [128, 8])
    cc_mk_out = dint("cc_mk_out", [256, 8])
    cc_C_in = dint("cc_C_in", [129, 2048])
    cc_C_out = dint("cc_C_out", [129, 2048])
    cc_xn1_in = dint("cc_xn1_in", [DM, 513], BF16)
    cc_xn1_out = dint("cc_xn1_out", [2 * DM, 513], BF16)
    cc_y0_in = dint("cc_y0_in", [DM, 513])
    cc_y0_out = dint("cc_y0_out", [2 * DM, 513])
    cc_xp0_in = dint("cc_xp0_in", [DM, 1])
    cc_xp0_out = dint("cc_xp0_out", [2 * DM, 1])
    cc_xp1_in = dint("cc_xp1_in", [DM, 1])
    cc_xp1_out = dint("cc_xp1_out", [2 * DM, 1])
    xp0c0_d = dint("xp0c0_d", [DM, 1])
    xp1c0_d = dint("xp1c0_d", [DM, 1])

    RG = [[0, 1], [2, 3], [4, 5], [6, 7]]
    p = Pgm(nc, coll=True)
    g = mk_glob(p, nc)
    g["rotm"] = p.sb("glob_rotm", [128, 128], BF16)
    rot_t = nc.inline_tensor(_rot_matrix(), "rot_t")
    p.emit("g", lambda e: e.dma_start(out=g["rotm"][:, :], in_=rot_t.ap()[:, :]),
           dma=True)
    g["hzf"] = p.sb("glob_hzf", [128, 1], F32)
    p.emit("y", lambda e: e.dma_start(out=g["hzf"][:, :], in_=hz_f[:, :]),
           dma=True)
    p.fence()

    p.push("a1_")
    emit_A1(p, g, {
        "x_fm": x_fm, "x_tm": x_tm, "wqkv": a1_wqkv, "wo": a1_wo,
        "g_row": a1_g, "maskA": maskA, "maskB": maskB,
        "s0_o": s0_d, "rout_o": rout_d, "xnfm_o": xnfm_d,
    })
    p.pop()
    p.fence()

    p.push("a2_")
    emit_A2(p, g, {
        "xn_fm": xnfm_d, "rout": rout_d, "x_tm": x_tm[128:WT, :],
        "s0_tm": s0_d, "W1": a2_W1, "b1": a2_b1, "W2": a2_W2, "b2": a2_b2,
        "x1_o": x1_d, "x1fm_o": x1fm_d,
    })
    p.pop()
    p.fence()

    p.push("b1a_")
    emit_B1a(p, g, {
        "x1p_tm": x1_d, "x1p_fm": x1fm_d,
        "wq": b1_wq, "wk": b1_wk, "wv": b1_wv, "wo": b1_wo,
        "vcols": b1_vc, "bo_r": b1_bo,
        "xatt_o": xatt_d, "xn1_o": xn1_d, "xn1fm_o": xn1fm_d,
        "xattfm_o": xattfm_d,
        "cc_mk_in": cc_mk_in, "cc_mk_out": cc_mk_out,
        "cc_C_in": cc_C_in, "cc_C_out": cc_C_out,
    }, pair=True)
    p.pop()
    p.fence()
    ev_t1 = p.emit("y", lambda e: e.dma_start(
        out=cc_xn1_in[:, :], in_=xn1fm_d[:, TPC - 513:TPC]), dma=True)
    p.emit("g", lambda e: e.collective_compute(
        "AllGather", ALU.bypass, replica_groups=RG,
        ins=[cc_xn1_in.opt()], outs=[cc_xn1_out.opt()]),
        waits=[ev_t1], coll=True)
    p.fence()

    p.push("b1b_")
    emit_B1b(p, g, {
        "xatt_tm": xatt_d, "xatt_fm": xattfm_d,
        "wff1": b1_wff1, "wff2": b1_wff2, "w_d": b1_wd,
        "vcols": b1_vc2, "bf1c": b1_bf1c, "bff2_r": b1_bff2, "bd_r": b1_bd,
        "s1_o": s1_d,
    })
    p.pop()
    p.fence()

    p.push("bf0a_")
    emit_Bf_attn(p, g, {
        "y_halo": cc_xn1_out[0:DM, :], "y_own": xn1fm_d, "hz": hz_f,
        "y_tm": xn1_d, "y_tm_cast": True,
        "wqkv": bfw[0]["wqkv"], "wo": bfw[0]["wo"],
        "qs_r": bfw[0]["qs_r"], "ks_r": bfw[0]["ks_r"],
        "tbls": bf_tbls, "masks": bf_masks,
        "xp_o": xp0_d, "xpfm_o": xp0fm_d,
    })
    p.pop()
    p.fence()
    ev_t2 = p.emit("y", lambda e: e.dma_start(
        out=cc_xp0_in[:, :], in_=xp0fm_d[:, TPC - 1:TPC],
        allow_slow_non_contiguous=True), dma=True)
    ev_c2 = p.emit("g", lambda e: e.collective_compute(
        "AllGather", ALU.bypass, replica_groups=RG,
        ins=[cc_xp0_in.opt()], outs=[cc_xp0_out.opt()]),
        waits=[ev_t2], coll=True)
    p.push("g0_")
    c0sb = p.sb("c0sb", [128, 8], F32)
    ev_l = p.emit("y", lambda e: e.dma_start(
        out=c0sb[:].rearrange("p (k c) -> p k c", c=1),
        in_=cc_xp0_out[0:DM, :].rearrange("(k p) c -> p k c", p=128),
        allow_slow_non_contiguous=True),
        waits=[ev_c2], dma=True)
    ev_m = p.emit("v", lambda e: e.tensor_scalar(
        c0sb[:, :], c0sb[:, :], g["hzf"][:, 0:1], None, ALU.mult),
        waits=[ev_l])
    ev_o = p.emit("y", lambda e: e.dma_start(
        out=xp0c0_d.rearrange("(k p) c -> p k c", p=128),
        in_=c0sb[:].rearrange("p (k c) -> p k c", c=1),
        allow_slow_non_contiguous=True),
        waits=[ev_m], dma=True)
    p.pop()
    p.fence()

    p.push("bf0f_")
    emit_Bf_ffn(p, g, {
        "xp_own": xp0fm_d, "xp_c0": xp0c0_d, "xp_tm": xp0_d,
        "w1": bfw[0]["w1"], "w2": bfw[0]["w2"], "vcols": bfw[0]["vcols"],
        "b1r": bfw[0]["b1r"],
        "b2_r": bfw[0]["b2_r"],
        "y_o": y0_d, "yfm_o": y0fm_d,
    })
    p.pop()
    p.fence()
    ev_t3 = p.emit("y", lambda e: e.dma_start(
        out=cc_y0_in[:, :], in_=y0fm_d[:, TPC - 513:TPC]), dma=True)
    p.emit("g", lambda e: e.collective_compute(
        "AllGather", ALU.bypass, replica_groups=RG,
        ins=[cc_y0_in.opt()], outs=[cc_y0_out.opt()]),
        waits=[ev_t3], coll=True)
    p.fence()

    p.push("bf1a_")
    emit_Bf_attn(p, g, {
        "y_halo": cc_y0_out[0:DM, :], "y_own": y0fm_d, "hz": hz_f,
        "y_tm": y0_d,
        "wqkv": bfw[1]["wqkv"], "wo": bfw[1]["wo"],
        "qs_r": bfw[1]["qs_r"], "ks_r": bfw[1]["ks_r"],
        "tbls": bf_tbls, "masks": bf_masks,
        "xp_o": xp1_d, "xpfm_o": xp1fm_d,
    })
    p.pop()
    p.fence()
    ev_t4 = p.emit("y", lambda e: e.dma_start(
        out=cc_xp1_in[:, :], in_=xp1fm_d[:, TPC - 1:TPC],
        allow_slow_non_contiguous=True), dma=True)
    ev_c4 = p.emit("g", lambda e: e.collective_compute(
        "AllGather", ALU.bypass, replica_groups=RG,
        ins=[cc_xp1_in.opt()], outs=[cc_xp1_out.opt()]),
        waits=[ev_t4], coll=True)
    p.push("g1_")
    c1sb = p.sb("c1sb", [128, 8], F32)
    ev_l = p.emit("y", lambda e: e.dma_start(
        out=c1sb[:].rearrange("p (k c) -> p k c", c=1),
        in_=cc_xp1_out[0:DM, :].rearrange("(k p) c -> p k c", p=128),
        allow_slow_non_contiguous=True),
        waits=[ev_c4], dma=True)
    ev_m = p.emit("v", lambda e: e.tensor_scalar(
        c1sb[:, :], c1sb[:, :], g["hzf"][:, 0:1], None, ALU.mult),
        waits=[ev_l])
    ev_o = p.emit("y", lambda e: e.dma_start(
        out=xp1c0_d.rearrange("(k p) c -> p k c", p=128),
        in_=c1sb[:].rearrange("p (k c) -> p k c", c=1),
        allow_slow_non_contiguous=True),
        waits=[ev_m], dma=True)
    p.pop()
    p.fence()

    p.push("bf1f_")
    emit_Bf_ffn(p, g, {
        "xp_own": xp1fm_d, "xp_c0": xp1c0_d, "xp_tm": xp1_d,
        "w1": bfw[1]["w1"], "w2": bfw[1]["w2"], "vcols": bfw[1]["vcols"],
        "b1r": bfw[1]["b1r"],
        "b2_r": bfw[1]["b2_r"],
        "y_o": y1_d,
        "x1_tm": x1_d, "s1_tm": s1_d, "x2_o": x2_o,
    }, final=True)
    p.pop()

    finish_all(p)
    return nc


def fused_in_maps(inputs):
    HID, NCH = 2730, 22
    ins = inputs
    x = np.asarray(ins["L0_attn_x"], np.float32)
    a1maps = _a1_inputs(x)
    W1 = np.ascontiguousarray(np.asarray(ins["L0_ffn_W1"], np.float32).astype(ml_dtypes.bfloat16))
    b1 = np.ascontiguousarray(np.asarray(ins["L0_ffn_b1"], np.float32))
    W2 = np.ascontiguousarray(np.asarray(ins["L0_ffn_W2"], np.float32).astype(ml_dtypes.bfloat16))
    b2 = np.ascontiguousarray(np.asarray(ins["L0_ffn_b2"], np.float32))
    bf_b1r = [np.ascontiguousarray(
        np.asarray(ins["L1_ffn_b1"], np.float32)[l].reshape(1, 2 * HID)
        .astype(ml_dtypes.bfloat16)) for l in range(2)]
    maps = []
    for c in range(NCORE):
        b, h = c // 2, c % 2
        T0 = h * TPC
        hzv = float(h)
        m = {
            "x_fm": a1maps[c]["x_fm"], "x_tm": a1maps[c]["x_tm"],
            "maskA": a1maps[c]["maskA"], "maskB": a1maps[c]["maskB"],
            "a1_wqkv": np.ascontiguousarray(np.asarray(ins["L0_attn_Wqkv"], np.float32).astype(ml_dtypes.bfloat16)),
            "a1_wo": np.ascontiguousarray(np.asarray(ins["L0_attn_Wo"], np.float32).astype(ml_dtypes.bfloat16)),
            "a1_g": np.ascontiguousarray(np.asarray(ins["L0_attn_g"], np.float32).reshape(1, DM)),
            "a2_W1": W1, "a2_b1": b1, "a2_W2": W2, "a2_b2": b2,
            "b1_wq": np.ascontiguousarray(np.asarray(ins["L1_attn_wq"], np.float32).astype(ml_dtypes.bfloat16)),
            "b1_wk": np.ascontiguousarray(np.asarray(ins["L1_attn_wk"], np.float32).astype(ml_dtypes.bfloat16)),
            "b1_wv": np.ascontiguousarray(np.asarray(ins["L1_attn_wv"], np.float32).astype(ml_dtypes.bfloat16)),
            "b1_wo": np.ascontiguousarray(np.asarray(ins["L1_attn_wo"], np.float32).astype(ml_dtypes.bfloat16)),
            "b1_vc": _vcols2(np.asarray(ins["L1_attn_ln1_g"], np.float32),
                             np.asarray(ins["L1_attn_ln1_b"], np.float32)),
            "b1_bo": np.ascontiguousarray(np.asarray(ins["L1_attn_bo"], np.float32).reshape(1, DM)),
            "b1_wff1": np.ascontiguousarray(np.asarray(ins["L1_attn_w_ff1"], np.float32).astype(ml_dtypes.bfloat16)),
            "b1_wff2": np.ascontiguousarray(np.asarray(ins["L1_attn_w_ff2"], np.float32).astype(ml_dtypes.bfloat16)),
            "b1_wd": np.ascontiguousarray(np.asarray(ins["L1_attn_w_d"], np.float32).astype(ml_dtypes.bfloat16)),
            "b1_vc2": _vcols2(np.asarray(ins["L1_attn_ln2_g"], np.float32),
                              np.asarray(ins["L1_attn_ln2_b"], np.float32)),
            "b1_bff2": np.ascontiguousarray(np.asarray(ins["L1_attn_b_ff2"], np.float32).reshape(1, DM)),
            "b1_bd": np.ascontiguousarray(np.asarray(ins["L1_attn_b_d"], np.float32).reshape(1, DM)),
            "bf_tbls": _bf_tables(T0),
            "bf_masks": _bf_masks(h == 0),
            "hz_bf": np.full((128, 1), hzv, ml_dtypes.bfloat16),
            "hz_f": np.full((128, 1), hzv, np.float32),
        }
        bf1 = np.asarray(ins["L1_attn_b_ff1"], np.float32)
        bf1c = np.zeros((128, 32), np.float32)
        for cc in range(32):
            bf1c[:, cc] = bf1[cc * 128:(cc + 1) * 128]
        m["b1_bf1c"] = bf1c
        for l in range(2):
            m[f"bf{l}_wqkv"] = np.ascontiguousarray(np.asarray(ins["L1_ffn_wqkv"], np.float32)[l].astype(ml_dtypes.bfloat16))
            m[f"bf{l}_wo"] = np.ascontiguousarray(np.asarray(ins["L1_ffn_wo"], np.float32)[l].astype(ml_dtypes.bfloat16))
            m[f"bf{l}_qs"] = np.ascontiguousarray(np.asarray(ins["L1_ffn_q_scale"], np.float32)[l].reshape(1, 64))
            m[f"bf{l}_ks"] = np.ascontiguousarray(np.asarray(ins["L1_ffn_k_scale"], np.float32)[l].reshape(1, 64))
            m[f"bf{l}_w1"] = np.ascontiguousarray(np.asarray(ins["L1_ffn_w1"], np.float32)[l].astype(ml_dtypes.bfloat16))
            m[f"bf{l}_w2"] = np.ascontiguousarray(np.asarray(ins["L1_ffn_w2"], np.float32)[l].astype(ml_dtypes.bfloat16))
            m[f"bf{l}_vc"] = _vcols2(np.asarray(ins["L1_ffn_ln_g"], np.float32)[l],
                                     np.asarray(ins["L1_ffn_ln_b"], np.float32)[l])
            m[f"bf{l}_b1r"] = bf_b1r[l]
            m[f"bf{l}_b2"] = np.ascontiguousarray(np.asarray(ins["L1_ffn_b2"], np.float32)[l].reshape(1, DM))
        maps.append(m)
    return maps


def run_fused(inputs):
    nc = build_fused()
    maps = fused_in_maps(inputs)
    res = run_bass_kernel_spmd(nc, maps, core_ids=list(range(NCORE)))
    return res.results


def kernel(**inputs):
    ins = {k: np.asarray(v) for k, v in inputs.items()}
    res = run_fused(ins)
    out = np.zeros((4, 2 * TPC, DM), np.float32)
    for c in range(NCORE):
        b, h = c // 2, c % 2
        out[b, h * TPC:(h + 1) * TPC] = np.asarray(res[c]["x2"], np.float32)
    return out


def kernel_unfused(**inputs):
    ins = {k: np.asarray(v) for k, v in inputs.items()}
    a1 = run_A1(ins)
    a2 = run_A2(ins, a1)
    x1_by_core = [np.asarray(a2[c]["x1"], np.float32) for c in range(NCORE)]
    b1a = run_B1a(ins, x1_by_core)
    b1b = run_B1b(ins, [np.asarray(b1a[c]["xatt"], np.float32)
                        for c in range(NCORE)])
    s1_by_core = [np.asarray(b1b[c]["s1"], np.float32) for c in range(NCORE)]
    xn1_by_core = [np.asarray(b1a[c]["xn1_bf"], np.float32)
                   for c in range(NCORE)]
    f0a = run_Bf_attn(ins, 0, xn1_by_core)
    f0f = run_Bf_ffn(ins, 0, [f0a[c]["xp"] for c in range(NCORE)])
    y0 = [np.asarray(f0f[c]["y_out"], np.float32) for c in range(NCORE)]
    f1a = run_Bf_attn(ins, 1, y0)
    f1f = run_Bf_ffn(ins, 1, [f1a[c]["xp"] for c in range(NCORE)],
                     final=True, x1_by_core=x1_by_core, s1_by_core=s1_by_core)
    out = np.zeros((4, 2 * TPC, DM), np.float32)
    for c in range(NCORE):
        b, h = c // 2, c % 2
        out[b, h * TPC:(h + 1) * TPC] = np.asarray(f1f[c]["x2"], np.float32)
    return out


def emit_A2(p, g, io):
    E, DFF, HD = 8, 4096, 2048
    xn_fm = io["xn_fm"]
    rout = io["rout"]
    x_tm = io["x_tm"]
    s0_tm = io["s0_tm"]
    W1 = io["W1"]
    b1 = io["b1"]
    W2 = io["W2"]
    b2 = io["b2"]
    x1_o = io["x1_o"]
    moe_o = io.get("moe_o")
    x1fm_o = io.get("x1fm_o")
    idf = g["idf"]
    ev_idf = None

    xn_g = [p.sb(f"xn_g{i}", [128, 8 * 512], BF16) for i in range(2)]
    y_acc = p.sb("y_acc", [128, 16 * 1024], F32)
    w1_sb = [p.sb(f"w1_sb{i}", [128, 8 * HD], BF16) for i in range(2)]
    w2_sb = p.sb("w2_sb", [128, 16 * 1024], BF16)
    hr_sb = p.sb("hr_sb", [128, 16 * 512], BF16)
    rt_sb = p.sb("rt_sb", [128, 16 * 2], F32)
    w_sb = p.sb("w_sb", [128, E * 16], F32)
    sR = p.sb("sR", [128, 16], F32)
    aR = p.sb("aR", [128, 16], F32)
    fR = p.sb("fR", [128, 16], F32)
    gR = p.sb("gR", [128, 16], F32)
    tR = p.sb("tR", [128, 16], F32)
    b1cp = p.sb("b1cp", [16, 128], F32)
    b1t = p.sb("b1t", [128, 16], F32)
    b2bf = p.sb("b2bf", [1, DM], BF16)
    ones_r = p.sb("ones_r", [1, 128], BF16)
    x1f_st = p.sb("x1f_st", [128, 512], F32)
    bx1f = Buf()
    xs = [p.sb("xs0", [128, DM], F32)] * 2
    ss = [p.sb("ss0", [128, DM], F32)] * 2
    PA = p.ps("PA", [128, 512])
    PB = p.ps("PB", [128, 512])
    PC = p.ps("PC", [128, 512])
    PD = p.ps("PD", [128, 512])
    PT = p.ps("PT", [128, 128])

    bxn = [Buf(), Buf()]
    by = Buf()
    bw1 = [Buf(), Buf()]
    bw2 = Buf()
    bhr = Buf()
    bPA, bPB, bPC, bPD, bPT = Buf(), Buf(), Buf(), Buf(), Buf()
    bb1cp, bb1t, bb2 = Buf(), Buf(), Buf()
    _b1_, _b2_ = Buf(), Buf()
    bxs = [_b1_] * 2
    bss = [_b2_] * 2

    ev_rt = p.emit("y", lambda e: e.dma_start(
        out=rt_sb[:].rearrange("p (t c) -> p t c", c=2),
        in_=rout.rearrange("(t p) c -> p t c", p=128)), dma=True)
    ev_ones = p.emit("v", lambda e: e.memset(ones_r[:, :], 1.0))
    ev_y0 = p.emit("v", lambda e: e.memset(y_acc[:, :], 0.0))
    by.wrote(ev_y0)
    rts = rt_sb[:].rearrange("p (t c) -> p t c", c=2)
    ev = p.emit("v", lambda e: e.tensor_tensor(
        sR[:, :], rts[:, :, 0], rts[:, :, 1], ALU.add), waits=[ev_rt])
    ev = p.emit("s", lambda e: e.activation(aR[:, :], sR[:, :], AF.Abs), waits=[ev])
    evf = p.emit("v", lambda e: e.memset(fR[:, :], 0.0))
    for k in range(1, 64):
        p.emit("v", lambda e, k=k: e.tensor_single_scalar(
            tR[:, :], aR[:, :], float(k), ALU.is_ge), waits=[ev] if k == 1 else [])
        evf = p.emit("v", lambda e: e.tensor_tensor(
            fR[:, :], fR[:, :], tR[:, :], ALU.add))
    ev_sg = p.emit("s", lambda e: e.sign(gR[:, :], sR[:, :]))
    ev_t = p.emit("v", lambda e: e.tensor_tensor(
        tR[:, :], gR[:, :], fR[:, :], ALU.mult), waits=[ev_sg, evf])
    ev_u = p.emit("v", lambda e: e.tensor_scalar(
        aR[:, :], tR[:, :], 0.125, 8.0, ALU.mult, ALU.add))
    evf2 = p.emit("v", lambda e: e.memset(fR[:, :], 0.0))
    for k in range(1, 16):
        p.emit("v", lambda e, k=k: e.tensor_single_scalar(
            gR[:, :], aR[:, :], float(k), ALU.is_ge))
        evf2 = p.emit("v", lambda e: e.tensor_tensor(
            fR[:, :], fR[:, :], gR[:, :], ALU.add))
    ev_h = p.emit("v", lambda e: e.scalar_tensor_tensor(
        sR[:, :], fR[:, :], -8.0, tR[:, :], ALU.mult, ALU.add))
    ev_h = p.emit("v", lambda e: e.tensor_scalar(
        sR[:, :], sR[:, :], 1.0, 64.0, ALU.mult, ALU.add))
    ev_w = None
    for e_ in range(E):
        p.emit("v", lambda e, e_=e_: e.tensor_single_scalar(
            tR[:, :], sR[:, :], float(e_), ALU.is_equal))
        p.emit("v", lambda e, e_=e_: e.tensor_single_scalar(
            gR[:, :], sR[:, :], float((e_ - 1) % E), ALU.is_equal))
        p.emit("v", lambda e, e_=e_: e.tensor_tensor(
            w_sb[:, 16 * e_:16 * (e_ + 1)], tR[:, :], gR[:, :], ALU.add))
        ev_w = p.emit("v", lambda e, e_=e_: e.tensor_scalar(
            w_sb[:, 16 * e_:16 * (e_ + 1)], w_sb[:, 16 * e_:16 * (e_ + 1)],
            0.5, None, ALU.mult))

    NTG = 4
    it = 0
    for e_ in range(E):
        ev_b2 = p.emit("g", lambda e, e_=e_: e.dma_start(
            out=b2bf[:, :], in_=b2[e_:e_ + 1, :]), waits=bb2.w_waits(), dma=True)
        bb2.wrote(ev_b2)
        for Hh in range(2):
            wbuf, bwb = w1_sb[it % 2], bw1[it % 2]
            evs_w1 = []
            for k in range(8):
                evs_w1.append(p.emit("g", lambda e, e_=e_, Hh=Hh, k=k, wbuf=wbuf: e.dma_start(
                    out=wbuf[:, k * HD:(k + 1) * HD],
                    in_=W1[e_, k * 128:(k + 1) * 128, Hh * HD:(Hh + 1) * HD]),
                    waits=bwb.w_waits() if k == 0 else [], dma=True))
            bwb.wrote(evs_w1[-1])
            for ev2_ in evs_w1:
                bwb.wrote(ev2_)
            evs_w2 = []
            for q in range(4):
                evs_w2.append(p.emit("g", lambda e, e_=e_, Hh=Hh, q=q: e.dma_start(
                    out=w2_sb[:, q * 4096:(q + 1) * 4096].rearrange(
                        "p (c n) -> p c n", n=1024),
                    in_=W2.rearrange("e (c p) n -> e p c n", p=128)[
                        e_, :, Hh * 16 + q * 4:Hh * 16 + (q + 1) * 4, :]),
                    waits=bw2.w_waits() if q == 0 else [], dma=True))
            for ev2_ in evs_w2:
                bw2.wrote(ev2_)
            ev_b1 = p.emit("y", lambda e, e_=e_, Hh=Hh: e.dma_start(
                out=b1cp[:, :],
                in_=b1.rearrange("e (c p) -> e c p", p=128)[e_, Hh * 16:(Hh + 1) * 16, :]),
                waits=bb1cp.w_waits(), dma=True)
            bb1cp.wrote(ev_b1)
            ev_b1t = p.emit("t", lambda e: e.transpose(
                PT[:, 0:16], b1cp[:, :], idf[0:16, 0:16]),
                waits=[ev_b1, ev_idf] + bPT.w_waits())
            bPT.wrote(ev_b1t)
            bb1cp.read(ev_b1t)
            ev_b1c = p.emit("v", lambda e: e.tensor_copy(b1t[:, :], PT[:, 0:16]),
                            waits=[ev_b1t] + bb1t.w_waits())
            bPT.read(ev_b1c)
            bb1t.wrote(ev_b1c)
            for gg in range(NTG):
                xbuf, bxb = xn_g[gg % 2], bxn[gg % 2]
                ev_x = p.emit("y", lambda e, gg=gg, xbuf=xbuf: e.dma_start(
                    out=xbuf[:].rearrange("p (k n) -> p k n", n=512),
                    in_=xn_fm.rearrange("(k p) n -> p k n", p=128)[
                        :, :, gg * 512:(gg + 1) * 512]),
                    waits=bxb.w_waits(), dma=True)
                bxb.wrote(ev_x)
                for c in range(16):
                    pb, bb = (PA, bPA) if c % 2 == 0 else (PB, bPB)
                    ev_mm = None
                    for k in range(8):
                        ev_mm = p.emit("t", lambda e, c=c, k=k, pb=pb, wbuf=wbuf, xbuf=xbuf: e.matmul(
                            pb[:, :],
                            wbuf[:, k * HD + c * 128:k * HD + (c + 1) * 128],
                            xbuf[:, k * 512:(k + 1) * 512],
                            start=(k == 0), stop=(k == 7)),
                            waits=(bb.w_waits() + [ev_x] + evs_w1) if k == 0 else [],
                            sig=(k == 7))
                    bb.wrote(ev_mm)
                    bxb.read(ev_mm)
                    bwb.read(ev_mm)
                    ev_r = p.emit("s", lambda e, c=c, pb=pb: e.activation(
                        hr_sb[:, c * 512:(c + 1) * 512], pb[:, :], AF.Relu,
                        bias=b1t[:, c:c + 1]),
                        waits=[ev_mm, ev_b1c] + (bhr.w_waits() if c == 0 else []))
                    bb.read(ev_r)
                    bb1t.read(ev_r)
                    bhr.wrote(ev_r)
                ev_hr = bhr.write_list[-1]
                for tt in range(4):
                    tok = gg * 4 + tt
                    wcol = 16 * e_ + tok
                    for nh in range(2):
                        pb, bb = (PC, bPC) if (tt * 2 + nh) % 2 == 0 else (PD, bPD)
                        first = True
                        ev_mm = None
                        if Hh == 0:
                            ev_mm = p.emit("t", lambda e, nh=nh, pb=pb: e.matmul(
                                pb[:, :], ones_r[:, :],
                                b2bf[:, nh * 512:(nh + 1) * 512],
                                start=True, stop=False),
                                waits=bb.w_waits() + [ev_b2, ev_ones, ev_hr] + evs_w2,
                                sig=False)
                            first = False
                        for c in range(16):
                            ev_mm = p.emit("t", lambda e, c=c, tt=tt, nh=nh, pb=pb, first=first: e.matmul(
                                pb[:, :],
                                hr_sb[:, c * 512 + tt * 128:c * 512 + tt * 128 + 128],
                                w2_sb[:, c * 1024 + nh * 512:c * 1024 + (nh + 1) * 512],
                                start=(first and c == 0), stop=(c == 15)),
                                waits=(bb.w_waits() + [ev_hr] + evs_w2) if c == 0 and first else [],
                                sig=(c == 15))
                        bb.wrote(ev_mm)
                        bhr.read(ev_mm)
                        bw2.read(ev_mm)
                        ev_st = p.emit("v", lambda e, tok=tok, nh=nh, pb=pb, wcol=wcol: e.scalar_tensor_tensor(
                            y_acc[:, tok * 1024 + nh * 512:tok * 1024 + (nh + 1) * 512],
                            pb[:, :], w_sb[:, wcol:wcol + 1],
                            y_acc[:, tok * 1024 + nh * 512:tok * 1024 + (nh + 1) * 512],
                            ALU.mult, ALU.add),
                            waits=[ev_mm, ev_w] + by.w_waits())
                        bb.read(ev_st)
                        by.wrote(ev_st)
                        by.read(ev_st)
            it += 1

    for tt in range(16):
        xb, bxb2 = xs[tt % 2], bxs[tt % 2]
        sb_, bsb2 = ss[tt % 2], bss[tt % 2]
        ya = y_acc[:, tt * 1024:(tt + 1) * 1024]
        ev_lx = p.emit("y", lambda e, tt=tt, xb=xb: e.dma_start(
            out=xb[:, :], in_=x_tm[tt * 128:(tt + 1) * 128, :]),
            waits=bxb2.w_waits(), dma=True)
        bxb2.wrote(ev_lx)
        ev_ls = p.emit("y", lambda e, tt=tt, sb_=sb_: e.dma_start(
            out=sb_[:, :], in_=s0_tm[tt * 128:(tt + 1) * 128, :]),
            waits=bsb2.w_waits(), dma=True)
        bsb2.wrote(ev_ls)
        ev_mo = None
        if moe_o is not None:
            ev_mo = p.emit("y", lambda e, tt=tt, ya=ya: e.dma_start(
                out=moe_o[tt * 128:(tt + 1) * 128, :], in_=ya),
                waits=by.write_list, dma=True)
            by.read(ev_mo)
        ev_si = p.emit("s", lambda e, ya=ya: e.activation(
            ya, ya, AF.Silu), waits=by.write_list + [ev_mo])
        by.wrote(ev_si)
        ev_m = p.emit("v", lambda e, ya=ya, sb_=sb_: e.tensor_tensor(
            ya, ya, sb_[:, :], ALU.mult), waits=[ev_si, ev_ls])
        bsb2.read(ev_m)
        by.wrote(ev_m)
        ev_a = p.emit("v", lambda e, ya=ya, xb=xb: e.tensor_tensor(
            ya, ya, xb[:, :], ALU.add), waits=[ev_lx])
        bxb2.read(ev_a)
        by.wrote(ev_a)
        ev_o = p.emit("y", lambda e, tt=tt, ya=ya: e.dma_start(
            out=x1_o[tt * 128:(tt + 1) * 128, :], in_=ya),
            waits=[ev_a], dma=True)
        by.read(ev_o)
        if x1fm_o is not None:
            for hf in range(2):
                pb, bb = (PA, bPA) if hf == 0 else (PB, bPB)
                evts = []
                for kc in range(4):
                    f8 = hf * 4 + kc
                    evts.append(p.emit("t", lambda e, tt=tt, f8=f8, kc=kc, pb=pb: e.transpose(
                        pb[:, kc * 128:(kc + 1) * 128],
                        y_acc[:, tt * 1024 + f8 * 128:tt * 1024 + (f8 + 1) * 128],
                        idf[:, :]),
                        waits=([ev_a] + bb.w_waits()) if kc == 0 else [],
                        sig=(kc == 3)))
                evt = evts[-1]
                bb.wrote(evt)
                by.read(evt)
                evc_ = p.emit("v", lambda e, pb=pb: e.tensor_copy(
                    x1f_st[:, :], pb[:, :]),
                    waits=[evt] + bx1f.w_waits())
                bb.read(evc_)
                bx1f.wrote(evc_)
                ev_fo = p.emit("y", lambda e, tt=tt, hf=hf: e.dma_start(
                    out=x1fm_o.rearrange("(k p) n -> p k n", p=128)[
                        :, hf * 4:(hf + 1) * 4, tt * 128:(tt + 1) * 128],
                    in_=x1f_st[:].rearrange("p (k n) -> p k n", n=128)),
                    waits=[evc_], dma=True)
                bx1f.read(ev_fo)


def build_A2():
    nc = bass.Bass("TRN2", target_bir_lowering=False)
    E, DFF = 8, 4096
    io = {
        "xn_fm": nc.dram_tensor("xn_fm", [DM, TPC], BF16, kind="ExternalInput").ap(),
        "rout": nc.dram_tensor("rout", [TPC, 2], F32, kind="ExternalInput").ap(),
        "x_tm": nc.dram_tensor("x_tm", [TPC, DM], F32, kind="ExternalInput").ap(),
        "s0_tm": nc.dram_tensor("s0_tm", [TPC, DM], F32, kind="ExternalInput").ap(),
        "W1": nc.dram_tensor("W1", [E, DM, DFF], F32, kind="ExternalInput").ap(),
        "b1": nc.dram_tensor("b1", [E, DFF], F32, kind="ExternalInput").ap(),
        "W2": nc.dram_tensor("W2", [E, DFF, DM], F32, kind="ExternalInput").ap(),
        "b2": nc.dram_tensor("b2", [E, DM], F32, kind="ExternalInput").ap(),
        "x1_o": nc.dram_tensor("x1", [TPC, DM], F32, kind="ExternalOutput").ap(),
        "moe_o": nc.dram_tensor("moe", [TPC, DM], F32, kind="ExternalOutput").ap(),
    }
    p = Pgm(nc)
    g = mk_glob(p, nc)
    p.fence()
    emit_A2(p, g, io)
    finish_all(p)
    return nc


def run_A2(inputs, a1_results):
    nc = build_A2()
    W1 = np.ascontiguousarray(np.asarray(inputs["L0_ffn_W1"], np.float32))
    b1 = np.ascontiguousarray(np.asarray(inputs["L0_ffn_b1"], np.float32))
    W2 = np.ascontiguousarray(np.asarray(inputs["L0_ffn_W2"], np.float32))
    b2 = np.ascontiguousarray(np.asarray(inputs["L0_ffn_b2"], np.float32))
    x = np.asarray(inputs["L0_attn_x"], np.float32)
    maps = []
    for c in range(NCORE):
        b, h = c // 2, c % 2
        T0 = h * TPC
        xn_fm = np.ascontiguousarray(np.asarray(a1_results[c]["xn_bf"]).T)
        maps.append({
            "xn_fm": xn_fm,
            "rout": np.ascontiguousarray(a1_results[c]["rout"]),
            "x_tm": np.ascontiguousarray(x[b, T0:T0 + TPC]),
            "s0_tm": np.ascontiguousarray(a1_results[c]["s0"]),
            "W1": W1, "b1": b1, "W2": W2, "b2": b2,
        })
    res = run_bass_kernel_spmd(nc, maps, core_ids=list(range(NCORE)))
    return res.results


def emit_B1a(p, g, io, pair=False):
    NT = TPC if pair else 2 * TPC
    x1p_tm = io["x1p_tm"]
    x1p_fm = io["x1p_fm"]
    wq = io["wq"]
    wk = io["wk"]
    wv = io["wv"]
    wo = io["wo"]
    vcols = io["vcols"]
    bo_r = io["bo_r"]
    xatt_o = io["xatt_o"]
    xn1_o = io["xn1_o"]
    xn1fm_o = io.get("xn1fm_o")
    xattfm_o = io.get("xattfm_o")
    if pair:
        cc_mk_in = io["cc_mk_in"]
        cc_mk_out = io["cc_mk_out"]
        cc_C_in = io["cc_C_in"]
        cc_C_out = io["cc_C_out"]
    idb = g["idb"]
    idf = g["idf"]
    ev_idb = ev_idf = None

    NTL = NT // 128
    wQ = p.sb("wQ", [128, 8 * DM], BF16)
    wK = p.sb("wK", [128, 8 * DM], BF16)
    wV = p.sb("wV", [128, 8 * DM], BF16)
    wO = p.sb("wO", [128, 8 * DM], BF16)
    bo_s = p.sb("bo_s", [1, DM], BF16)
    xn1f_st = p.sb("xn1f_st", [128, 1024], BF16)
    bxn1f = Buf()
    xaf_st = p.sb("xaf_st", [128, 512], F32)
    bxaf = Buf()
    vc = p.sb("vc", [128, 16], F32)
    st32a = p.sb("st32a", [128, 32], F32)
    st32b = p.sb("st32b", [128, 32], F32)
    st32c = p.sb("st32c", [128, 32], F32)
    st32d = p.sb("st32d", [128, 32], F32)
    R12 = p.sb("R12", [128, 32], F32)
    A12 = p.sb("A12", [128, 32], F32)
    Rrow = p.sb("Rrow", [1, NT], BF16)
    Arow = p.sb("Arow", [1, NT], BF16)
    ones1c = p.sb("ones1c", [128, 1], F32)
    onesb = p.sb("onesb", [1, 128], BF16)
    onesf = p.sb("onesf", [1, 128], F32)
    onescol = p.sb("onescol", [128, 1], BF16)
    tmpd = p.sb("tmpd", [128, 1], F32)
    xtt = [p.sb(f"xtt{i}", [128, DM], F32) for i in range(2)]
    xft = p.sb("xft", [128, 8 * 512], F32)
    xh = p.sb("xh", [128, 8 * 512], BF16)
    xn1t = p.sb("xn1t", [128, DM], BF16)
    mk = p.sb("mk", [128, 8], F32)
    nmk = p.sb("nmk", [128, 8], F32)
    tmp8 = p.sb("tmp8", [128, 8], F32)
    mbro = p.sb("mbro", [128, DM], F32)
    ke_tm = p.sb("ke_tm", [128, 4 * DM], BF16)
    v_tm = p.sb("v_tm", [128, 4 * DM], BF16)
    Cacc = p.sb("Cacc", [128, 8 * 256], F32)
    C_sb = p.sb("C_sb", [128, 8 * 256], BF16)
    lkrow = p.sb("lkrow", [1, DM], F32)
    mrow = p.sb("mrow", [1, DM], F32)
    rlc = p.sb("rlc", [128, 8], F32)
    q_tm = p.sb("q_tm", [128, DM], BF16)
    qT = p.sb("qT", [128, 8 * 128], BF16)
    at_fm = p.sb("at_fm", [128, 8 * 128], BF16)
    mq = p.sb("mq", [128, 4], F32)
    lq = [p.sb(f"lq{i}", [128, 1], F32) for i in range(4)]
    rq = p.sb("rq", [128, 4], F32)
    x_att = p.sb("x_att", [128, 16 * DM], BF16)
    PA = p.ps("PA", [128, 512])
    PB = p.ps("PB", [128, 512])
    PC = p.ps("PC", [128, 512])
    PD = p.ps("PD", [128, 512])
    PE_ = p.ps("PE_", [128, 512])
    PF = p.ps("PF", [128, 256], BF16)
    PG = p.ps("PG", [128, 256], BF16)
    PR = p.ps("PR", [1, 512])

    bwQ, bwK, bwV, bwO = Buf(), Buf(), Buf(), Buf()
    bPA, bPB, bPC, bPD, bPE, bPF, bPG, bPR = (Buf() for _ in range(8))
    bxtt = [Buf(), Buf()]
    bxft, bxh, bxn1t = Buf(), Buf(), Buf()
    bst, brows = Buf(), Buf()
    bke, bv_, bCa, bC = Buf(), Buf(), Buf(), Buf()
    bmk, bmb, blk, bmrow, brlc = Buf(), Buf(), Buf(), Buf(), Buf()
    bq, bqT, batf = Buf(), Buf(), Buf()
    bxa = Buf()
    bxc2, bhh = Buf(), Buf()
    bx3 = [Buf(), Buf()]
    bx3f = Buf()
    bs1t = [Buf(), Buf()]
    btmpd = Buf()

    ev_vc = p.emit("y", lambda e: e.dma_start(out=vc[:, :], in_=vcols[:, :]), dma=True)
    ev_bo = p.emit("g", lambda e: e.dma_start(out=bo_s[:, :], in_=bo_r[:, :]), dma=True)
    ev_o1 = p.emit("v", lambda e: e.memset(ones1c[:, :], 1.0))
    ev_ob = p.emit("v", lambda e: e.memset(onesb[:, :], 1.0))
    ev_of = p.emit("v", lambda e: e.memset(onesf[:, :], 1.0))
    ev_oc = p.emit("v", lambda e: e.memset(onescol[:, :], 1.0))

    def load_w(tgt, btgt, src):
        evs = []
        for k in range(8):
            evs.append(p.emit("g", lambda e, k=k, tgt=tgt, src=src: e.dma_start(
                out=tgt[:, k * DM:(k + 1) * DM],
                in_=src[k * 128:(k + 1) * 128, :]),
                waits=btgt.w_waits() if k == 0 else [], dma=True))
        for ev_ in evs:
            btgt.wrote(ev_)
        return evs

    evs_wq = load_w(wQ, bwQ, wq)
    evs_wk = load_w(wK, bwK, wk)
    evs_wv = load_w(wV, bwV, wv)
    evs_wo = load_w(wO, bwO, wo)

    if pair:
        for tset in (st32a, st32b, st32c, st32d, R12, A12):
            ev0 = p.emit("v", lambda e, tset=tset: e.memset(tset[:, :], 0.0))
        bst.wrote(ev0)
    for j in range(NTL):
        buf, bb = xtt[j % 2], bxtt[j % 2]
        ev_ld = p.emit("y", lambda e, j=j, buf=buf: e.dma_start(
            out=buf[:, :], in_=x1p_tm[j * 128:(j + 1) * 128, :]),
            waits=bb.w_waits(), dma=True)
        bb.wrote(ev_ld)
        ev1 = p.emit("v", lambda e, buf=buf, j=j: e.tensor_reduce(
            st32a[:, j:j + 1], buf[:, :], axis=AX.X, op=ALU.add),
            waits=[ev_ld] + (bst.w_waits() if j == 0 else []))
        bb.read(ev1)
        ev_sq = p.emit("s", lambda e, buf=buf, j=j: e.activation(
            xft[:, 0:DM], buf[:, :], AF.Square),
            waits=[ev_ld] + bxft.w_waits())
        bb.read(ev_sq)
        bxft.wrote(ev_sq)
        ev2 = p.emit("v", lambda e, j=j: e.tensor_reduce(
            st32b[:, j:j + 1], xft[:, 0:DM], axis=AX.X, op=ALU.add),
            waits=[ev_sq])
        bxft.read(ev2)
    ev1f = p.emit("v", lambda e: e.engine_nop(),
                  waits=[(ev1[0], ev1[1], True), (ev2[0], ev2[1], True)],
                  force=True)
    pe = ev1f

    def vstep(fn, extra=()):
        nonlocal pe
        pe = p.emit("v", fn, waits=[pe] + list(extra), force=True)
        return pe

    vstep(lambda e: e.tensor_scalar(st32c[:, :], st32a[:, :], 1.0 / DM, None, ALU.mult))
    vstep(lambda e: e.tensor_tensor(st32d[:, :], st32c[:, :], st32c[:, :], ALU.mult))
    vstep(lambda e: e.scalar_tensor_tensor(st32b[:, :], st32b[:, :], 1.0 / DM,
                                           st32d[:, :], ALU.mult, ALU.subtract))
    ev_ve = vstep(lambda e: e.tensor_scalar(st32a[:, :], st32b[:, :], 1.0, 1e-5,
                                            ALU.mult, ALU.add))
    ev_sa = p.emit("s", lambda e: e.activation(R12[:, :], st32a[:, :], AF.Sqrt),
                   waits=[ev_ve])
    vstep(lambda e: e.reciprocal(R12[:, :], R12[:, :]), extra=[ev_sa])
    for _ in range(2):
        vstep(lambda e: e.tensor_tensor(st32d[:, :], R12[:, :], R12[:, :], ALU.mult))
        vstep(lambda e: e.tensor_tensor(st32d[:, :], st32d[:, :], st32a[:, :], ALU.mult))
        vstep(lambda e: e.tensor_scalar(st32d[:, :], st32d[:, :], -0.5, 1.5, ALU.mult, ALU.add))
        vstep(lambda e: e.tensor_tensor(R12[:, :], R12[:, :], st32d[:, :], ALU.mult))
    vstep(lambda e: e.tensor_scalar(st32d[:, :], st32c[:, :], -1.0, None, ALU.mult))
    ev_a1 = vstep(lambda e: e.tensor_tensor(A12[:, :], st32d[:, :], R12[:, :], ALU.mult))
    for j in range(16):
        buf, bb = xtt[j % 2], bxtt[j % 2]
        ev_ld2 = p.emit("y", lambda e, j=j, buf=buf: e.dma_start(
            out=buf[:, :], in_=x1p_tm[j * 128:(j + 1) * 128, :]),
            waits=bb.w_waits(), dma=True)
        bb.wrote(ev_ld2)
        evx = p.emit("v", lambda e, buf=buf, j=j: e.tensor_scalar(
            xn1t[:, :], buf[:, :], R12[:, j:j + 1], A12[:, j:j + 1],
            ALU.mult, ALU.add),
            waits=[ev_a1, ev_ld2] + bxn1t.w_waits())
        bb.read(evx)
        bst.read(evx)
        bxn1t.wrote(evx)
        evxo = p.emit("y", lambda e, j=j: e.dma_start(
            out=xn1_o[j * 128:(j + 1) * 128, :], in_=xn1t[:, :]),
            waits=[evx], dma=True)
        bxn1t.read(evxo)
        if xn1fm_o is not None:
            evcs = []
            for f8 in range(8):
                psT, bbT = (PF, bPF) if f8 % 2 == 0 else (PG, bPG)
                evt = p.emit("t", lambda e, f8=f8, psT=psT: e.transpose(
                    psT[:, 0:128], xn1t[:, f8 * 128:(f8 + 1) * 128], idb[:, :]),
                    waits=[evx] + bbT.w_waits())
                bbT.wrote(evt)
                bxn1t.read(evt)
                evc_ = p.emit("v", lambda e, f8=f8, psT=psT: e.tensor_copy(
                    xn1f_st[:, f8 * 128:(f8 + 1) * 128], psT[:, 0:128]),
                    waits=[evt] + (bxn1f.w_waits() if f8 == 0 else []))
                bbT.read(evc_)
                bxn1f.wrote(evc_)
                evcs.append(evc_)
            ev_fo = p.emit("y", lambda e, j=j: e.dma_start(
                out=xn1fm_o.rearrange("(k p) n -> p k n", p=128)[:, :, j * 128:(j + 1) * 128],
                in_=xn1f_st[:].rearrange("p (k n) -> p k n", n=128)),
                waits=evcs, dma=True)
            bxn1f.read(ev_fo)
    vstep(lambda e: e.tensor_tensor(st32d[:, :], R12[:, :], R12[:, :], ALU.mult))
    vstep(lambda e: e.tensor_tensor(st32d[:, :], st32d[:, :], st32b[:, :], ALU.mult))
    ev_v2 = vstep(lambda e: e.tensor_scalar(st32b[:, :], st32d[:, :], 1.0, 1e-5,
                                            ALU.mult, ALU.add))
    ev_sb2 = p.emit("s", lambda e: e.activation(st32d[:, :], st32b[:, :], AF.Sqrt),
                    waits=[ev_v2])
    vstep(lambda e: e.reciprocal(st32d[:, :], st32d[:, :]), extra=[ev_sb2])
    for _ in range(2):
        vstep(lambda e: e.tensor_tensor(st32a[:, :], st32d[:, :], st32d[:, :], ALU.mult))
        vstep(lambda e: e.tensor_tensor(st32a[:, :], st32a[:, :], st32b[:, :], ALU.mult))
        vstep(lambda e: e.tensor_scalar(st32a[:, :], st32a[:, :], -0.5, 1.5, ALU.mult, ALU.add))
        vstep(lambda e: e.tensor_tensor(st32d[:, :], st32d[:, :], st32a[:, :], ALU.mult))
    vstep(lambda e: e.tensor_tensor(A12[:, :], A12[:, :], st32d[:, :], ALU.mult))
    ev_RA = vstep(lambda e: e.tensor_tensor(R12[:, :], R12[:, :], st32d[:, :], ALU.mult))
    bst.wrote(ev_RA)

    for j in range(NTL):
        seg = slice(j * 128, (j + 1) * 128)
        evd1 = p.emit("v", lambda e, j=j: e.tensor_scalar(
            tmpd[:, :], ones1c[:, :], R12[:, j:j + 1], None, ALU.mult),
            waits=[ev_RA, ev_o1] + btmpd.w_waits())
        bst.read(evd1)
        btmpd.wrote(evd1)
        evt1 = p.emit("t", lambda e: e.transpose(
            PR[0:1, 0:128], tmpd[:, :], idf[:, :]),
            waits=[evd1, ev_idf] + bPR.w_waits())
        bPR.wrote(evt1)
        btmpd.read(evt1)
        evc1 = p.emit("v", lambda e, seg=seg: e.tensor_copy(
            Rrow[:, seg], PR[0:1, 0:128]), waits=[evt1])
        bPR.read(evc1)
        brows.wrote(evc1)
        evd2 = p.emit("v", lambda e, j=j: e.tensor_scalar(
            tmpd[:, :], ones1c[:, :], A12[:, j:j + 1], None, ALU.mult),
            waits=[evt1] + btmpd.w_waits())
        bst.read(evd2)
        btmpd.wrote(evd2)
        evt2 = p.emit("t", lambda e: e.transpose(
            PR[0:1, 128:256], tmpd[:, :], idf[:, :]),
            waits=[evd2] + bPR.w_waits())
        bPR.wrote(evt2)
        btmpd.read(evt2)
        evc2 = p.emit("v", lambda e, seg=seg: e.tensor_copy(
            Arow[:, seg], PR[0:1, 128:256]), waits=[evt2])
        bPR.read(evc2)
        brows.wrote(evc2)
    ev_rows = evc2

    SCLW = float(256 ** (-0.25))
    ev_sq_ = []
    for k in range(8):
        ev_sq_.append(p.emit("v", lambda e, k=k: e.tensor_scalar(
            wQ[:, k * DM:(k + 1) * DM], wQ[:, k * DM:(k + 1) * DM],
            SCLW, None, ALU.mult), waits=[evs_wq[k]]))
        bwQ.wrote(ev_sq_[-1])
    ev_sk_ = []
    for k in range(8):
        ev_sk_.append(p.emit("v", lambda e, k=k: e.tensor_scalar(
            wK[:, k * DM:(k + 1) * DM], wK[:, k * DM:(k + 1) * DM],
            SCLW, None, ALU.mult), waits=[evs_wk[k]]))
        bwK.wrote(ev_sk_[-1])

    def build_xh(nt):
        ev_bR = p.emit("t", lambda e, nt=nt: e.matmul(
            PA[:, :], onesb[:, :], Rrow[0:1, nt * 512:(nt + 1) * 512],
            start=True, stop=True),
            waits=[ev_rows, ev_ob] + bPA.w_waits())
        bPA.wrote(ev_bR)
        ev_bA = p.emit("t", lambda e, nt=nt: e.matmul(
            PB[:, :], onesb[:, :], Arow[0:1, nt * 512:(nt + 1) * 512],
            start=True, stop=True),
            waits=bPB.w_waits())
        bPB.wrote(ev_bA)
        ev_lx = p.emit("y", lambda e, nt=nt: e.dma_start(
            out=xft[:].rearrange("p (k n) -> p k n", n=512),
            in_=x1p_fm.rearrange("(k p) n -> p k n", p=128)[:, :, nt * 512:(nt + 1) * 512]),
            waits=bxft.w_waits(), dma=True)
        bxft.wrote(ev_lx)
        ev_h = None
        for f in range(8):
            p.emit("v", lambda e, f=f: e.tensor_tensor(
                xft[:, f * 512:(f + 1) * 512], xft[:, f * 512:(f + 1) * 512],
                PA[:, :], ALU.mult), waits=[ev_lx, ev_bR])
            p.emit("v", lambda e, f=f: e.tensor_tensor(
                xft[:, f * 512:(f + 1) * 512], xft[:, f * 512:(f + 1) * 512],
                PB[:, :], ALU.add), waits=[ev_bA])
            ev_h = p.emit("v", lambda e, f=f: e.tensor_scalar(
                xh[:, f * 512:(f + 1) * 512], xft[:, f * 512:(f + 1) * 512],
                vc[:, f:f + 1], vc[:, 8 + f:8 + f + 1], ALU.mult, ALU.add),
                waits=[ev_vc] + (bxh.w_waits() if f == 0 else []))
            bxh.wrote(ev_h)
        bPA.read(ev_h)
        bPB.read(ev_h)
        bxft.read(ev_h)
        return ev_h

    ev_mk0 = p.emit("v", lambda e: e.memset(mk[:, :], -3.0e38))
    bmk.wrote(ev_mk0)
    for nt in range(NT // 512):
        ev_h = build_xh(nt)
        for kc in range(8):
            pb, bb = (PC, bPC) if kc % 2 == 0 else (PD, bPD)
            ev_mm = None
            for f in range(8):
                ev_mm = p.emit("t", lambda e, kc=kc, f=f, pb=pb: e.matmul(
                    pb[:, :], wK[:, f * DM + kc * 128:f * DM + (kc + 1) * 128],
                    xh[:, f * 512:(f + 1) * 512],
                    start=(f == 0), stop=(f == 7)),
                    waits=(bb.w_waits() + [ev_h] + ev_sk_) if f == 0 else [],
                    sig=(f == 7))
            bb.wrote(ev_mm)
            bxh.read(ev_mm)
            ev_r = p.emit("v", lambda e, kc=kc, pb=pb: e.tensor_reduce(
                tmp8[:, kc:kc + 1], pb[:, :], axis=AX.X, op=ALU.max),
                waits=[ev_mm])
            bb.read(ev_r)
            ev_mx = p.emit("v", lambda e, kc=kc: e.tensor_tensor(
                mk[:, kc:kc + 1], mk[:, kc:kc + 1], tmp8[:, kc:kc + 1], ALU.max))
            bmk.wrote(ev_mx)
    if pair:
        ev_dmk = p.emit("y", lambda e: e.dma_start(
            out=cc_mk_in[:, :], in_=mk[:, :]),
            waits=bmk.write_list, dma=True)
        bmk.read(ev_dmk)
        ev_ccm = p.emit("g", lambda e: e.collective_compute(
            "AllGather", ALU.bypass,
            replica_groups=[[0, 1], [2, 3], [4, 5], [6, 7]],
            ins=[cc_mk_in.opt()],
            outs=[cc_mk_out.opt()]),
            waits=[ev_dmk], coll=True)
        ev_l0 = p.emit("y", lambda e: e.dma_start(
            out=tmp8[:, :], in_=cc_mk_out[0:128, :]),
            waits=[ev_ccm], dma=True)
        ev_l1 = p.emit("y", lambda e: e.dma_start(
            out=nmk[:, :], in_=cc_mk_out[128:256, :]),
            waits=[ev_ccm], dma=True)
        ev_mxp = p.emit("v", lambda e: e.tensor_tensor(
            mk[:, :], tmp8[:, :], nmk[:, :], ALU.max),
            waits=[ev_l0, ev_l1] + bmk.w_waits())
        bmk.wrote(ev_mxp)
    ev_nm = p.emit("v", lambda e: e.tensor_scalar(
        nmk[:, :], mk[:, :], -1.0, None, ALU.mult), force=True)
    bmk.read(ev_nm)
    bmk.wrote(ev_nm)
    ev_mrow = None
    for kc in range(8):
        evd = p.emit("v", lambda e, kc=kc: e.tensor_scalar(
            tmpd[:, :], ones1c[:, :], mk[:, kc:kc + 1], None, ALU.mult),
            waits=[ev_nm, ev_o1] + btmpd.w_waits())
        bmk.read(evd)
        btmpd.wrote(evd)
        evt = p.emit("t", lambda e: e.transpose(
            PR[0:1, 0:128], tmpd[:, :], idf[:, :]),
            waits=[evd, ev_idf] + bPR.w_waits())
        bPR.wrote(evt)
        btmpd.read(evt)
        ev_mrow = p.emit("v", lambda e, kc=kc: e.tensor_copy(
            mrow[:, kc * 128:(kc + 1) * 128], PR[0:1, 0:128]), waits=[evt])
        bPR.read(ev_mrow)
        bmrow.wrote(ev_mrow)
    for half in range(2):
        ev_bm = p.emit("t", lambda e, half=half: e.matmul(
            PE_[:, :], onesf[:, :], mrow[0:1, half * 512:(half + 1) * 512],
            start=True, stop=True), waits=[ev_mrow, ev_of] + bPE.w_waits(), sig=True)
        bPE.wrote(ev_bm)
        ev_mb = p.emit("v", lambda e, half=half: e.tensor_copy(
            mbro[:, half * 512:(half + 1) * 512], PE_[:, :]), waits=[ev_bm])
        bPE.read(ev_mb)
        bmb.wrote(ev_mb)
    ev_lk0 = p.emit("v", lambda e: e.memset(lkrow[:, :], 0.0))
    blk.wrote(ev_lk0)
    ev_C0 = p.emit("v", lambda e: e.memset(Cacc[:, :], 0.0))
    bCa.wrote(ev_C0)

    for nt in range(NT // 512):
        ev_h = build_xh(nt)
        for sub in range(4):
            tk = slice(sub * 128 * 8, 0)
            for half in range(2):
                pb, bb = (PC, bPC) if half == 0 else (PD, bPD)
                ev_mm = None
                for f in range(8):
                    ev_mm = p.emit("t", lambda e, sub=sub, half=half, f=f, pb=pb: e.matmul(
                        pb[:, :],
                        xh[:, f * 512 + sub * 128:f * 512 + (sub + 1) * 128],
                        wK[:, f * DM + half * 512:f * DM + (half + 1) * 512],
                        start=(f == 0), stop=(f == 7)),
                        waits=(bb.w_waits() + [ev_h] + ev_sk_) if f == 0 else [],
                        sig=(f == 7))
                bb.wrote(ev_mm)
                bxh.read(ev_mm)
                ev_s1_ = p.emit("v", lambda e, sub=sub, half=half, pb=pb: e.tensor_tensor(
                    ke_tm[:, sub * DM + half * 512:sub * DM + (half + 1) * 512],
                    pb[:, :], mbro[:, half * 512:(half + 1) * 512], ALU.subtract),
                    waits=[ev_mm, bmb.writes.get("v")] + (bke.w_waits() if sub == 0 and half == 0 else []))
                bb.read(ev_s1_)
                bke.wrote(ev_s1_)
                ev_e1 = p.emit("s", lambda e, sub=sub, half=half: e.activation(
                    ke_tm[:, sub * DM + half * 512:sub * DM + (half + 1) * 512],
                    ke_tm[:, sub * DM + half * 512:sub * DM + (half + 1) * 512],
                    AF.Exp), waits=[ev_s1_])
                bke.wrote(ev_e1)
                pb2, bb2 = (PC, bPC) if half == 1 else (PD, bPD)
                ev_mm2 = None
                for f in range(8):
                    ev_mm2 = p.emit("t", lambda e, sub=sub, half=half, f=f, pb2=pb2: e.matmul(
                        pb2[:, :],
                        xh[:, f * 512 + sub * 128:f * 512 + (sub + 1) * 128],
                        wV[:, f * DM + half * 512:f * DM + (half + 1) * 512],
                        start=(f == 0), stop=(f == 7)),
                        waits=(bb2.w_waits() + [ev_h] + evs_wv) if f == 0 else [],
                        sig=(f == 7))
                bb2.wrote(ev_mm2)
                bxh.read(ev_mm2)
                ev_v1 = p.emit("s", lambda e, sub=sub, half=half, pb2=pb2: e.activation(
                    v_tm[:, sub * DM + half * 512:sub * DM + (half + 1) * 512],
                    pb2[:, :], AF.Copy),
                    waits=[ev_mm2] + (bv_.w_waits() if sub == 0 and half == 0 else []))
                bb2.read(ev_v1)
                bv_.wrote(ev_v1)
        ev_keall = bke.write_list[-1]
        ev_vall = bv_.write_list[-1]
        for half in range(2):
            ev_mm = None
            for sub in range(4):
                ev_mm = p.emit("t", lambda e, sub=sub, half=half: e.matmul(
                    PR[0:1, 0:512], onescol[:, :],
                    ke_tm[:, sub * DM + half * 512:sub * DM + (half + 1) * 512],
                    start=(sub == 0), stop=(sub == 3)),
                    waits=(bPR.w_waits() + [ev_keall, ev_oc]) if sub == 0 else [],
                    sig=(sub == 3))
            bPR.wrote(ev_mm)
            bke.read(ev_mm)
            ev_lka = p.emit("v", lambda e, half=half: e.tensor_tensor(
                lkrow[:, half * 512:(half + 1) * 512],
                lkrow[:, half * 512:(half + 1) * 512], PR[0:1, 0:512], ALU.add),
                waits=[ev_mm])
            bPR.read(ev_lka)
            blk.wrote(ev_lka)
        for hd in range(4):
            for dc in range(2):
                pb, bb = (PC, bPC) if (hd * 2 + dc) % 2 == 0 else (PD, bPD)
                ev_mm = None
                for sub in range(4):
                    ev_mm = p.emit("t", lambda e, hd=hd, dc=dc, sub=sub, pb=pb: e.matmul(
                        pb[:, 0:256],
                        ke_tm[:, sub * DM + hd * 256 + dc * 128:sub * DM + hd * 256 + (dc + 1) * 128],
                        v_tm[:, sub * DM + hd * 256:sub * DM + (hd + 1) * 256],
                        start=(sub == 0), stop=(sub == 3)),
                        waits=(bb.w_waits() + [ev_keall, ev_vall]) if sub == 0 else [],
                        sig=(sub == 3))
                bb.wrote(ev_mm)
                bke.read(ev_mm)
                bv_.read(ev_mm)
                ev_ca = p.emit("v", lambda e, hd=hd, dc=dc, pb=pb: e.tensor_tensor(
                    Cacc[:, (hd * 2 + dc) * 256:(hd * 2 + dc + 1) * 256],
                    Cacc[:, (hd * 2 + dc) * 256:(hd * 2 + dc + 1) * 256],
                    pb[:, 0:256], ALU.add), waits=[ev_mm])
                bb.read(ev_ca)
                bCa.wrote(ev_ca)
    ev_call = bCa.write_list[-1]
    ev_lkall = blk.write_list[-1]

    if pair:
        ev_dC = p.emit("y", lambda e: e.dma_start(
            out=cc_C_in[0:128, :], in_=Cacc[:, :]),
            waits=bCa.write_list, dma=True)
        bCa.read(ev_dC)
        ev_dl = p.emit("y", lambda e: e.dma_start(
            out=cc_C_in[128:129, 0:DM], in_=lkrow[:, :]),
            waits=blk.write_list, dma=True)
        blk.read(ev_dl)
        ev_dl2 = p.emit("y", lambda e: e.dma_start(
            out=cc_C_in[128:129, DM:2 * DM], in_=lkrow[:, :]),
            waits=blk.write_list, dma=True)
        blk.read(ev_dl2)
        ev_ccC = p.emit("g", lambda e: e.collective_compute(
            "AllReduce", ALU.add,
            replica_groups=[[0, 1], [2, 3], [4, 5], [6, 7]],
            ins=[cc_C_in.opt()],
            outs=[cc_C_out.opt()]),
            waits=[ev_dC, ev_dl, ev_dl2], coll=True)
        ev_lC = p.emit("y", lambda e: e.dma_start(
            out=Cacc[:, :], in_=cc_C_out[0:128, :]),
            waits=[ev_ccC] + bCa.w_waits(), dma=True)
        bCa.wrote(ev_lC)
        ev_ll = p.emit("y", lambda e: e.dma_start(
            out=lkrow[:, :], in_=cc_C_out[128:129, 0:DM]),
            waits=[ev_ccC] + blk.w_waits(), dma=True)
        blk.wrote(ev_ll)
        ev_call = ev_lC
        ev_lkall = ev_ll

    ev_rl = p.emit("v", lambda e: e.reciprocal(lkrow[:, :], lkrow[:, :]),
                   waits=[ev_lkall], force=True)
    blk.wrote(ev_rl)
    ev_rlc = None
    for kc in range(8):
        evt = p.emit("t", lambda e, kc=kc: e.transpose(
            PE_[:, 0:1], lkrow[0:1, kc * 128:(kc + 1) * 128], idf[0:1, 0:1]),
            waits=[ev_rl, ev_idf] + bPE.w_waits())
        bPE.wrote(evt)
        blk.read(evt)
        ev_rlc = p.emit("v", lambda e, kc=kc: e.tensor_copy(
            rlc[:, kc:kc + 1], PE_[:, 0:1]), waits=[evt])
        bPE.read(ev_rlc)
        brlc.wrote(ev_rlc)
    ev_cn = None
    for hd in range(4):
        for dc in range(2):
            blkc = hd * 2 + dc
            ev_cn = p.emit("v", lambda e, blkc=blkc, hd=hd, dc=dc: e.tensor_scalar(
                C_sb[:, blkc * 256:(blkc + 1) * 256],
                Cacc[:, blkc * 256:(blkc + 1) * 256],
                rlc[:, hd * 2 + dc:hd * 2 + dc + 1], None, ALU.mult),
                waits=[ev_rlc, ev_call] + (bC.w_waits() if blkc == 0 else []))
            bCa.read(ev_cn)
            brlc.read(ev_cn)
            bC.wrote(ev_cn)
    ev_cready = ev_cn

    for nt in range(4):
        ev_h = build_xh(nt)
        for sub in range(4):
            tile = nt * 4 + sub
            for half in range(2):
                pb, bb = (PC, bPC) if half == 0 else (PD, bPD)
                ev_mm = None
                for f in range(8):
                    ev_mm = p.emit("t", lambda e, sub=sub, half=half, f=f, pb=pb: e.matmul(
                        pb[:, :],
                        xh[:, f * 512 + sub * 128:f * 512 + (sub + 1) * 128],
                        wQ[:, f * DM + half * 512:f * DM + (half + 1) * 512],
                        start=(f == 0), stop=(f == 7)),
                        waits=(bb.w_waits() + [ev_h] + ev_sq_) if f == 0 else [],
                        sig=(f == 7))
                bb.wrote(ev_mm)
                bxh.read(ev_mm)
                ev_qc = p.emit("s", lambda e, half=half, pb=pb: e.activation(
                    q_tm[:, half * 512:(half + 1) * 512], pb[:, :], AF.Copy),
                    waits=[ev_mm] + (bq.w_waits() if half == 0 else []))
                bb.read(ev_qc)
                bq.wrote(ev_qc)
            ev_qm = p.emit("v", lambda e: e.tensor_reduce(
                mq[:, :].rearrange("p (h o) -> p h o", o=1),
                q_tm[:, :].rearrange("p (h n) -> p h n", h=4),
                axis=AX.X, op=ALU.max, negate=True), waits=[bq.writes.get("s")])
            ev_ex = None
            for hd in range(4):
                ev_ex = p.emit("s", lambda e, hd=hd: e.activation(
                    q_tm[:, hd * 256:(hd + 1) * 256],
                    q_tm[:, hd * 256:(hd + 1) * 256], AF.Exp,
                    bias=mq[:, hd:hd + 1], accum_out=lq[hd][:, :]),
                    waits=[ev_qm])
                bq.wrote(ev_ex)
            ev_rq = p.emit("v", lambda e: e.reciprocal(rq[:, 0:1], lq[0][:, :]),
                           waits=[ev_ex], force=True)
            for hd in range(1, 4):
                ev_rq = p.emit("v", lambda e, hd=hd: e.reciprocal(
                    rq[:, hd:hd + 1], lq[hd][:, :]), force=True)
            ev_qs = None
            for hd in range(4):
                ev_qs = p.emit("v", lambda e, hd=hd: e.tensor_scalar(
                    q_tm[:, hd * 256:(hd + 1) * 256],
                    q_tm[:, hd * 256:(hd + 1) * 256],
                    rq[:, hd:hd + 1], None, ALU.mult), waits=[ev_rq])
                bq.wrote(ev_qs)
            for kc in range(8):
                psT, bbT = (PF, bPF) if kc % 2 == 0 else (PG, bPG)
                evt = p.emit("t", lambda e, kc=kc, psT=psT: e.transpose(
                    psT[:, 0:128], q_tm[:, kc * 128:(kc + 1) * 128], idb[:, :]),
                    waits=[ev_qs, ev_idb] + bbT.w_waits())
                bbT.wrote(evt)
                bq.read(evt)
                ev_qt = p.emit("v", lambda e, kc=kc, psT=psT: e.tensor_copy(
                    qT[:, kc * 128:(kc + 1) * 128], psT[:, 0:128]),
                    waits=[evt] + (bqT.w_waits() if kc == 0 else []))
                bbT.read(ev_qt)
                bqT.wrote(ev_qt)
            ev_qtall = bqT.write_list[-1]
            for hd in range(4):
                for ec in range(2):
                    pb, bb = (PC, bPC) if (hd * 2 + ec) % 2 == 0 else (PD, bPD)
                    ev_mm = None
                    for dc in range(2):
                        ev_mm = p.emit("t", lambda e, hd=hd, ec=ec, dc=dc, pb=pb: e.matmul(
                            pb[:, 0:128],
                            C_sb[:, (hd * 2 + dc) * 256 + ec * 128:(hd * 2 + dc) * 256 + (ec + 1) * 128],
                            qT[:, (hd * 2 + dc) * 128:(hd * 2 + dc + 1) * 128],
                            start=(dc == 0), stop=(dc == 1)),
                            waits=(bb.w_waits() + [ev_qtall, ev_cready]) if dc == 0 else [],
                            sig=(dc == 1))
                    bb.wrote(ev_mm)
                    bqT.read(ev_mm)
                    bC.read(ev_mm)
                    ev_ae = p.emit("s", lambda e, hd=hd, ec=ec, pb=pb: e.activation(
                        at_fm[:, (hd * 2 + ec) * 128:(hd * 2 + ec + 1) * 128],
                        pb[:, 0:128], AF.Copy),
                        waits=[ev_mm] + (batf.w_waits() if hd == 0 and ec == 0 else []))
                    bb.read(ev_ae)
                    batf.wrote(ev_ae)
            ev_atall = batf.write_list[-1]
            ev_lxn = p.emit("y", lambda e, tile=tile: e.dma_start(
                out=xn1t[:, :], in_=xn1_o[tile * 128:(tile + 1) * 128, :]),
                waits=bxn1t.w_waits(), dma=True)
            bxn1t.wrote(ev_lxn)
            for half in range(2):
                pb, bb = (PE_, bPE) if half == 0 else (PC, bPC)
                ev_mm = p.emit("t", lambda e, half=half, pb=pb: e.matmul(
                    pb[:, :], onesb[:, :], bo_s[:, half * 512:(half + 1) * 512],
                    start=True, stop=False),
                    waits=bb.w_waits() + [ev_atall, ev_bo, ev_ob] + evs_wo, sig=False)
                for oc in range(8):
                    ev_mm = p.emit("t", lambda e, half=half, oc=oc, pb=pb: e.matmul(
                        pb[:, :],
                        at_fm[:, oc * 128:(oc + 1) * 128],
                        wO[:, oc * DM + half * 512:oc * DM + (half + 1) * 512],
                        start=False, stop=(oc == 7)),
                        waits=[], sig=(oc == 7))
                bb.wrote(ev_mm)
                batf.read(ev_mm)
                ev_xa = p.emit("v", lambda e, half=half, pb=pb: e.tensor_tensor(
                    xft[:, half * 512:(half + 1) * 512], pb[:, :],
                    xn1t[:, half * 512:(half + 1) * 512], ALU.add),
                    waits=[ev_mm, ev_lxn] + (bxft.w_waits() if half == 0 else []))
                bb.read(ev_xa)
                bxn1t.read(ev_xa)
                bxft.wrote(ev_xa)
            ev_xout = p.emit("y", lambda e, tile=tile: e.dma_start(
                out=xatt_o[tile * 128:(tile + 1) * 128, :], in_=xft[:, 0:DM]),
                waits=[ev_xa], dma=True)
            bxft.read(ev_xout)
            if xattfm_o is not None:
                for hf in range(2):
                    pb, bb = (PA, bPA) if hf == 0 else (PB, bPB)
                    for kc in range(4):
                        f8 = hf * 4 + kc
                        evt = p.emit("t", lambda e, f8=f8, kc=kc, pb=pb: e.transpose(
                            pb[:, kc * 128:(kc + 1) * 128],
                            xft[:, f8 * 128:(f8 + 1) * 128], idf[:, :]),
                            waits=([ev_xa] + bb.w_waits()) if kc == 0 else [],
                            sig=(kc == 3))
                    bb.wrote(evt)
                    bxft.read(evt)
                    evc_ = p.emit("v", lambda e, pb=pb: e.tensor_copy(
                        xaf_st[:, :], pb[:, :]),
                        waits=[evt] + bxaf.w_waits())
                    bb.read(evc_)
                    bxaf.wrote(evc_)
                    ev_fo = p.emit("y", lambda e, tile=tile, hf=hf: e.dma_start(
                        out=xattfm_o.rearrange("(k p) n -> p k n", p=128)[
                            :, hf * 4:(hf + 1) * 4, tile * 128:(tile + 1) * 128],
                        in_=xaf_st[:].rearrange("p (k n) -> p k n", n=128)),
                        waits=[evc_], dma=True)
                    bxaf.read(ev_fo)


def build_B1a():
    nc = bass.Bass("TRN2", target_bir_lowering=False)
    NT = 2 * TPC
    io = {
        "x1p_tm": nc.dram_tensor("x1p_tm", [NT, DM], F32, kind="ExternalInput").ap(),
        "x1p_fm": nc.dram_tensor("x1p_fm", [DM, NT], F32, kind="ExternalInput").ap(),
        "wq": nc.dram_tensor("wq", [DM, DM], F32, kind="ExternalInput").ap(),
        "wk": nc.dram_tensor("wk", [DM, DM], F32, kind="ExternalInput").ap(),
        "wv": nc.dram_tensor("wv", [DM, DM], F32, kind="ExternalInput").ap(),
        "wo": nc.dram_tensor("wo", [DM, DM], F32, kind="ExternalInput").ap(),
        "vcols": nc.dram_tensor("vcols", [128, 16], F32, kind="ExternalInput").ap(),
        "bo_r": nc.dram_tensor("bo_r", [1, DM], F32, kind="ExternalInput").ap(),
        "xatt_o": nc.dram_tensor("xatt", [TPC, DM], F32, kind="ExternalOutput").ap(),
        "xn1_o": nc.dram_tensor("xn1_bf", [TPC, DM], BF16, kind="ExternalOutput").ap(),
    }
    p = Pgm(nc)
    g = mk_glob(p, nc)
    p.fence()
    emit_B1a(p, g, io, pair=False)
    finish_all(p)
    return nc


def _vcols2(v1, v2):
    out = np.zeros((128, 16), np.float32)
    for k in range(8):
        out[:, k] = v1[k * 128:(k + 1) * 128]
        out[:, 8 + k] = v2[k * 128:(k + 1) * 128]
    return out


def run_B1a(inputs, x1_by_core):
    nc = build_B1a()
    maps = []
    wqv = np.ascontiguousarray(np.asarray(inputs["L1_attn_wq"], np.float32))
    wkv = np.ascontiguousarray(np.asarray(inputs["L1_attn_wk"], np.float32))
    wvv = np.ascontiguousarray(np.asarray(inputs["L1_attn_wv"], np.float32))
    wov = np.ascontiguousarray(np.asarray(inputs["L1_attn_wo"], np.float32))
    vc = _vcols2(np.asarray(inputs["L1_attn_ln1_g"], np.float32),
                 np.asarray(inputs["L1_attn_ln1_b"], np.float32))
    bo = np.ascontiguousarray(np.asarray(inputs["L1_attn_bo"], np.float32).reshape(1, DM))
    for c in range(NCORE):
        b, h = c // 2, c % 2
        own = x1_by_core[c]
        other = x1_by_core[c + 1 - 2 * h]
        x1p = np.ascontiguousarray(np.concatenate([own, other], axis=0))
        maps.append({
            "x1p_tm": x1p, "x1p_fm": np.ascontiguousarray(x1p.T),
            "wq": wqv, "wk": wkv, "wv": wvv, "wo": wov,
            "vcols": vc, "bo_r": bo,
        })
    res = run_bass_kernel_spmd(nc, maps, core_ids=list(range(NCORE)))
    return res.results


def emit_B1b(p, g, io):
    TW = 256
    NTL = TPC // 128
    NNT = TPC // TW
    xatt_tm = io["xatt_tm"]
    xatt_fm = io["xatt_fm"]
    wff1 = io["wff1"]
    wff2 = io["wff2"]
    w_d = io["w_d"]
    vcols = io["vcols"]
    bf1c = io["bf1c"]
    bff2_r = io["bff2_r"]
    bd_r = io["bd_r"]
    s1_o = io["s1_o"]
    idb = g["idb"]
    idf = g["idf"]
    ev_idb = ev_idf = None

    wF1 = p.sb("wF1", [128, 8 * 4 * DM], BF16)
    wF2 = p.sb("wF2", [128, 32 * DM], BF16)
    wD = p.sb("wD", [128, 8 * DM], BF16)
    vc = p.sb("vc", [128, 16], F32)
    cb1 = p.sb("cb1", [128, 32], F32)
    bff2_s = p.sb("bff2_s", [1, DM], BF16)
    bd_s = p.sb("bd_s", [1, DM], BF16)
    st16a = p.sb("st16a", [128, 16], F32)
    st16b = p.sb("st16b", [128, 16], F32)
    st16c = p.sb("st16c", [128, 16], F32)
    st16d = p.sb("st16d", [128, 16], F32)
    R2 = p.sb("R2", [128, 16], F32)
    A2 = p.sb("A2", [128, 16], F32)
    Rrow = p.sb("Rrow", [1, TPC], BF16)
    Arow = p.sb("Arow", [1, TPC], BF16)
    ones1c = p.sb("ones1c", [128, 1], F32)
    onesb = p.sb("onesb", [1, 128], BF16)
    tmpd = p.sb("tmpd", [128, 1], F32)
    xtt = [p.sb(f"xtt{i}", [128, DM], F32) for i in range(2)]
    xft = p.sb("xft", [128, 8 * TW], F32)
    xc2 = p.sb("xc2", [128, 8 * TW], BF16)
    hh = p.sb("hh", [128, 32 * TW], BF16)
    x3t = p.sb("x3t", [128, DM], F32)
    x3f = p.sb("x3f", [128, 8 * 128], BF16)
    s1t = p.sb("s1t", [128, DM], F32)
    PA = p.ps("PA", [128, 512])
    PB = p.ps("PB", [128, 512])
    PC = p.ps("PC", [128, 512])
    PD = p.ps("PD", [128, 512])
    PF = p.ps("PF", [128, 128], BF16)
    PG = p.ps("PG", [128, 128], BF16)
    PR = p.ps("PR", [1, 512])

    bwF1, bwF2, bwD = Buf(), Buf(), Buf()
    bPA, bPB, bPC, bPD, bPF, bPG, bPR = (Buf() for _ in range(7))
    bxtt = [Buf(), Buf()]
    bxft, bxc2, bhh = Buf(), Buf(), Buf()
    bst, brows, btmpd = Buf(), Buf(), Buf()
    bx3, bx3f, bs1t = Buf(), Buf(), Buf()

    ev_vc = p.emit("y", lambda e: e.dma_start(out=vc[:, :], in_=vcols[:, :]), dma=True)
    ev_cb1 = p.emit("y", lambda e: e.dma_start(out=cb1[:, :], in_=bf1c[:, :]), dma=True)
    ev_b2 = p.emit("g", lambda e: e.dma_start(out=bff2_s[:, :], in_=bff2_r[:, :]), dma=True)
    ev_bd = p.emit("g", lambda e: e.dma_start(out=bd_s[:, :], in_=bd_r[:, :]), dma=True)
    ev_o1 = p.emit("v", lambda e: e.memset(ones1c[:, :], 1.0))
    ev_ob = p.emit("v", lambda e: e.memset(onesb[:, :], 1.0))
    evs_f1 = []
    for k in range(8):
        evs_f1.append(p.emit("g", lambda e, k=k: e.dma_start(
            out=wF1[:, k * 4 * DM:(k + 1) * 4 * DM],
            in_=wff1[k * 128:(k + 1) * 128, :]), dma=True))
    evs_f2 = []
    for q in range(8):
        evs_f2.append(p.emit("g", lambda e, q=q: e.dma_start(
            out=wF2[:, q * 4096:(q + 1) * 4096].rearrange("p (c n) -> p c n", n=DM),
            in_=wff2.rearrange("(c p) n -> p c n", p=128)[:, q * 4:(q + 1) * 4, :]),
            dma=True))
    evs_wd = []
    for k in range(8):
        evs_wd.append(p.emit("g", lambda e, k=k: e.dma_start(
            out=wD[:, k * DM:(k + 1) * DM],
            in_=w_d[k * 128:(k + 1) * 128, :]), dma=True))

    for j in range(NTL):
        buf, bb = xtt[j % 2], bxtt[j % 2]
        ev_ld = p.emit("y", lambda e, j=j, buf=buf: e.dma_start(
            out=buf[:, :], in_=xatt_tm[j * 128:(j + 1) * 128, :]),
            waits=bb.w_waits(), dma=True)
        bb.wrote(ev_ld)
        ev1 = p.emit("v", lambda e, buf=buf, j=j: e.tensor_reduce(
            st16a[:, j:j + 1], buf[:, :], axis=AX.X, op=ALU.add),
            waits=[ev_ld])
        bb.read(ev1)
        ev_sq = p.emit("s", lambda e, buf=buf: e.activation(
            xft[:, 0:DM], buf[:, :], AF.Square), waits=[ev_ld] + bxft.w_waits())
        bb.read(ev_sq)
        bxft.wrote(ev_sq)
        ev2 = p.emit("v", lambda e, j=j: e.tensor_reduce(
            st16b[:, j:j + 1], xft[:, 0:DM], axis=AX.X, op=ALU.add), waits=[ev_sq])
        bxft.read(ev2)
    ev1f = p.emit("v", lambda e: e.engine_nop(),
                  waits=[(ev1[0], ev1[1], True), (ev2[0], ev2[1], True)], force=True)
    pe = ev1f

    def vstep(fn, extra=()):
        nonlocal pe
        pe = p.emit("v", fn, waits=[pe] + list(extra), force=True)
        return pe

    vstep(lambda e: e.tensor_scalar(st16c[:, :], st16a[:, :], 1.0 / DM, None, ALU.mult))
    vstep(lambda e: e.tensor_tensor(st16d[:, :], st16c[:, :], st16c[:, :], ALU.mult))
    vstep(lambda e: e.scalar_tensor_tensor(st16b[:, :], st16b[:, :], 1.0 / DM,
                                           st16d[:, :], ALU.mult, ALU.subtract))
    ev_ve = vstep(lambda e: e.tensor_scalar(st16a[:, :], st16b[:, :], 1.0, 1e-5,
                                            ALU.mult, ALU.add))
    ev_sa = p.emit("s", lambda e: e.activation(R2[:, :], st16a[:, :], AF.Sqrt),
                   waits=[ev_ve])
    vstep(lambda e: e.reciprocal(R2[:, :], R2[:, :]), extra=[ev_sa])
    for _ in range(2):
        vstep(lambda e: e.tensor_tensor(st16d[:, :], R2[:, :], R2[:, :], ALU.mult))
        vstep(lambda e: e.tensor_tensor(st16d[:, :], st16d[:, :], st16a[:, :], ALU.mult))
        vstep(lambda e: e.tensor_scalar(st16d[:, :], st16d[:, :], -0.5, 1.5, ALU.mult, ALU.add))
        vstep(lambda e: e.tensor_tensor(R2[:, :], R2[:, :], st16d[:, :], ALU.mult))
    vstep(lambda e: e.tensor_scalar(st16d[:, :], st16c[:, :], -1.0, None, ALU.mult))
    ev_RA = vstep(lambda e: e.tensor_tensor(A2[:, :], st16d[:, :], R2[:, :], ALU.mult))
    bst.wrote(ev_RA)
    for j in range(NTL):
        seg = slice(j * 128, (j + 1) * 128)
        evd1 = p.emit("v", lambda e, j=j: e.tensor_scalar(
            tmpd[:, :], ones1c[:, :], R2[:, j:j + 1], None, ALU.mult),
            waits=[ev_RA, ev_o1] + btmpd.w_waits())
        bst.read(evd1)
        btmpd.wrote(evd1)
        evt1 = p.emit("t", lambda e: e.transpose(
            PR[0:1, 0:128], tmpd[:, :], idf[:, :]),
            waits=[evd1, ev_idf] + bPR.w_waits())
        bPR.wrote(evt1)
        btmpd.read(evt1)
        evc1 = p.emit("v", lambda e, seg=seg: e.tensor_copy(
            Rrow[:, seg], PR[0:1, 0:128]), waits=[evt1])
        bPR.read(evc1)
        brows.wrote(evc1)
        evd2 = p.emit("v", lambda e, j=j: e.tensor_scalar(
            tmpd[:, :], ones1c[:, :], A2[:, j:j + 1], None, ALU.mult),
            waits=[evt1] + btmpd.w_waits())
        bst.read(evd2)
        btmpd.wrote(evd2)
        evt2 = p.emit("t", lambda e: e.transpose(
            PR[0:1, 128:256], tmpd[:, :], idf[:, :]),
            waits=[evd2] + bPR.w_waits())
        bPR.wrote(evt2)
        btmpd.read(evt2)
        evc2 = p.emit("v", lambda e, seg=seg: e.tensor_copy(
            Arow[:, seg], PR[0:1, 128:256]), waits=[evt2])
        bPR.read(evc2)
        brows.wrote(evc2)
    ev_rows = evc2

    for nt in range(NNT):
        ev_bR = p.emit("t", lambda e, nt=nt: e.matmul(
            PA[:, 0:TW], onesb[:, :], Rrow[0:1, nt * TW:(nt + 1) * TW],
            start=True, stop=True), waits=[ev_rows, ev_ob] + bPA.w_waits())
        bPA.wrote(ev_bR)
        ev_bA = p.emit("t", lambda e, nt=nt: e.matmul(
            PB[:, 0:TW], onesb[:, :], Arow[0:1, nt * TW:(nt + 1) * TW],
            start=True, stop=True), waits=bPB.w_waits())
        bPB.wrote(ev_bA)
        ev_lx = p.emit("y", lambda e, nt=nt: e.dma_start(
            out=xft[:].rearrange("p (k n) -> p k n", n=TW),
            in_=xatt_fm.rearrange("(k p) n -> p k n", p=128)[:, :, nt * TW:(nt + 1) * TW]),
            waits=bxft.w_waits(), dma=True)
        bxft.wrote(ev_lx)
        ev_h = None
        for f in range(8):
            p.emit("v", lambda e, f=f: e.tensor_tensor(
                xft[:, f * TW:(f + 1) * TW], xft[:, f * TW:(f + 1) * TW],
                PA[:, 0:TW], ALU.mult), waits=[ev_lx, ev_bR])
            p.emit("v", lambda e, f=f: e.tensor_tensor(
                xft[:, f * TW:(f + 1) * TW], xft[:, f * TW:(f + 1) * TW],
                PB[:, 0:TW], ALU.add), waits=[ev_bA])
            ev_h = p.emit("v", lambda e, f=f: e.tensor_scalar(
                xc2[:, f * TW:(f + 1) * TW], xft[:, f * TW:(f + 1) * TW],
                vc[:, f:f + 1], vc[:, 8 + f:8 + f + 1], ALU.mult, ALU.add),
                waits=[ev_vc] + (bxc2.w_waits() if f == 0 else []))
            bxc2.wrote(ev_h)
        bPA.read(ev_h)
        bPB.read(ev_h)
        bxft.read(ev_h)
        for c in range(32):
            pb, bb = (PC, bPC) if c % 2 == 0 else (PD, bPD)
            ev_mm = None
            for f in range(8):
                ev_mm = p.emit("t", lambda e, c=c, f=f, pb=pb: e.matmul(
                    pb[:, 0:TW],
                    wF1[:, f * 4096 + c * 128:f * 4096 + (c + 1) * 128],
                    xc2[:, f * TW:(f + 1) * TW],
                    start=(f == 0), stop=(f == 7)),
                    waits=(bb.w_waits() + [ev_h] + evs_f1) if f == 0 else [],
                    sig=(f == 7))
            bb.wrote(ev_mm)
            bxc2.read(ev_mm)
            ev_g = p.emit("s", lambda e, c=c, pb=pb: e.activation(
                hh[:, c * TW:(c + 1) * TW], pb[:, 0:TW], AF.Gelu_apprx_tanh,
                bias=cb1[:, c:c + 1]),
                waits=[ev_mm, ev_cb1] + (bhh.w_waits() if c == 0 else []))
            bb.read(ev_g)
            bhh.wrote(ev_g)
        ev_hh = bhh.write_list[-1]
        for sub in range(TW // 128):
            tile = nt * (TW // 128) + sub
            buf, bb2 = xtt[sub % 2], bxtt[sub % 2]
            ev_lr = p.emit("y", lambda e, tile=tile, buf=buf: e.dma_start(
                out=buf[:, :], in_=xatt_tm[tile * 128:(tile + 1) * 128, :]),
                waits=bb2.w_waits(), dma=True)
            bb2.wrote(ev_lr)
            for half in range(2):
                pb, bb = (PA, bPA) if half == 0 else (PB, bPB)
                ev_mm = p.emit("t", lambda e, half=half, pb=pb: e.matmul(
                    pb[:, :], onesb[:, :], bff2_s[:, half * 512:(half + 1) * 512],
                    start=True, stop=False),
                    waits=bb.w_waits() + [ev_hh, ev_b2, ev_ob] + evs_f2, sig=False)
                for c in range(32):
                    ev_mm = p.emit("t", lambda e, c=c, sub=sub, half=half, pb=pb: e.matmul(
                        pb[:, :],
                        hh[:, c * TW + sub * 128:c * TW + sub * 128 + 128],
                        wF2[:, c * DM + half * 512:c * DM + (half + 1) * 512],
                        start=False, stop=(c == 31)),
                        waits=[], sig=(c == 31))
                bb.wrote(ev_mm)
                bhh.read(ev_mm)
                ev_x3 = p.emit("v", lambda e, half=half, pb=pb, buf=buf: e.tensor_tensor(
                    x3t[:, half * 512:(half + 1) * 512], pb[:, :],
                    buf[:, half * 512:(half + 1) * 512], ALU.add),
                    waits=[ev_mm, ev_lr] + (bx3.w_waits() if half == 0 else []))
                bb.read(ev_x3)
                bb2.read(ev_x3)
                bx3.wrote(ev_x3)
            ev_c3 = p.emit("v", lambda e: e.tensor_copy(
                x3f[:, 0:0 + DM] if False else xc2[:, 0:DM], x3t[:, :]),
                waits=[ev_x3] + bxc2.w_waits(), sig=True)
            bx3.read(ev_c3)
            bxc2.wrote(ev_c3)
            for kc in range(8):
                psT, bbT = (PF, bPF) if kc % 2 == 0 else (PG, bPG)
                evt = p.emit("t", lambda e, kc=kc, psT=psT: e.transpose(
                    psT[:, :], xc2[:, kc * 128:(kc + 1) * 128], idb[:, :]),
                    waits=[ev_c3, ev_idb] + bbT.w_waits())
                bbT.wrote(evt)
                bxc2.read(evt)
                ev_xf = p.emit("v", lambda e, kc=kc, psT=psT: e.tensor_copy(
                    x3f[:, kc * 128:(kc + 1) * 128], psT[:, :]),
                    waits=[evt] + (bx3f.w_waits() if kc == 0 else []))
                bbT.read(ev_xf)
                bx3f.wrote(ev_xf)
            ev_x3f = bx3f.write_list[-1]
            for half in range(2):
                pb, bb = (PC, bPC) if half == 0 else (PD, bPD)
                ev_mm = p.emit("t", lambda e, half=half, pb=pb: e.matmul(
                    pb[:, :], onesb[:, :], bd_s[:, half * 512:(half + 1) * 512],
                    start=True, stop=False),
                    waits=bb.w_waits() + [ev_x3f, ev_bd] + evs_wd, sig=False)
                for kc in range(8):
                    ev_mm = p.emit("t", lambda e, kc=kc, half=half, pb=pb: e.matmul(
                        pb[:, :],
                        x3f[:, kc * 128:(kc + 1) * 128],
                        wD[:, kc * DM + half * 512:kc * DM + (half + 1) * 512],
                        start=False, stop=(kc == 7)),
                        waits=[], sig=(kc == 7))
                bb.wrote(ev_mm)
                bx3f.read(ev_mm)
                ev_s1 = p.emit("v", lambda e, half=half, pb=pb: e.tensor_copy(
                    s1t[:, half * 512:(half + 1) * 512], pb[:, :]),
                    waits=[ev_mm] + (bs1t.w_waits() if half == 0 else []))
                bb.read(ev_s1)
                bs1t.wrote(ev_s1)
            ev_so = p.emit("y", lambda e, tile=tile: e.dma_start(
                out=s1_o[tile * 128:(tile + 1) * 128, :], in_=s1t[:, :]),
                waits=[ev_s1], dma=True)
            bs1t.read(ev_so)


def build_B1b():
    nc = bass.Bass("TRN2", target_bir_lowering=False)
    io = {
        "xatt_tm": nc.dram_tensor("xatt_tm", [TPC, DM], F32, kind="ExternalInput").ap(),
        "xatt_fm": nc.dram_tensor("xatt_fm", [DM, TPC], F32, kind="ExternalInput").ap(),
        "wff1": nc.dram_tensor("wff1", [DM, 4 * DM], F32, kind="ExternalInput").ap(),
        "wff2": nc.dram_tensor("wff2", [4 * DM, DM], F32, kind="ExternalInput").ap(),
        "w_d": nc.dram_tensor("w_d", [DM, DM], F32, kind="ExternalInput").ap(),
        "vcols": nc.dram_tensor("vcols", [128, 16], F32, kind="ExternalInput").ap(),
        "bf1c": nc.dram_tensor("bf1c", [128, 32], F32, kind="ExternalInput").ap(),
        "bff2_r": nc.dram_tensor("bff2_r", [1, DM], F32, kind="ExternalInput").ap(),
        "bd_r": nc.dram_tensor("bd_r", [1, DM], F32, kind="ExternalInput").ap(),
        "s1_o": nc.dram_tensor("s1", [TPC, DM], F32, kind="ExternalOutput").ap(),
    }
    p = Pgm(nc)
    g = mk_glob(p, nc)
    p.fence()
    emit_B1b(p, g, io)
    finish_all(p)
    return nc


def run_B1b(inputs, xatt_by_core):
    nc = build_B1b()
    wff1 = np.ascontiguousarray(np.asarray(inputs["L1_attn_w_ff1"], np.float32))
    wff2 = np.ascontiguousarray(np.asarray(inputs["L1_attn_w_ff2"], np.float32))
    wd = np.ascontiguousarray(np.asarray(inputs["L1_attn_w_d"], np.float32))
    vc = _vcols2(np.asarray(inputs["L1_attn_ln2_g"], np.float32),
                 np.asarray(inputs["L1_attn_ln2_b"], np.float32))
    bf1 = np.asarray(inputs["L1_attn_b_ff1"], np.float32)
    bf1c = np.zeros((128, 32), np.float32)
    for c in range(32):
        bf1c[:, c] = bf1[c * 128:(c + 1) * 128]
    bff2 = np.ascontiguousarray(np.asarray(inputs["L1_attn_b_ff2"], np.float32).reshape(1, DM))
    bd = np.ascontiguousarray(np.asarray(inputs["L1_attn_b_d"], np.float32).reshape(1, DM))
    maps = []
    for c in range(NCORE):
        xa = np.ascontiguousarray(xatt_by_core[c])
        maps.append({
            "xatt_tm": xa, "xatt_fm": np.ascontiguousarray(xa.T),
            "wff1": wff1, "wff2": wff2, "w_d": wd,
            "vcols": vc, "bf1c": bf1c, "bff2_r": bff2, "bd_r": bd,
        })
    res = run_bass_kernel_spmd(nc, maps, core_ids=list(range(NCORE)))
    return res.results


def _rot_matrix():
    R = np.zeros((128, 128), np.float32)
    for b0 in (0, 64):
        for i in range(32):
            R[b0 + i, b0 + 32 + i] = -1.0
            R[b0 + 32 + i, b0 + i] = 1.0
    return np.ascontiguousarray(R.T).astype(BF16NP)


def emit_Bf_attn(p, g, io):
    EXT = 2561
    KE = 2560
    IN = 512
    y_fm = io.get("y_fm")
    y_halo = io.get("y_halo")
    y_own = io.get("y_own")
    hz = io.get("hz")
    y_tm = io["y_tm"]
    y_tm_cast = io.get("y_tm_cast", False)
    wqkv = io["wqkv"]
    wo = io["wo"]
    qs_r = io["qs_r"]
    ks_r = io["ks_r"]
    tbls = io["tbls"]
    masks = io["masks"]
    xp_o = io["xp_o"]
    xpfm_o = io.get("xpfm_o")
    idb = g["idb"]
    idf = g["idf"]
    rotm = g["rotm"]
    ev_idb = ev_rot = None

    yb = p.sb("yb", [128, 8 * EXT], BF16)
    wQKV = p.sb("wQKV", [128, 8 * 3 * IN], BF16)
    wO = p.sb("wO", [128, 4 * DM], BF16)
    qsr = p.sb("qsr", [1, 64], F32)
    ksr = p.sb("ksr", [1, 64], F32)
    tb = p.sb("tb", [128, 4 * EXT], BF16)
    mk_sb = p.sb("mk_sb", [128, 5 * 640], BF16)
    q_sb = p.sb("q_sb", [128, 4 * TPC], BF16)
    k_sb = p.sb("k_sb", [128, 4 * KE], BF16)
    v_sb = p.sb("v_sb", [128, 20 * IN], BF16)
    o_sb = p.sb("o_sb", [128, 4 * TPC], BF16)
    sqs = [p.sb(f"sqs{i}", [128, 512], F32) for i in range(4)]
    ssr = [p.sb(f"ssr{i}", [1, 512], F32) for i in range(2)]
    eps1 = p.sb("eps1", [1, 1], F32)
    onescol = p.sb("onescol", [128, 1], F32)
    scl = [p.sb(f"scl{i}", [128, 512], F32) for i in range(2)]
    t1b = [p.sb(f"t1b{i}", [128, 512], BF16) for i in range(2)]
    sm2 = [p.sb(f"sm2_{i}", [128, 640], BF16) for i in range(3)]
    ptb = [p.sb(f"ptb{i}", [128, 640], BF16) for i in range(2)]
    m1 = [p.sb(f"m1_{i}", [128, 1], F32) for i in range(3)]
    ls1 = [p.sb(f"ls1_{i}", [128, 1], F32) for i in range(3)]
    rl2 = [p.sb(f"rl2_{i}", [128, 1], F32) for i in range(3)]
    yt = [p.sb("yt0", [128, DM], F32)] * 2
    xpt = p.sb("xpt", [128, DM], F32)
    xpf_st = p.sb("xpf_st", [128, 512], F32)
    bxpf = Buf()
    PA = p.ps("PA", [128, 512])
    PB = p.ps("PB", [128, 512])
    PC = p.ps("PC", [128, 512])
    PD = p.ps("PD", [128, 512])
    PT0 = p.ps("PT0", [128, 640], BF16)
    PT1 = p.ps("PT1", [128, 640], BF16)
    PO0 = p.ps("PO0", [64, 128])
    PO1 = p.ps("PO1", [64, 128])

    bPA, bPB, bPC, bPD = Buf(), Buf(), Buf(), Buf()
    bPT = [Buf(), Buf()]
    bPO = [Buf(), Buf()]
    byb, bq, bk, bv, bo = Buf(), Buf(), Buf(), Buf(), Buf()
    bsq = [Buf(), Buf(), Buf(), Buf()]
    bssr = [Buf(), Buf()]
    bscl = [Buf(), Buf()]
    bt1 = [Buf(), Buf()]
    bsm = [Buf(), Buf(), Buf()]
    bpt = [Buf(), Buf()]
    _byt = Buf()
    byt = [_byt] * 2
    bxpt = Buf()

    ev_qsr = p.emit("y", lambda e: e.dma_start(out=qsr[:, :], in_=qs_r[:, :]), dma=True)
    ev_ksr = p.emit("y", lambda e: e.dma_start(out=ksr[:, :], in_=ks_r[:, :]), dma=True)
    ev_tb = p.emit("g", lambda e: e.dma_start(out=tb[:, :], in_=tbls[:, :]), dma=True)
    ev_mk = p.emit("g", lambda e: e.dma_start(out=mk_sb[:, :], in_=masks[:, :]), dma=True)
    ev_e1 = p.emit("v", lambda e: e.memset(eps1[:, :], 1e-24))
    ev_oc = p.emit("v", lambda e: e.memset(onescol[:, :], 1.0))
    evs_y = []
    if y_fm is not None:
        for k in range(8):
            evs_y.append(p.emit("g", lambda e, k=k: e.dma_start(
                out=yb[:, k * EXT:(k + 1) * EXT],
                in_=y_fm[k * 128:(k + 1) * 128, :]), dma=True))
    else:
        hzb = p.sb("hzb", [128, 1], F32)
        ev_hz = p.emit("y", lambda e: e.dma_start(out=hzb[:, :], in_=hz[:, :]),
                       dma=True)
        for k in range(8):
            ev_h_ = p.emit("g", lambda e, k=k: e.dma_start(
                out=yb[:, k * EXT:k * EXT + 513],
                in_=y_halo[k * 128:(k + 1) * 128, :]), dma=True)
            ev_hm = p.emit("v", lambda e, k=k: e.tensor_scalar(
                yb[:, k * EXT:k * EXT + 513], yb[:, k * EXT:k * EXT + 513],
                hzb[:, 0:1], None, ALU.mult), waits=[ev_h_, ev_hz])
            evs_y.append(ev_hm)
            evs_y.append(p.emit("g", lambda e, k=k: e.dma_start(
                out=yb[:, k * EXT + 513:(k + 1) * EXT],
                in_=y_own[k * 128:(k + 1) * 128, :]), dma=True))
    evs_w = []
    for k in range(8):
        evs_w.append(p.emit("g", lambda e, k=k: e.dma_start(
            out=wQKV[:, k * 3 * IN:(k + 1) * 3 * IN],
            in_=wqkv[k * 128:(k + 1) * 128, :]), dma=True))
    evs_wo = []
    for k in range(4):
        evs_wo.append(p.emit("g", lambda e, k=k: e.dma_start(
            out=wO[:, k * DM:(k + 1) * DM],
            in_=wo[k * 128:(k + 1) * 128, :]), dma=True))

    def ysl(f, base, width):
        off = base - (1 if f >= 4 else 0)
        return yb[:, f * EXT + off:f * EXT + off + width]

    _base = p.prefix
    p.prefix = _base + "prj."
    for nt in range(5):
        base = 1 + nt * 512
        for kc in range(4):
            pb, bb = (PA, bPA) if kc % 2 == 0 else (PB, bPB)
            ev_mm = None
            for f in range(8):
                ev_mm = p.emit("t", lambda e, nt=nt, kc=kc, f=f, pb=pb, base=base: e.matmul(
                    pb[:, :], wQKV[:, f * 3 * IN + IN + kc * 128:f * 3 * IN + IN + (kc + 1) * 128],
                    ysl(f, base, 512), start=(f == 0), stop=(f == 7)),
                    waits=(bb.w_waits() + evs_y + evs_w) if f == 0 else [],
                    sig=(f == 7))
            bb.wrote(ev_mm)
            byb.read(ev_mm)
            ev_ev = p.emit("s", lambda e, nt=nt, kc=kc, pb=pb: e.activation(
                k_sb[:, kc * KE + nt * 512:kc * KE + (nt + 1) * 512], pb[:, :],
                AF.Copy), waits=[ev_mm] + (bk.w_waits() if nt == 0 and kc == 0 else []))
            bb.read(ev_ev)
            bk.wrote(ev_ev)
        if nt >= 1:
            for qc in range(4):
                pb, bb = (PC, bPC) if qc % 2 == 0 else (PD, bPD)
                ev_mm = None
                for f in range(8):
                    ev_mm = p.emit("t", lambda e, nt=nt, qc=qc, f=f, pb=pb, base=base: e.matmul(
                        pb[:, :], wQKV[:, f * 3 * IN + qc * 128:f * 3 * IN + (qc + 1) * 128],
                        ysl(f, base, 512), start=(f == 0), stop=(f == 7)),
                        waits=(bb.w_waits() + evs_y + evs_w) if f == 0 else [],
                        sig=(f == 7))
                bb.wrote(ev_mm)
                byb.read(ev_mm)
                ev_ev = p.emit("s", lambda e, nt=nt, qc=qc, pb=pb: e.activation(
                    q_sb[:, qc * TPC + (nt - 1) * 512:qc * TPC + nt * 512], pb[:, :],
                    AF.Copy), waits=[ev_mm] + (bq.w_waits() if nt == 1 and qc == 0 else []))
                bb.read(ev_ev)
                bq.wrote(ev_ev)
        for sub in range(4):
            s_ = nt * 4 + sub
            pb, bb = (PC, bPC) if sub % 2 == 0 else (PD, bPD)
            ev_mm = None
            for f in range(8):
                ev_mm = p.emit("t", lambda e, s_=s_, f=f, pb=pb, base=base, sub=sub: e.matmul(
                    pb[:, :],
                    ysl(f, base + sub * 128, 128),
                    wQKV[:, f * 3 * IN + 2 * IN:f * 3 * IN + 3 * IN],
                    start=(f == 0), stop=(f == 7)),
                    waits=(bb.w_waits() + evs_y + evs_w) if f == 0 else [],
                    sig=(f == 7))
            bb.wrote(ev_mm)
            byb.read(ev_mm)
            ev_ev = p.emit("s", lambda e, s_=s_, pb=pb: e.activation(
                v_sb[:, s_ * IN:(s_ + 1) * IN], pb[:, :], AF.Copy),
                waits=[ev_mm] + (bv.w_waits() if s_ == 0 else []))
            bb.read(ev_ev)
            bv.wrote(ev_ev)
    ev_kall = bk.write_list[-1]
    ev_qall = bq.write_list[-1]
    ev_vall = bv.write_list[-1]

    p.prefix = _base + "rot."
    def l2rot(tag, sb_, bsb, nseg, seglen, qs_ap, ev_qs_, cos_off, sin_off, tab_base):
        ev_seed = [bsb.writes.get("s"), bsb.writes.get("v")]
        n_it = nseg * 4
        ST = {}

        def sl_of(it):
            nt, ch = it // 4, it % 4
            return (nt, sb_[:, ch * seglen + nt * 512:ch * seglen + nt * 512 + 512])

        def rS1(it):
            nt, sl_ = sl_of(it)
            j4 = it % 4
            st = ST[it] = {}
            ev_s2 = p.emit("s", lambda e, sl_=sl_, j4=j4: e.activation(
                sqs[j4][:, :], sl_, AF.Square),
                waits=bsq[j4].w_waits() + ev_seed)
            bsq[j4].wrote(ev_s2)
            st["sq"] = ev_s2

        def rS2(it):
            j4, j2 = it % 4, it % 2
            st = ST[it]
            pb, bb = (PA, bPA) if j2 == 0 else (PB, bPB)
            ev_cp = None
            for hh_ in range(2):
                ev_mm = p.emit("t", lambda e, hh_=hh_, pb=pb, j4=j4: e.matmul(
                    pb[:, :][0:1, :],
                    onescol[hh_ * 64:(hh_ + 1) * 64, :],
                    sqs[j4][hh_ * 64:(hh_ + 1) * 64, :],
                    start=True, stop=True),
                    waits=[st["sq"], ev_oc] + bb.w_waits())
                bb.wrote(ev_mm)
                bsq[j4].read(ev_mm)
                ev_sq2 = p.emit("s", lambda e, pb=pb, j2=j2: e.activation(
                    ssr[j2][:, :], pb[:, :][0:1, :], AF.Sqrt, bias=eps1[:, :]),
                    waits=[ev_mm, ev_e1] + bssr[j2].w_waits())
                bb.read(ev_sq2)
                bssr[j2].wrote(ev_sq2)
                ev_rc = p.emit("v", lambda e, j2=j2: e.reciprocal(
                    ssr[j2][:, :], ssr[j2][:, :]), waits=[ev_sq2], force=True)
                bssr[j2].wrote(ev_rc)
                ev_sc = p.emit("t", lambda e, qs_ap=qs_ap, pb=pb, j2=j2: e.matmul(
                    pb[0:64, :], qs_ap, ssr[j2][:, :], start=True, stop=True),
                    waits=[ev_rc, ev_qs_] + bb.w_waits(), sig=True)
                bb.wrote(ev_sc)
                bssr[j2].read(ev_sc)
                ev_cp = p.emit("v", lambda e, hh_=hh_, pb=pb, j2=j2: e.tensor_copy(
                    scl[j2][hh_ * 64:(hh_ + 1) * 64, :], pb[:, :][0:64, :]),
                    waits=[ev_sc] + (bscl[j2].w_waits() if hh_ == 0 else []))
                bb.read(ev_cp)
                bscl[j2].wrote(ev_cp)
            st["scl"] = ev_cp

        def rS3(it):
            nt, sl_ = sl_of(it)
            j2 = it % 2
            st = ST[it]
            ev_qp = p.emit("v", lambda e, sl_=sl_, j2=j2: e.tensor_tensor(
                sl_, sl_, scl[j2][:, :], ALU.mult),
                waits=[st["scl"], st["sq"]])
            bscl[j2].read(ev_qp)
            pbr, bbr = (PC, bPC) if j2 == 0 else (PD, bPD)
            ev_rt = p.emit("t", lambda e, sl_=sl_, pbr=pbr: e.matmul(
                pbr[:, :], rotm[:, :], sl_, start=True, stop=True),
                waits=[ev_qp, ev_rot] + bbr.w_waits())
            bbr.wrote(ev_rt)
            ev_t1 = p.emit("v", lambda e, sl_=sl_, nt=nt, cos_off=cos_off, tab_base=tab_base, j2=j2: e.tensor_tensor(
                t1b[j2][:, :], sl_, tb[:, cos_off + tab_base + nt * 512:cos_off + tab_base + nt * 512 + 512],
                ALU.mult), waits=[ev_tb] + bt1[j2].w_waits())
            bt1[j2].wrote(ev_t1)
            st["rt"] = ev_rt
            st["t1"] = ev_t1

        def rS4(it):
            nt, sl_ = sl_of(it)
            j4, j2 = it % 4, it % 2
            st = ST.pop(it)
            pbr, bbr = (PC, bPC) if j2 == 0 else (PD, bPD)
            ev_t2 = p.emit("v", lambda e, nt=nt, sin_off=sin_off, tab_base=tab_base, pbr=pbr, j4=j4: e.tensor_tensor(
                sqs[j4][:, :], pbr[:, :], tb[:, sin_off + tab_base + nt * 512:sin_off + tab_base + nt * 512 + 512],
                ALU.mult), waits=[st["rt"]] + bsq[j4].w_waits())
            bbr.read(ev_t2)
            bsq[j4].wrote(ev_t2)
            ev_qq = p.emit("v", lambda e, sl_=sl_, j4=j4, j2=j2: e.tensor_tensor(
                sl_, t1b[j2][:, :], sqs[j4][:, :], ALU.add),
                waits=[st["t1"]])
            bt1[j2].read(ev_qq)
            bsq[j4].read(ev_qq)
            last_qq[0] = ev_qq

        for step in range(n_it + 3):
            if step < n_it:
                rS1(step)
            if 1 <= step <= n_it:
                rS2(step - 1)
            if 2 <= step <= n_it + 1:
                rS3(step - 2)
            if step >= 3:
                rS4(step - 3)

    last_qq = [None]
    l2rot("q", q_sb, bq, 4, TPC, qsr[:, :], ev_qsr, 0, EXT, 513)
    ev_qrot = last_qq[0]
    bq.wrote(ev_qrot)
    l2rot("k", k_sb, bk, 5, KE, ksr[:, :], ev_ksr, 2 * EXT, 3 * EXT, 1)
    ev_krot = last_qq[0]
    bk.wrote(ev_krot)

    p.prefix = _base + "att."
    W_ = 640
    NIT = 128
    STM = {}

    def att_dec(it):
        return it // 32, (it // 4) % 8, it % 4

    def att_S1(it):
        w, hd, qi = att_dec(it)
        i3 = it % 3
        base_r = (hd % 2) * 64
        qcc = hd // 2
        qcol = qcc * TPC + w * 512 + qi * 128
        kcol = qcc * KE + w * 512 + qi * 128
        pbS1, bbS1 = (PA, bPA) if it % 2 == 0 else (PC, bPC)
        pbS2, bbS2 = (PB, bPB) if it % 2 == 0 else (PD, bPD)
        ev_s1_ = p.emit("t", lambda e, base_r=base_r, qcol=qcol, kcol=kcol, pbS1=pbS1: e.matmul(
            pbS1[:, :],
            q_sb[base_r:base_r + 64, qcol:qcol + 128],
            k_sb[base_r:base_r + 64, kcol:kcol + 512],
            start=True, stop=True),
            waits=bbS1.w_waits() + [ev_krot, ev_qrot])
        bbS1.wrote(ev_s1_)
        bq.read(ev_s1_)
        bk.read(ev_s1_)
        ev_s2_ = p.emit("t", lambda e, base_r=base_r, qcol=qcol, kcol=kcol, pbS2=pbS2: e.matmul(
            pbS2[:, 0:128],
            q_sb[base_r:base_r + 64, qcol:qcol + 128],
            k_sb[base_r:base_r + 64, kcol + 512:kcol + 640],
            start=True, stop=True),
            waits=bbS2.w_waits())
        bbS2.wrote(ev_s2_)
        bq.read(ev_s2_)
        bk.read(ev_s2_)
        mslot = (1 + qi) if w == 0 else 0
        ev_m1 = p.emit("v", lambda e, i3=i3, mslot=mslot, pbS1=pbS1: e.scalar_tensor_tensor(
            sm2[i3][:, 0:512], pbS1[:, :], 8.0,
            mk_sb[:, mslot * W_:mslot * W_ + 512], ALU.mult, ALU.add),
            waits=[ev_s1_, ev_s2_, ev_mk] + bsm[i3].w_waits())
        bbS1.read(ev_m1)
        ev_m2 = p.emit("v", lambda e, i3=i3, mslot=mslot, pbS2=pbS2: e.scalar_tensor_tensor(
            sm2[i3][:, 512:640], pbS2[:, 0:128], 8.0,
            mk_sb[:, mslot * W_ + 512:(mslot + 1) * W_], ALU.mult, ALU.add))
        bbS2.read(ev_m2)
        bsm[i3].wrote(ev_m2)
        ev_mx = p.emit("v", lambda e, i3=i3: e.tensor_reduce(
            m1[i3][:, :], sm2[i3][:, 0:W_], axis=AX.X, op=ALU.max,
            negate=True))
        STM[it] = {"mx": ev_mx}

    def att_S2(it):
        i3 = it % 3
        st = STM[it]
        ev_e1_ = p.emit("s", lambda e, i3=i3: e.activation(
            sm2[i3][:, 0:W_], sm2[i3][:, 0:W_], AF.Exp,
            bias=m1[i3][:, :], accum_out=ls1[i3][:, :]), waits=[st["mx"]])
        bsm[i3].wrote(ev_e1_)
        ev_rl_ = p.emit("v", lambda e, i3=i3: e.reciprocal(
            rl2[i3][:, :], ls1[i3][:, :]), waits=[ev_e1_], force=True)
        ev_pn = p.emit("v", lambda e, i3=i3: e.tensor_scalar(
            sm2[i3][:, 0:W_], sm2[i3][:, 0:W_], rl2[i3][:, :], None,
            ALU.mult), waits=[ev_rl_])
        bsm[i3].wrote(ev_pn)
        psT, bbT = (PT0, bPT[0]) if it % 2 == 0 else (PT1, bPT[1])
        evt = None
        for kc in range(5):
            evt = p.emit("t", lambda e, i3=i3, kc=kc, psT=psT: e.transpose(
                psT[:, kc * 128:(kc + 1) * 128],
                sm2[i3][:, kc * 128:(kc + 1) * 128], idb[:, :]),
                waits=([ev_pn] + bbT.w_waits()) if kc == 0 else [],
                sig=(kc == 4))
        bbT.wrote(evt)
        bsm[i3].read(evt)
        evc_h = p.emit("v", lambda e, it=it, psT=psT: e.tensor_copy(
            ptb[it % 2][:, 0:640], psT[:, 0:640]),
            waits=[evt] + bpt[it % 2].w_waits())
        bbT.read(evc_h)
        bpt[it % 2].wrote(evc_h)
        st["pt"] = evc_h

    def att_S3(it):
        w, hd, qi = att_dec(it)
        st = STM.pop(it)
        base_r = (hd % 2) * 64
        qcc = hd // 2
        psO, bbO = (PO0, bPO[0]) if it % 2 == 0 else (PO1, bPO[1])
        ev_pv = None
        for kc in range(5):
            s_ = 4 * w + qi + kc
            ev_pv = p.emit("t", lambda e, s_=s_, hd=hd, kc=kc, it=it, psO=psO: e.matmul(
                psO[:, :],
                v_sb[:, s_ * IN + hd * 64:s_ * IN + (hd + 1) * 64],
                ptb[it % 2][:, kc * 128:(kc + 1) * 128],
                start=(kc == 0), stop=(kc == 4)),
                waits=([st["pt"], ev_vall] + bbO.w_waits()) if kc == 0 else [],
                sig=(kc == 4))
        bpt[it % 2].read(ev_pv)
        bv.read(ev_pv)
        bbO.wrote(ev_pv)
        ev_oe = p.emit("s", lambda e, base_r=base_r, qcc=qcc, w=w, qi=qi, psO=psO: e.activation(
            o_sb[base_r:base_r + 64, qcc * TPC + w * 512 + qi * 128:qcc * TPC + w * 512 + (qi + 1) * 128],
            psO[:, :], AF.Copy),
            waits=[ev_pv] + (bo.w_waits() if it == 0 else []))
        bbO.read(ev_oe)
        bo.wrote(ev_oe)

    for step in range(NIT + 2):
        if step < NIT:
            att_S1(step)
        if 1 <= step <= NIT:
            att_S2(step - 1)
        if step >= 2:
            att_S3(step - 2)
    ev_oall = bo.write_list[-1]

    p.prefix = _base + "out."
    for tt in range(16):
        buf, bb2 = yt[tt % 2], byt[tt % 2]
        ev_ly = p.emit("g" if y_tm_cast else "y",
                       lambda e, tt=tt, buf=buf: e.dma_start(
                           out=buf[:, :], in_=y_tm[tt * 128:(tt + 1) * 128, :]),
                       waits=bb2.w_waits(), dma=True)
        bb2.wrote(ev_ly)
        for half in range(2):
            pb, bb = (PC, bPC) if half == 0 else (PD, bPD)
            ev_mm = None
            for oc in range(4):
                ev_mm = p.emit("t", lambda e, tt=tt, half=half, oc=oc, pb=pb: e.matmul(
                    pb[:, :],
                    o_sb[:, oc * TPC + tt * 128:oc * TPC + (tt + 1) * 128],
                    wO[:, oc * DM + half * 512:oc * DM + (half + 1) * 512],
                    start=(oc == 0), stop=(oc == 3)),
                    waits=(bb.w_waits() + [ev_oall] + evs_wo) if oc == 0 else [],
                    sig=(oc == 3))
            bb.wrote(ev_mm)
            bo.read(ev_mm)
            ev_xp = p.emit("v", lambda e, half=half, pb=pb, buf=buf: e.tensor_tensor(
                xpt[:, half * 512:(half + 1) * 512], pb[:, :],
                buf[:, half * 512:(half + 1) * 512], ALU.add),
                waits=[ev_mm, ev_ly] + (bxpt.w_waits() if half == 0 else []))
            bb.read(ev_xp)
            bb2.read(ev_xp)
            bxpt.wrote(ev_xp)
        ev_xo = p.emit("y", lambda e, tt=tt: e.dma_start(
            out=xp_o[tt * 128:(tt + 1) * 128, :], in_=xpt[:, :]),
            waits=[ev_xp], dma=True)
        bxpt.read(ev_xo)
        if xpfm_o is not None:
            for hf in range(2):
                pb, bb = (PA, bPA) if hf == 0 else (PB, bPB)
                for kc in range(4):
                    f8 = hf * 4 + kc
                    evt = p.emit("t", lambda e, f8=f8, kc=kc, pb=pb: e.transpose(
                        pb[:, kc * 128:(kc + 1) * 128],
                        xpt[:, f8 * 128:(f8 + 1) * 128], idf[:, :]),
                        waits=([ev_xp] + bb.w_waits()) if kc == 0 else [],
                        sig=(kc == 3))
                bb.wrote(evt)
                bxpt.read(evt)
                evc_ = p.emit("v", lambda e, pb=pb: e.tensor_copy(
                    xpf_st[:, :], pb[:, :]),
                    waits=[evt] + bxpf.w_waits())
                bb.read(evc_)
                bxpf.wrote(evc_)
                ev_fo = p.emit("y", lambda e, tt=tt, hf=hf: e.dma_start(
                    out=xpfm_o.rearrange("(k p) n -> p k n", p=128)[
                        :, hf * 4:(hf + 1) * 4, tt * 128:(tt + 1) * 128],
                    in_=xpf_st[:].rearrange("p (k n) -> p k n", n=128)),
                    waits=[evc_], dma=True)
                bxpf.read(ev_fo)
    p.prefix = _base


def build_Bf_attn():
    nc = bass.Bass("TRN2", target_bir_lowering=False)
    EXT, IN = 2561, 512
    io = {
        "y_fm": nc.dram_tensor("y_fm", [DM, EXT], F32, kind="ExternalInput").ap(),
        "y_tm": nc.dram_tensor("y_tm", [TPC, DM], F32, kind="ExternalInput").ap(),
        "wqkv": nc.dram_tensor("wqkv", [DM, 3 * IN], F32, kind="ExternalInput").ap(),
        "wo": nc.dram_tensor("wo", [IN, DM], F32, kind="ExternalInput").ap(),
        "qs_r": nc.dram_tensor("qs_r", [1, 64], F32, kind="ExternalInput").ap(),
        "ks_r": nc.dram_tensor("ks_r", [1, 64], F32, kind="ExternalInput").ap(),
        "tbls": nc.dram_tensor("tbls", [128, 4 * EXT], F32, kind="ExternalInput").ap(),
        "masks": nc.dram_tensor("masks", [128, 5 * 640], F32, kind="ExternalInput").ap(),
        "xp_o": nc.dram_tensor("xp", [TPC, DM], F32, kind="ExternalOutput").ap(),
    }
    p = Pgm(nc)
    g = mk_glob(p, nc)
    g["rotm"] = p.sb("glob_rotm", [128, 128], BF16)
    rot_t = nc.inline_tensor(_rot_matrix(), "rot_t")
    p.emit("g", lambda e: e.dma_start(out=g["rotm"][:, :], in_=rot_t.ap()[:, :]), dma=True)
    p.fence()
    emit_Bf_attn(p, g, io)
    finish_all(p)
    return nc


def _bf_tables(T0):
    EXT = 2561
    t = np.arange(T0 - 513, T0 + 2048, dtype=np.float64)
    half = np.arange(0, 64, 2, dtype=np.float64)
    inv = 1.0 / 10000.0 ** (half / 64.0)
    fr = t[:, None] * inv[None, :]
    fr = np.concatenate([fr, fr], axis=1)
    base = (half + 0.4 * 64) / (1.4 * 64)
    power = (t - 2048.0) / 256.0
    scale = base[None, :] ** power[:, None]
    scale = np.concatenate([scale, scale], axis=1)
    cos, sin = np.cos(fr), np.sin(fr)
    blocks = [cos * scale, sin * scale, cos / scale, sin / scale]
    out = np.zeros((128, 4 * EXT), np.float32)
    for i, b in enumerate(blocks):
        bT = b.T.astype(np.float32)
        out[0:64, i * EXT:(i + 1) * EXT] = bT
        out[64:128, i * EXT:(i + 1) * EXT] = bT
    return out


def _bf_masks(first):
    out = np.zeros((128, 5 * 640), np.float32)
    i = np.arange(128)[:, None]
    j = np.arange(640)[None, :]
    band = (j >= i) & (j <= i + 512)
    out[:, 0:640] = np.where(band, 0.0, NEG)
    for qi in range(4):
        af = band & ((j + qi * 128) >= 512) if first else band
        out[:, (1 + qi) * 640:(2 + qi) * 640] = np.where(af, 0.0, NEG)
    return out


def run_Bf_attn(inputs, l, y_by_core):
    nc = build_Bf_attn()
    wqkv = np.ascontiguousarray(np.asarray(inputs["L1_ffn_wqkv"], np.float32)[l])
    wo = np.ascontiguousarray(np.asarray(inputs["L1_ffn_wo"], np.float32)[l])
    qs = np.ascontiguousarray(np.asarray(inputs["L1_ffn_q_scale"], np.float32)[l].reshape(1, 64))
    ks = np.ascontiguousarray(np.asarray(inputs["L1_ffn_k_scale"], np.float32)[l].reshape(1, 64))
    maps = []
    for c in range(NCORE):
        b, h = c // 2, c % 2
        T0 = h * TPC
        own = np.asarray(y_by_core[c], np.float32)
        if h == 0:
            halo = np.zeros((513, DM), np.float32)
        else:
            halo = np.asarray(y_by_core[c - 1], np.float32)[-513:]
        yext = np.concatenate([halo, own], axis=0)
        maps.append({
            "y_fm": np.ascontiguousarray(yext.T), "y_tm": np.ascontiguousarray(own),
            "wqkv": wqkv, "wo": wo, "qs_r": qs, "ks_r": ks,
            "tbls": _bf_tables(T0), "masks": _bf_masks(h == 0),
        })
    res = run_bass_kernel_spmd(nc, maps, core_ids=list(range(NCORE)))
    return res.results


def emit_Bf_ffn(p, g, io, final=False):
    HID = 2730
    NCH = 22
    TW = 128
    NNT = TPC // TW
    xp_fm = io.get("xp_fm")
    xp_own = io.get("xp_own")
    xp_c0 = io.get("xp_c0")
    xp_tm = io["xp_tm"]
    w1 = io["w1"]
    w2 = io["w2"]
    vcols = io["vcols"]
    b1r = io["b1r"]
    b2_r = io["b2_r"]
    y_o = io["y_o"]
    yfm_o = io.get("yfm_o")
    if final:
        x1_tm = io["x1_tm"]
        s1_tm = io["s1_tm"]
        x2_o = io["x2_o"]
    idf = g["idf"]
    idb = g["idb"]

    w1a = p.sb("w1a", [128, 8 * HID], BF16)
    w1g = p.sb("w1g", [128, 8 * HID], BF16)
    w2s = p.sb("w2s", [128, NCH * DM], BF16)
    vc = p.sb("vc", [128, 16], F32)
    b2s = p.sb("b2s", [1, DM], BF16)
    onesb = p.sb("onesb", [1, 128], BF16)
    onescf = p.sb("onescf", [128, 1], F32)
    Rrb = p.sb("Rrb", [1, TPC], BF16)
    Arb = p.sb("Arb", [1, TPC], BF16)
    p.push(p.prefix + "st_")
    PR = p.ps("PR", [1, 512])
    sq_s = p.sb("sq_s", [128, TW], F32)
    murow = p.sb("murow", [1, TPC], F32)
    ssrow = p.sb("ssrow", [1, TPC], F32)
    rowt = p.sb("rowt", [1, TPC], F32)
    rowt2 = p.sb("rowt2", [1, TPC], F32)
    p.prefix = p.prefix[:-3]
    xft = [p.sb(f"xft{i}", [128, 8 * TW], F32) for i in range(2)]
    hn = [p.sb(f"hn{i}", [128, 8 * TW], BF16) for i in range(2)]
    aa = p.sb("aa", [128, NCH * TW], BF16)
    gg = p.sb("gg", [128, NCH * TW], BF16)
    xpt = [p.sb("xpt0", [128, DM], F32)] * 2
    yt = p.sb("yt", [128, DM], F32)
    if yfm_o is not None:
        yf_st = p.sb("yf_st", [128, 512], F32)
    byf = Buf()
    PA = p.ps("PA", [128, 512])
    PB = p.ps("PB", [128, 512])
    PC = p.ps("PC", [128, 512])
    PD = p.ps("PD", [128, 512])
    PTb0 = p.ps("PTb0", [128, 512], BF16)
    PTb1 = p.ps("PTb1", [128, 512], BF16)

    bPA, bPB, bPC, bPD, bPR = (Buf() for _ in range(5))
    bPTb0, bPTb1 = Buf(), Buf()
    bPEb = [Buf(), Buf()]
    bsq, bmu, bss, bR, bA = Buf(), Buf(), Buf(), Buf(), Buf()
    bxft = [Buf(), Buf()]
    bhn = [Buf(), Buf()]
    baa, bgg = Buf(), Buf()
    _bx = Buf()
    bxpt = [_bx] * 2
    byt = Buf()
    bx1t, bs1t = Buf(), Buf()

    ev_idf = None
    ev_vc = p.emit("y", lambda e: e.dma_start(out=vc[:, :], in_=vcols[:, :]), dma=True)
    ev_b2 = p.emit("g", lambda e: e.dma_start(out=b2s[:, :], in_=b2_r[:, :]), dma=True)
    ev_ob = p.emit("v", lambda e: e.memset(onesb[:, :], 1.0))
    ev_ocf = p.emit("v", lambda e: e.memset(onescf[:, :], 1.0))
    evs_w1a = []
    evs_w1g = []
    for k in range(8):
        evs_w1a.append(p.emit("g", lambda e, k=k: e.dma_start(
            out=w1a[:, k * HID:(k + 1) * HID],
            in_=w1[k * 128:(k + 1) * 128, 0:HID]), dma=True))
        evs_w1g.append(p.emit("g", lambda e, k=k: e.dma_start(
            out=w1g[:, k * HID:(k + 1) * HID],
            in_=w1[k * 128:(k + 1) * 128, HID:2 * HID]), dma=True))
    evs_w2 = []
    for c in range(NCH):
        rows = 128 if c < NCH - 1 else HID - 128 * (NCH - 1)
        evs_w2.append(p.emit("g", lambda e, c=c, rows=rows: e.dma_start(
            out=w2s[0:rows, c * DM:(c + 1) * DM],
            in_=w2[c * 128:c * 128 + rows, :]), dma=True))

    def load_xft(nt, xf, bxf):
        if xp_fm is not None:
            ev_lx = p.emit("y", lambda e, nt=nt: e.dma_start(
                out=xf[:, 0:4 * TW].rearrange("p (k n) -> p k n", n=TW),
                in_=xp_fm.rearrange("(k p) n -> p k n", p=128)[
                    :, 0:4, 1 + nt * TW:1 + (nt + 1) * TW]),
                waits=bxf.w_waits(), dma=True)
            bxf.wrote(ev_lx)
            evs_sh = []
            for f in range(4, 8):
                evs_sh.append(p.emit("y", lambda e, nt=nt, f=f: e.dma_start(
                    out=xf[:, f * TW:(f + 1) * TW],
                    in_=xp_fm[f * 128:(f + 1) * 128, nt * TW:(nt + 1) * TW]),
                    dma=True))
        else:
            ev_lx = p.emit("y", lambda e, nt=nt: e.dma_start(
                out=xf[:, 0:4 * TW].rearrange("p (k n) -> p k n", n=TW),
                in_=xp_own.rearrange("(k p) n -> p k n", p=128)[
                    :, 0:4, nt * TW:(nt + 1) * TW]),
                waits=bxf.w_waits(), dma=True)
            bxf.wrote(ev_lx)
            evs_sh = []
            for f in range(4, 8):
                if nt == 0:
                    evs_sh.append(p.emit("y", lambda e, f=f: e.dma_start(
                        out=xf[:, f * TW:f * TW + 1],
                        in_=xp_c0[f * 128:(f + 1) * 128, :]), dma=True))
                    evs_sh.append(p.emit("y", lambda e, f=f: e.dma_start(
                        out=xf[:, f * TW + 1:(f + 1) * TW],
                        in_=xp_own[f * 128:(f + 1) * 128, 0:TW - 1]), dma=True))
                else:
                    evs_sh.append(p.emit("y", lambda e, nt=nt, f=f: e.dma_start(
                        out=xf[:, f * TW:(f + 1) * TW],
                        in_=xp_own[f * 128:(f + 1) * 128,
                                   nt * TW - 1:(nt + 1) * TW - 1]),
                        dma=True))
        for e2 in evs_sh:
            bxf.wrote(e2)
        return ev_lx, evs_sh

    for nt in range(NNT):
        xf_, bxf_ = xft[nt % 2], bxft[nt % 2]
        ev_lx, evs_sh = load_xft(nt, xf_, bxf_)
        ev_mm = None
        for f in range(8):
            ev_mm = p.emit("t", lambda e, f=f, nt=nt, xf_=xf_: e.matmul(
                PR[0:1, 0:TW], onescf[:, :], xf_[:, f * TW:(f + 1) * TW],
                start=(f == 0), stop=(f == 7)),
                waits=([ev_lx] + evs_sh + [ev_ocf] + bPR.w_waits()) if f == 0 else [],
                sig=(f == 7))
        bPR.wrote(ev_mm)
        bxf_.read(ev_mm)
        ev_cpm = p.emit("v", lambda e, nt=nt: e.tensor_copy(
            murow[:, nt * TW:(nt + 1) * TW], PR[0:1, 0:TW]), waits=[ev_mm])
        bPR.read(ev_cpm)
        bmu.wrote(ev_cpm)
        ev_mm2 = None
        for f in range(8):
            ev_q = p.emit("s", lambda e, f=f, xf_=xf_: e.activation(
                sq_s[:, :], xf_[:, f * TW:(f + 1) * TW], AF.Square),
                waits=bsq.w_waits() + ([ev_lx] + evs_sh if f == 0 else []))
            bsq.wrote(ev_q)
            bxf_.read(ev_q)
            ev_mm2 = p.emit("t", lambda e, f=f: e.matmul(
                PR[0:1, 256:256 + TW], onescf[:, :], sq_s[:, :],
                start=(f == 0), stop=(f == 7)),
                waits=[ev_q] + (bPR.w_waits() if f == 0 else []), sig=True)
            bPR.wrote(ev_mm2)
            bsq.read(ev_mm2)
        ev_cps = p.emit("v", lambda e, nt=nt: e.tensor_copy(
            ssrow[:, nt * TW:(nt + 1) * TW], PR[0:1, 256:256 + TW]), waits=[ev_mm2])
        bPR.read(ev_cps)
        bss.wrote(ev_cps)
    pe = p.emit("v", lambda e: e.engine_nop(),
                waits=[(x[0], x[1], True) for x in (bmu.write_list[-1:] + bss.write_list[-1:])],
                force=True)

    def vrow(fn, extra=()):
        nonlocal pe
        pe = p.emit("v", fn, waits=[pe] + list(extra), force=True)
        return pe

    vrow(lambda e: e.tensor_scalar(murow[:, :], murow[:, :], 1.0 / DM, None, ALU.mult))
    vrow(lambda e: e.tensor_tensor(rowt[:, :], murow[:, :], murow[:, :], ALU.mult))
    vrow(lambda e: e.scalar_tensor_tensor(ssrow[:, :], ssrow[:, :], 1.0 / DM,
                                          rowt[:, :], ALU.mult, ALU.subtract))
    ev_ve = vrow(lambda e: e.tensor_scalar(ssrow[:, :], ssrow[:, :], 1.0, 1e-5,
                                           ALU.mult, ALU.add))
    ev_sa = p.emit("s", lambda e: e.activation(rowt[:, :], ssrow[:, :], AF.Sqrt),
                   waits=[ev_ve])
    vrow(lambda e: e.reciprocal(rowt[:, :], rowt[:, :]), extra=[ev_sa])
    for _ in range(2):
        vrow(lambda e: e.tensor_tensor(rowt2[:, :], rowt[:, :], rowt[:, :], ALU.mult))
        vrow(lambda e: e.tensor_tensor(rowt2[:, :], rowt2[:, :], ssrow[:, :], ALU.mult))
        vrow(lambda e: e.tensor_scalar(rowt2[:, :], rowt2[:, :], -0.5, 1.5, ALU.mult, ALU.add))
        vrow(lambda e: e.tensor_tensor(rowt[:, :], rowt[:, :], rowt2[:, :], ALU.mult))
    vrow(lambda e: e.tensor_scalar(murow[:, :], murow[:, :], -1.0, None, ALU.mult))
    vrow(lambda e: e.tensor_tensor(murow[:, :], murow[:, :], rowt[:, :], ALU.mult))
    vrow(lambda e: e.tensor_copy(Rrb[:, :], rowt[:, :]))
    ev_rows = vrow(lambda e: e.tensor_copy(Arb[:, :], murow[:, :]))
    bR.wrote(ev_rows)

    _pfx = p.prefix
    p.pop()
    p.prefix = _pfx
    p.fence()
    b1rs = p.sb("b1rs", [1, 2 * HID], BF16)
    if final:
        x1t = p.sb("x1t", [128, DM], F32)
        s1t = p.sb("s1t", [128, DM], F32)
    PEb_t = p.ps("PEb", [128, 512])
    ev_b1r = p.emit("g", lambda e: e.dma_start(out=b1rs[:, :], in_=b1r[:, :]),
                    dma=True)

    FST = {}

    def tile_front(nt):
        r2 = nt % 2
        xf_, bxf_ = xft[r2], bxft[r2]
        hn_, bhn_ = hn[r2], bhn[r2]
        bbb = bPEb[r2]
        cb = r2 * 2 * TW
        pbb = PEb_t[:, cb:cb + 2 * TW]
        ev_bR = p.emit("t", lambda e, nt=nt, pbb=pbb: e.matmul(
            pbb[:, 0:TW], onesb[:, :], Rrb[0:1, nt * TW:(nt + 1) * TW],
            start=True, stop=True), waits=[ev_rows, ev_ob] + bbb.w_waits())
        ev_bA = p.emit("t", lambda e, nt=nt, pbb=pbb: e.matmul(
            pbb[:, TW:2 * TW], onesb[:, :], Arb[0:1, nt * TW:(nt + 1) * TW],
            start=True, stop=True), waits=[])
        bbb.wrote(ev_bA)
        ev_lx, evs_sh = load_xft(nt, xf_, bxf_)
        ev_h = None
        for f in range(8):
            p.emit("v", lambda e, f=f, xf_=xf_, pbb=pbb: e.tensor_tensor(
                xf_[:, f * TW:(f + 1) * TW], xf_[:, f * TW:(f + 1) * TW],
                pbb[:, 0:TW], ALU.mult), waits=[ev_lx, ev_bR, ev_bA] + evs_sh)
            p.emit("v", lambda e, f=f, xf_=xf_, pbb=pbb: e.tensor_tensor(
                xf_[:, f * TW:(f + 1) * TW], xf_[:, f * TW:(f + 1) * TW],
                pbb[:, TW:2 * TW], ALU.add), waits=[])
            ev_h = p.emit("v", lambda e, f=f, xf_=xf_, hn_=hn_: e.tensor_scalar(
                hn_[:, f * TW:(f + 1) * TW], xf_[:, f * TW:(f + 1) * TW],
                vc[:, f:f + 1], vc[:, 8 + f:8 + f + 1], ALU.mult, ALU.add),
                waits=[ev_vc] + (bhn_.w_waits() if f == 0 else []))
            bhn_.wrote(ev_h)
        bbb.read(ev_h)
        bxf_.read(ev_h)
        FST[nt] = ev_h

    def tile_back(nt):
        r2 = nt % 2
        hn_, bhn_ = hn[r2], bhn[r2]
        ev_h = FST.pop(nt)
        NDC = (HID + 511) // 512
        ev_ff1 = [None, None]
        for half in range(2):
            wt, evw = (w1a, evs_w1a) if half == 0 else (w1g, evs_w1g)
            tgt, btgt = (aa, baa) if half == 0 else (gg, bgg)
            for dc in range(NDC):
                ncols = 512 if dc < NDC - 1 else HID - 512 * (NDC - 1)
                pb, bb = (PC, bPC) if dc % 2 == 0 else (PD, bPD)
                ev_mm = p.emit("t", lambda e, half=half, dc=dc, ncols=ncols, pb=pb: e.matmul(
                    pb[:, 0:ncols], onesb[:, :],
                    b1rs[0:1, half * HID + dc * 512:half * HID + dc * 512 + ncols],
                    start=True, stop=False),
                    waits=bb.w_waits() + [ev_b1r, ev_ob, ev_h] + evw, sig=False)
                for f in range(8):
                    ev_mm = p.emit("t", lambda e, half=half, dc=dc, ncols=ncols, f=f, pb=pb, wt=wt, hn_=hn_: e.matmul(
                        pb[:, 0:ncols],
                        hn_[:, f * TW:(f + 1) * TW],
                        wt[:, f * HID + dc * 512:f * HID + dc * 512 + ncols],
                        start=False, stop=(f == 7)),
                        waits=[], sig=(f == 7))
                bb.wrote(ev_mm)
                bhn_.read(ev_mm)
                if half == 0:
                    ev_ev = p.emit("v", lambda e, dc=dc, ncols=ncols, pb=pb, tgt=tgt: e.tensor_copy(
                        tgt[:, dc * 512:dc * 512 + ncols], pb[:, 0:ncols]),
                        waits=[ev_mm] + btgt.w_waits())
                else:
                    ev_ev = p.emit("s", lambda e, dc=dc, ncols=ncols, pb=pb, tgt=tgt: e.activation(
                        tgt[:, dc * 512:dc * 512 + ncols], pb[:, 0:ncols], AF.Gelu),
                        waits=[ev_mm] + btgt.w_waits())
                bb.read(ev_ev)
                btgt.wrote(ev_ev)
                ev_ff1[half] = ev_ev
        ev_aall, ev_gall = ev_ff1
        ev_e = None
        for dc in range(NDC):
            ncols = 512 if dc < NDC - 1 else HID - 512 * (NDC - 1)
            ev_e = p.emit("v", lambda e, dc=dc, ncols=ncols: e.tensor_tensor(
                aa[:, dc * 512:dc * 512 + ncols], aa[:, dc * 512:dc * 512 + ncols],
                gg[:, dc * 512:dc * 512 + ncols], ALU.mult),
                waits=([ev_aall, ev_gall] + baa.w_waits()) if dc == 0 else [])
            baa.wrote(ev_e)
            bgg.read(ev_e)
        ev_tp = None
        for rd in range(6):
            cs = list(range(rd * 4, min(rd * 4 + 4, NCH)))
            psT, bbT = (PTb0, bPTb0) if rd % 2 == 0 else (PTb1, bPTb1)
            evt = None
            for i_, c in enumerate(cs):
                rows = 128 if c < NCH - 1 else HID - 128 * (NCH - 1)
                evt = p.emit("t", lambda e, c=c, i_=i_, rows=rows, psT=psT: e.transpose(
                    psT[0:rows, i_ * 128:i_ * 128 + 128],
                    aa[:, c * 128:c * 128 + rows], idb[:, :]),
                    waits=([ev_e] + bbT.w_waits()) if i_ == 0 else [],
                    sig=(i_ == len(cs) - 1))
            bbT.wrote(evt)
            baa.read(evt)
            ev_tp = p.emit("v", lambda e, rd=rd, ncs=len(cs), psT=psT: e.tensor_copy(
                gg[:, rd * 512:rd * 512 + ncs * 128], psT[:, 0:ncs * 128]),
                waits=[evt] + bgg.w_waits())
            bbT.read(ev_tp)
            bgg.wrote(ev_tp)
        for sub in range(TW // 128):
            tile = nt * (TW // 128) + sub
            xb = xpt[sub % 2]
            ev_lr = p.emit("y", lambda e, tile=tile, xb=xb: e.dma_start(
                out=xb[:, :], in_=xp_tm[tile * 128:(tile + 1) * 128, :]),
                waits=bxpt[0].w_waits(), dma=True)
            bxpt[0].wrote(ev_lr)
            for half in range(2):
                pb, bb = (PA, bPA) if half == 0 else (PB, bPB)
                ev_mm = p.emit("t", lambda e, half=half, pb=pb: e.matmul(
                    pb[:, :], onesb[:, :], b2s[:, half * 512:(half + 1) * 512],
                    start=True, stop=False),
                    waits=bb.w_waits() + [ev_tp, ev_b2, ev_ob] + evs_w2, sig=False)
                for c in range(NCH):
                    rows = 128 if c < NCH - 1 else HID - 128 * (NCH - 1)
                    gcol = (c // 4) * 512 + (c % 4) * 128
                    ev_mm = p.emit("t", lambda e, c=c, gcol=gcol, half=half, pb=pb, rows=rows: e.matmul(
                        pb[:, :],
                        gg[0:rows, gcol:gcol + 128],
                        w2s[0:rows, c * DM + half * 512:c * DM + (half + 1) * 512],
                        start=False, stop=(c == NCH - 1)),
                        waits=[], sig=(c == NCH - 1))
                bb.wrote(ev_mm)
                bgg.read(ev_mm)
                ev_y = p.emit("v", lambda e, half=half, pb=pb, xb=xb: e.tensor_tensor(
                    yt[:, half * 512:(half + 1) * 512], pb[:, :],
                    xb[:, half * 512:(half + 1) * 512], ALU.add),
                    waits=[ev_mm, ev_lr] + (byt.w_waits() if half == 0 else []))
                bb.read(ev_y)
                bxpt[0].read(ev_y)
                byt.wrote(ev_y)
            ev_yo = p.emit("y", lambda e, tile=tile: e.dma_start(
                out=y_o[tile * 128:(tile + 1) * 128, :], in_=yt[:, :]),
                waits=[ev_y], dma=True)
            byt.read(ev_yo)
            if yfm_o is not None:
                for hf in range(2):
                    pb, bb = (PC, bPC) if hf == 0 else (PD, bPD)
                    for kc in range(4):
                        f8 = hf * 4 + kc
                        evt = p.emit("t", lambda e, f8=f8, kc=kc, pb=pb: e.transpose(
                            pb[:, kc * 128:(kc + 1) * 128],
                            yt[:, f8 * 128:(f8 + 1) * 128], idf[:, :]),
                            waits=([ev_y] + bb.w_waits()) if kc == 0 else [],
                            sig=(kc == 3))
                    bb.wrote(evt)
                    byt.read(evt)
                    evc_ = p.emit("v", lambda e, pb=pb: e.tensor_copy(
                        yf_st[:, :], pb[:, :]),
                        waits=[evt] + byf.w_waits())
                    bb.read(evc_)
                    byf.wrote(evc_)
                    ev_fo = p.emit("y", lambda e, tile=tile, hf=hf: e.dma_start(
                        out=yfm_o.rearrange("(k p) n -> p k n", p=128)[
                            :, hf * 4:(hf + 1) * 4, tile * 128:(tile + 1) * 128],
                        in_=yf_st[:].rearrange("p (k n) -> p k n", n=128)),
                        waits=[evc_], dma=True)
                    byf.read(ev_fo)
            if final:
                ev_l1 = p.emit("y", lambda e, tile=tile: e.dma_start(
                    out=x1t[:, :], in_=x1_tm[tile * 128:(tile + 1) * 128, :]),
                    waits=bx1t.w_waits(), dma=True)
                bx1t.wrote(ev_l1)
                ev_l2 = p.emit("y", lambda e, tile=tile: e.dma_start(
                    out=s1t[:, :], in_=s1_tm[tile * 128:(tile + 1) * 128, :]),
                    waits=bs1t.w_waits(), dma=True)
                bs1t.wrote(ev_l2)
                ev_g1 = p.emit("s", lambda e: e.activation(
                    yt[:, :], yt[:, :], AF.Silu), waits=[ev_y, ev_yo])
                byt.wrote(ev_g1)
                ev_m1_ = p.emit("v", lambda e: e.tensor_tensor(
                    yt[:, :], yt[:, :], s1t[:, :], ALU.mult), waits=[ev_g1, ev_l2])
                bs1t.read(ev_m1_)
                byt.wrote(ev_m1_)
                ev_a1_ = p.emit("v", lambda e: e.tensor_tensor(
                    yt[:, :], yt[:, :], x1t[:, :], ALU.add), waits=[ev_l1])
                bx1t.read(ev_a1_)
                byt.wrote(ev_a1_)
                ev_x2o = p.emit("y", lambda e, tile=tile: e.dma_start(
                    out=x2_o[tile * 128:(tile + 1) * 128, :], in_=yt[:, :]),
                    waits=[ev_a1_], dma=True)
                byt.read(ev_x2o)

    tile_front(0)
    for nt in range(NNT):
        if nt + 1 < NNT:
            tile_front(nt + 1)
        tile_back(nt)


def build_Bf_ffn(final=False):
    nc = bass.Bass("TRN2", target_bir_lowering=False)
    HID, NCH = 2730, 22
    io = {
        "xp_fm": nc.dram_tensor("xp_fm", [DM, TPC + 1], F32, kind="ExternalInput").ap(),
        "xp_tm": nc.dram_tensor("xp_tm", [TPC, DM], F32, kind="ExternalInput").ap(),
        "w1": nc.dram_tensor("w1", [DM, 2 * HID], F32, kind="ExternalInput").ap(),
        "w2": nc.dram_tensor("w2", [HID, DM], F32, kind="ExternalInput").ap(),
        "vcols": nc.dram_tensor("vcols", [128, 16], F32, kind="ExternalInput").ap(),
        "b1r": nc.dram_tensor("b1r", [1, 2 * HID], BF16, kind="ExternalInput").ap(),
        "b2_r": nc.dram_tensor("b2_r", [1, DM], F32, kind="ExternalInput").ap(),
        "y_o": nc.dram_tensor("y_out", [TPC, DM], F32, kind="ExternalOutput").ap(),
    }
    if final:
        io["x1_tm"] = nc.dram_tensor("x1_tm", [TPC, DM], F32, kind="ExternalInput").ap()
        io["s1_tm"] = nc.dram_tensor("s1_tm", [TPC, DM], F32, kind="ExternalInput").ap()
        io["x2_o"] = nc.dram_tensor("x2", [TPC, DM], F32, kind="ExternalOutput").ap()
    p = Pgm(nc)
    g = mk_glob(p, nc)
    p.fence()
    emit_Bf_ffn(p, g, io, final=final)
    finish_all(p)
    return nc


def run_Bf_ffn(inputs, l, xp_by_core, final=False, x1_by_core=None, s1_by_core=None):
    nc = build_Bf_ffn(final=final)
    HID = 2730
    NCH = 22
    w1v = np.ascontiguousarray(np.asarray(inputs["L1_ffn_w1"], np.float32)[l])
    w2v = np.ascontiguousarray(np.asarray(inputs["L1_ffn_w2"], np.float32)[l])
    vc = _vcols2(np.asarray(inputs["L1_ffn_ln_g"], np.float32)[l],
                 np.asarray(inputs["L1_ffn_ln_b"], np.float32)[l])
    b1rv = np.ascontiguousarray(
        np.asarray(inputs["L1_ffn_b1"], np.float32)[l].reshape(1, 2 * HID)
        .astype(ml_dtypes.bfloat16))
    b2v = np.ascontiguousarray(np.asarray(inputs["L1_ffn_b2"], np.float32)[l].reshape(1, DM))
    maps = []
    for c in range(NCORE):
        b, h = c // 2, c % 2
        own = np.asarray(xp_by_core[c], np.float32)
        halo = (np.zeros((1, DM), np.float32) if h == 0
                else np.asarray(xp_by_core[c - 1], np.float32)[-1:])
        xpe = np.concatenate([halo, own], axis=0)
        m = {
            "xp_fm": np.ascontiguousarray(xpe.T), "xp_tm": np.ascontiguousarray(own),
            "w1": w1v, "w2": w2v, "vcols": vc,
            "b1r": b1rv, "b2_r": b2v,
        }
        if final:
            m["x1_tm"] = np.ascontiguousarray(x1_by_core[c])
            m["s1_tm"] = np.ascontiguousarray(s1_by_core[c])
        maps.append(m)
    res = run_bass_kernel_spmd(nc, maps, core_ids=list(range(NCORE)))
    return res.results
